# version 03
# baseline (speedup 1.0000x reference)
import numpy as np
import ml_dtypes
import concourse.bass as bass
import concourse.mybir as mybir
from concourse.bass_utils import run_bass_kernel_spmd

F32 = mybir.dt.float32
BF16 = mybir.dt.bfloat16
U8 = mybir.dt.uint8
ALU = mybir.AluOpType
ACT = mybir.ActivationFunctionType
AX = mybir.AxisListType.X
bfdt = ml_dtypes.bfloat16

T, D = 2048, 1024
NCH = 16
NPAIR = 8
NE = 10
NT = 8
NEG = -28672.0

LV_N = [2048, 1024, 512, 256, 128, 64]
LV_OFF = [0, 2048, 3072, 3584, 3840, 3968]
LV_NC = [16, 8, 4, 2, 1, 1]
LV_CW = [128, 128, 128, 128, 128, 64]
LV_P32 = [("a", 0), ("b", 0), ("a", 0), ("b", 0), ("a", 512), ("b", 256)]
LV_ACC = ["a", "b", "a", "b", "a", "b"]
LV_SLOT0 = {1: 0, 2: 8, 3: 12, 4: 14, 5: 15}
SLOTW = 66

ENGS = ("sync", "tensor", "vector", "scalar", "gpsimd")
SEMS = ("t", "v", "s", "g", "dS", "dG", "dA", "dB", "dC", "dD", "dE", "dF", "b")
DMASEMS = ("dS", "dG", "dA", "dB", "dC", "dD", "dE", "dF")
BAR = 1024


class Prog:
    def __init__(self, nc, dma_amts=None):
        self.nc = nc
        self.items = {e: [] for e in ENGS}
        self.cnt = {s: 0 for s in SEMS}
        self.lastw = {e: {} for e in ENGS}
        self.bank_readers = {i: [] for i in range(8)}
        self.dma_amts = dma_amts
        self.dma_idx = 0
        self.dma_splits = {}
        self.sems = None
        self.bar_tok = None

    def op(self, eng, fn, waits=(), inc=None, amt=1, dma_id=None):
        ws = []
        if self.bar_tok is not None and eng != "sync":
            waits = [self.bar_tok] + list(waits)
        for w in waits:
            if w is None:
                continue
            s, v = w
            if v <= 0:
                continue
            assert v <= self.cnt[s], f"forward-token violation: {s}={v} > {self.cnt[s]}"
            if self.lastw[eng].get(s, 0) < v:
                ws.append((s, v))
                self.lastw[eng][s] = v
        tok = None
        if inc is not None:
            self.cnt[inc] += amt
            tok = (inc, self.cnt[inc])
        self.items[eng].append([ws, fn, inc, amt, tok, dma_id])
        return tok

    def dma(self, q, out, in_, waits=(), sem=None):
        if sem is None:
            sem = "dS" if q == "sync" else "dG"
        idx = self.dma_idx
        self.dma_idx += 1
        amt = 16 if self.dma_amts is None else self.dma_amts[idx]
        return self.op(q, lambda e, o=out, i=in_: e.dma_start(o, i), waits=waits,
                       inc=sem, amt=amt, dma_id=idx)

    def bank_w(self, i):
        w = self.bank_readers[i]
        self.bank_readers[i] = []
        return w

    def bank_r(self, i, tok):
        self.bank_readers[i].append(tok)

    def fence(self, engs=("tensor", "vector", "scalar")):
        m = {"tensor": "t", "vector": "v", "scalar": "s", "gpsimd": "g"}
        toks = []
        for e in engs:
            sem = m[e]
            assert self.items[e], f"fence on empty engine {e}"
            it = self.items[e][-1]
            if it[2] is None:
                self.cnt[sem] += 1
                it[2] = sem
                it[4] = (sem, self.cnt[sem])
            toks.append(it[4])
        return toks

    def emit(self, sems):
        self.sems = sems
        with self.nc.Block() as block:
            for eng in ENGS:
                if not self.items[eng]:
                    continue

                def mk(eng):
                    def body(e):
                        for (ws, fn, inc, amt, _tok, dma_id) in self.items[eng]:
                            for (s, v) in ws:
                                e.wait_ge(sems[s], v)
                            n0 = self.nc.n_instructions()
                            inst = fn(e)
                            if dma_id is not None:
                                self.dma_splits[dma_id] = self.nc.n_instructions() - n0
                            if inc is not None:
                                inst.then_inc(sems[inc], 16 if dma_id is not None else amt)
                    return body

                getattr(block, eng)(mk(eng))


def host_consts():
    i = np.arange(128)
    blk = i // 16
    ident32 = np.eye(128, dtype=np.float32)
    ma = np.zeros((9, 128), np.float32)
    mb0 = np.zeros((9, 128), np.float32)
    mb1 = np.zeros((9, 128), np.float32)
    ma[0, :] = NEG
    mb0[0, :] = 1.0
    mb1[0, :] = 1.0
    for b in range(8):
        ma[1 + b, blk == b] = -NEG
        mb0[1 + b, blk == b] = 1.0
        mb1[1 + b, blk == (b ^ 1)] = 1.0
    plo = np.zeros((128, 128), np.float32)
    phi = np.zeros((128, 128), np.float32)
    for m in range(64):
        plo[2 * m, m] = 1.0
        plo[2 * m + 1, m] = 1.0
        phi[2 * m, 64 + m] = 1.0
        phi[2 * m + 1, 64 + m] = 1.0
    oh = np.zeros((8, 8 * 128), np.float32)
    for e in range(8):
        oh[e, e * 128:(e + 1) * 128] = 1.0
    return {
        "c_ident32": ident32,
        "c_ma": ma.astype(bfdt), "c_mb0": mb0.astype(bfdt), "c_mb1": mb1.astype(bfdt),
        "c_plo": plo.astype(bfdt), "c_phi": phi.astype(bfdt),
        "c_oh16": oh.astype(bfdt),
    }


def build_program(stages=("ffn1", "attn", "ffn2"), debug=False,
                  attn_pairs=NPAIR, attn_heads=2, attn_chunks=None, attn_setup=3,
                  ffn2_ne=NE, ffn2_out=True, dma_amts=None):
    nc = bass.Bass("TRN2", target_bir_lowering=False)
    p = Prog(nc, dma_amts=dma_amts)

    for s_ in ("t", "v", "s", "g") + DMASEMS:
        p.op("sync", lambda e, s_=s_: e.sem_clear(p.sems[s_]))
    p.op("sync", lambda e: e.sem_inc(p.sems["b"], BAR))
    p.cnt["b"] = BAR
    p.bar_tok = ("b", BAR)

    def din(name, shape, dtype=F32):
        return nc.dram_tensor(name, list(shape), dtype, kind="ExternalInput").ap()

    x_h = din("x", (T, D))
    wq_h = din("wq", (D, D)); bq_h = din("bq_fm_in", (128, 8))
    wk_h = din("wk", (D, D)); bk_h = din("bk_fm_in", (128, 8))
    wv_h = din("wv", (D, D)); bv_h = din("bv_bc", (128, D))
    rw_h = din("router_w", (D, 8)); rb_h = din("rb_bc", (128, 8))
    rw1_h = din("routed_w1", (8, D, 512))
    rw3_h = din("routed_w3", (8, D, 512))
    rw2_h = din("routed_w2", (8, 512, D))
    sw1_h = din("shared_w1", (2, D, 512))
    sw3_h = din("shared_w3", (2, D, 512))
    sw2_h = din("shared_w2", (2, 512, D))
    id32_h = din("c_ident32", (128, 128))
    ma_h = din("c_ma", (9, 128), BF16)
    mb0_h = din("c_mb0", (9, 128), BF16)
    mb1_h = din("c_mb1", (9, 128), BF16)
    plo_h = din("c_plo", (128, 128), BF16)
    phi_h = din("c_phi", (128, 128), BF16)
    oh16_h = din("c_oh16", (8, 1024), BF16)

    y_h = nc.dram_tensor("y", [T, D], F32, kind="ExternalOutput").ap()
    dbg = {}
    if debug:
        for name, shape, dt_ in [
            ("d_xres1", (T, D), F32), ("d_xres2", (T, D), F32),
            ("d_xlnfm", (128, 8 * 2048), BF16), ("d_gb16", (8, 2048), BF16),
            ("d_q16", (128, 4096), BF16), ("d_k16", (128, 4096), BF16),
            ("d_acc", (65, 2048), F32), ("d_v16", (128, 2048), BF16),
        ]:
            dbg[name] = nc.dram_tensor(name, list(shape), dt_, kind="ExternalOutput").ap()

    A = {}

    def sb(name, shape, dtype):
        A[name] = nc.alloc_sbuf_tensor("sb_" + name, list(shape), dtype)
        return A[name]

    sb("x_res", (128, NCH * 1024), F32)
    sb("xln_fm", (128, 8 * 2048), BF16)
    sb("xln_tmp0", (128, 1024), F32); sb("xln_tmp1", (128, 1024), F32)
    sb("ident32", (128, 128), F32)
    sb("ma", (9, 128), BF16); sb("mb0", (9, 128), BF16); sb("mb1", (9, 128), BF16)
    sb("plo", (128, 128), BF16); sb("phi", (128, 128), BF16)
    sb("oh16", (8, 1024), BF16)
    sb("router_w16", (128, 64), BF16)
    sb("rb_bc", (128, 8), F32)
    sb("bq_fm", (128, 8), F32); sb("bk_fm", (128, 8), F32); sb("bq_s", (128, 8), F32)
    sb("bv_bc", (128, 1024), F32)
    sb("bnst", (128, 24), F32)
    sb("mv", (128, 8), F32)
    sb("rch", (128, 16), F32)
    sb("cells", (128, 8), F32)

    ARENA = 76800
    arena_t = nc.alloc_sbuf_tensor("arena", [128, ARENA], U8)
    arena_base = nc.lookup_mloc(arena_t).addr

    def place(name, shape, dtype, off):
        nbytes = int(np.prod(shape[1:])) * (2 if dtype == BF16 else 4)
        assert off % 32 == 0 and off + nbytes <= ARENA, (name, off, nbytes)
        A[name] = nc.alloc_sbuf_tensor_at(name, list(shape), dtype, offset=arena_base + off)
        return A[name]

    place("w13_0", (128, 8 * 1024), BF16, 0)
    place("w13_1", (128, 8 * 1024), BF16, 16384)
    place("w2_0", (128, 4 * 1024), BF16, 32768)
    place("w2_1", (128, 4 * 1024), BF16, 40960)
    place("h_sb0", (128, 4 * 256), BF16, 49152)
    place("h_sb1", (128, 4 * 256), BF16, 51200)
    place("s_tmp0", (128, 256), F32, 53248)
    place("s_tmp1", (128, 256), F32, 54272)
    place("h0_0", (128, 256), F32, 55296)
    place("h0_1", (128, 256), F32, 56320)
    place("logits", (128, 128), F32, 57344)
    place("gscr", (128, 64), F32, 57856)
    place("gb16", (8, 2048), BF16, 58112)
    place("wq0", (128, 1024), BF16, 0)
    place("wq1", (128, 1024), BF16, 2048)
    place("wk0", (128, 1024), BF16, 4096)
    place("wk1", (128, 1024), BF16, 6144)
    place("wv0", (128, 1024), BF16, 8192)
    place("wv1", (128, 1024), BF16, 10240)
    place("v16", (128, 2048), BF16, 12288)
    place("qp32a", (128, 2048), F32, 16384)
    place("qp32b", (128, 1024), F32, 24576)
    place("q16", (128, 4096), BF16, 28672)
    place("k16", (128, 4096), BF16, 36864)
    place("acc_a0", (65, 2048), F32, 45056)
    place("acc_b0", (65, 1024), F32, 53248)
    place("acc_a1", (65, 2048), F32, 64448)
    place("acc_b1", (65, 1024), F32, 72640)
    place("vlv0", (128, 16 * SLOTW), BF16, 57344)
    place("vlv1", (128, 16 * SLOTW), BF16, 59456)
    place("vstage0", (128, 80), BF16, 61568)
    place("vstage1", (128, 80), BF16, 61728)
    place("A_sb0", (128, 128), F32, 61888)
    place("A_sb1", (128, 128), F32, 62400)
    place("AT_sb0", (128, 128), BF16, 62912)
    place("AT_sb1", (128, 128), BF16, 63168)

    P = [nc.alloc_psum_tensor(f"P{i}", [128, 512], F32) for i in range(8)]

    p.dma("sync", A["x_res"][:].rearrange("p (c d) -> p c d", c=NCH),
          x_h.rearrange("(c p) d -> p c d", p=128))
    for nm, h in [("ident32", id32_h), ("ma", ma_h), ("mb0", mb0_h), ("mb1", mb1_h),
                  ("plo", plo_h), ("phi", phi_h), ("oh16", oh16_h),
                  ("rb_bc", rb_h), ("bv_bc", bv_h)]:
        p.dma("sync", A[nm][:], h)
    p.dma("sync", A["bq_fm"][:], bq_h)
    p.dma("sync", A["bk_fm"][:], bk_h)
    consts_tok = ("dS", p.cnt["dS"])
    x_tok = bq_tok = bk_tok = consts_tok
    rtr_tok = p.dma("gpsimd", A["router_w16"][:].rearrange("p (dc e) -> p dc e", dc=8),
                    rw_h.rearrange("(dc p) e -> p dc e", p=128), sem="dG")
    p.op("vector", lambda v: v.memset(A["cells"][:, 0:1], 1e-5))
    bqs_tok = p.op("vector", lambda v: v.tensor_scalar_mul(A["bq_s"][:], A["bq_fm"][:], 0.125),
                   waits=[bq_tok], inc="v")

    wtoks = {}
    wfree = {}

    def load_expert(e, extra_waits):
        b = e % 2
        s13 = "dA" if b == 0 else "dB"
        s2 = "dC" if b == 0 else "dD"
        if e < 8:
            w1, w3, w2 = rw1_h[e], rw3_h[e], rw2_h[e]
        else:
            w1, w3, w2 = sw1_h[e - 8], sw3_h[e - 8], sw2_h[e - 8]
        p.dma("gpsimd",
              A[f"w13_{b}"][:].rearrange("p (kc j) -> p kc j", kc=8)[:, :, 0:512],
              w1.rearrange("(kc p) j -> p kc j", p=128),
              waits=list(extra_waits) + wfree.get(f"w13_{b}", []), sem=s13)
        t2 = p.dma("gpsimd",
                   A[f"w13_{b}"][:].rearrange("p (kc j) -> p kc j", kc=8)[:, :, 512:1024],
                   w3.rearrange("(kc p) j -> p kc j", p=128), sem=s13)
        t3 = p.dma("gpsimd",
                   A[f"w2_{b}"][:].rearrange("p (jc d) -> p jc d", jc=4),
                   w2.rearrange("(jc p) d -> p jc d", p=128),
                   waits=list(extra_waits) + wfree.get(f"w2_{b}", []), sem=s2)
        wtoks[f"w13_{b}"] = t2
        wtoks[f"w2_{b}"] = t3
        wfree[f"w13_{b}"] = []
        wfree[f"w2_{b}"] = []

    guards = {
        "xln_tp": {},
        "ln_nb": {},
        "ln_sc": {},
        "gates_tp": {},
        "stmp": {},
        "hbuf": {},
    }

    def emit_ln(xres_toks):
        xln_ready = [None] * NCH
        for c0 in range(0, NCH, 2):
            agg_toks = {}
            for c in (c0, c0 + 1):
                pp = c % 2
                xs = A["x_res"][:, c * 1024:(c + 1) * 1024]
                w = [xres_toks[c]] if xres_toks else [x_tok]
                p.op("vector", lambda v, xs=xs, pp=pp: v.bn_stats(
                    A["bnst"][:, pp * 12:pp * 12 + 6], xs[:, 0:512]), waits=w)
                p.op("vector", lambda v, xs=xs, pp=pp: v.bn_stats(
                    A["bnst"][:, pp * 12 + 6:pp * 12 + 12], xs[:, 512:1024]), inc="v")
            for c in (c0, c0 + 1):
                pp = c % 2
                agg_toks[c] = p.op("vector", lambda v, pp=pp: v.bn_aggr(
                    A["mv"][:, pp * 2:pp * 2 + 2], A["bnst"][:, pp * 12:pp * 12 + 12]),
                    waits=[guards["ln_sc"].get(pp), ("v", p.cnt["v"])], inc="v")
            ln_toks = {}
            for c in (c0, c0 + 1):
                pp = c % 2
                ln_toks[c] = p.op("scalar", lambda s, pp=pp: s.activation(
                    A["rch"][:, pp * 4:pp * 4 + 1], A["mv"][:, pp * 2 + 1:pp * 2 + 2],
                    ACT.Ln, bias=A["cells"][:, 0:1], scale=1.0),
                    waits=[agg_toks[c], guards["ln_nb"].get(pp)], inc="s")
                guards["ln_sc"][pp] = ln_toks[c]
            r_toks = {}
            for c in (c0, c0 + 1):
                pp = c % 2
                r_toks[c] = p.op("scalar", lambda s, pp=pp: s.activation(
                    A["rch"][:, pp * 4 + 1:pp * 4 + 2], A["rch"][:, pp * 4:pp * 4 + 1],
                    ACT.Exp, bias=0.0, scale=-0.5), waits=[ln_toks[c]], inc="s")
            nb_toks = {}
            for c in (c0, c0 + 1):
                pp = c % 2
                nb_toks[c] = p.op("vector", lambda v, pp=pp: v.scalar_tensor_tensor(
                    A["rch"][:, pp * 4 + 2:pp * 4 + 3], A["mv"][:, pp * 2:pp * 2 + 1], -1.0,
                    A["rch"][:, pp * 4 + 1:pp * 4 + 2], ALU.mult, ALU.mult),
                    waits=[r_toks[c], agg_toks[c]], inc="v")
                guards["ln_nb"][pp] = nb_toks[c]
            for c in (c0, c0 + 1):
                pp = c % 2
                xs = A["x_res"][:, c * 1024:(c + 1) * 1024]
                a_tok = p.op("scalar", lambda s, xs=xs, pp=pp: s.activation(
                    A[f"xln_tmp{pp}"][:], xs, ACT.Identity,
                    bias=A["rch"][:, pp * 4 + 2:pp * 4 + 3],
                    scale=A["rch"][:, pp * 4 + 1:pp * 4 + 2]),
                    waits=[nb_toks[c], guards["xln_tp"].get(pp)], inc="s")
                ev_s = ev_v = None
                tp = None
                for dcc in range(8):
                    bi = dcc % 2
                    bank = P[bi]
                    tp = p.op("tensor", lambda t, pp=pp, dcc=dcc, bank=bank: t.transpose(
                        bank[:, 0:128], A[f"xln_tmp{pp}"][:, dcc * 128:(dcc + 1) * 128],
                        A["ident32"][:]),
                        waits=[a_tok, consts_tok] + p.bank_w(bi), inc="t")
                    dst = A["xln_fm"][:, dcc * 2048 + c * 128: dcc * 2048 + (c + 1) * 128]
                    if dcc % 4 < 2:
                        ev_s = p.op("scalar", lambda s, dst=dst, bank=bank: s.copy(
                            dst, bank[:, 0:128]), waits=[tp], inc="s")
                        p.bank_r(bi, ev_s)
                    else:
                        ev_v = p.op("vector", lambda v, dst=dst, bank=bank: v.tensor_copy(
                            dst, bank[:, 0:128]), waits=[tp], inc="v")
                        p.bank_r(bi, ev_v)
                guards["xln_tp"][pp] = tp
                xln_ready[c] = [ev_s, ev_v]
        return xln_ready

    def emit_gates(xln_ready):
        gb_ready = [None] * NCH
        lg_toks = [None] * NCH
        for c in range(NCH):
            bi = 2 + c % 2
            bank = P[bi]
            for dc in range(8):
                waits = []
                if dc == 0:
                    waits = list(xln_ready[c]) + [rtr_tok] + p.bank_w(bi)
                p.op("tensor", lambda t, dc=dc, c=c, bank=bank: t.matmul(
                    bank[:, 0:8],
                    A["xln_fm"][:, dc * 2048 + c * 128: dc * 2048 + (c + 1) * 128],
                    A["router_w16"][:, dc * 8:(dc + 1) * 8],
                    start=(dc == 0), stop=(dc == 7)),
                    waits=waits, inc=("t" if dc == 7 else None))
            mm = ("t", p.cnt["t"])
            lg = A["logits"][:, c * 8:(c + 1) * 8]
            ev = p.op("vector", lambda v, lg=lg, bank=bank: v.tensor_add(
                lg, bank[:, 0:8], A["rb_bc"][:]), waits=[mm, consts_tok], inc="v")
            p.bank_r(bi, ev)
            lg_toks[c] = ev
        g = A["gscr"]
        for c0 in range(0, NCH, 2):
            steps = {c0: [], c0 + 1: []}
            for c in (c0, c0 + 1):
                o = (c % 2) * 32
                lg = A["logits"][:, c * 8:(c + 1) * 8]
                st = steps[c]
                gtp = guards["gates_tp"].get(c % 2)
                st.append(("vector", lambda v, lg=lg, o=o: v.reduce_max(g[:, o:o + 1], lg, AX), [gtp, lg_toks[c]]))
                st.append(("vector", lambda v, lg=lg, o=o: v.reduce_max(g[:, o + 1:o + 2], lg, AX, negate=True), []))
                st.append(("vector", lambda v, lg=lg, o=o: v.tensor_single_scalar(g[:, o + 8:o + 16], lg, g[:, o:o + 1], ALU.is_ge), []))
                st.append(("vector", lambda v, lg=lg, o=o: v.scalar_tensor_tensor(g[:, o + 16:o + 24], g[:, o + 8:o + 16], NEG, lg, ALU.mult, ALU.add), []))
                st.append(("vector", lambda v, o=o: v.reduce_max(g[:, o + 2:o + 3], g[:, o + 16:o + 24], AX), []))
                st.append(("scalar", lambda s, lg=lg, o=o: s.activation(g[:, o + 8:o + 16], lg, ACT.Exp, bias=g[:, o + 1:o + 2], scale=1.0), ["v"]))
                st.append(("scalar", lambda s, o=o: s.activation(g[:, o + 3:o + 4], g[:, o + 2:o + 3], ACT.Exp, bias=g[:, o + 1:o + 2], scale=1.0), ["v"]))
                st.append(("vector", lambda v, lg=lg, o=o: v.tensor_single_scalar(g[:, o + 16:o + 24], lg, g[:, o + 2:o + 3], ALU.is_ge), ["s"]))
                st.append(("vector", lambda v, o=o: v.tensor_scalar_add(g[:, o + 4:o + 5], g[:, o + 3:o + 4], 1.0), ["s"]))
                st.append(("vector", lambda v, o=o: v.reciprocal(g[:, o + 5:o + 6], g[:, o + 4:o + 5]), []))
                st.append(("vector", lambda v, o=o: v.tensor_scalar_mul(g[:, o + 6:o + 7], g[:, o + 5:o + 6], 0.5), []))
                st.append(("vector", lambda v, o=o: v.scalar_tensor_tensor(g[:, o + 24:o + 32], g[:, o + 8:o + 16], g[:, o + 6:o + 7], g[:, o + 16:o + 24], ALU.mult, ALU.mult), ["s"]))
            for i in range(len(steps[c0])):
                for c in (c0, c0 + 1):
                    eng, fn, dep = steps[c][i]
                    w = []
                    if eng == "vector":
                        w.append(("v", p.cnt["v"]))
                    for d in dep:
                        if d == "v":
                            w.append(("v", p.cnt["v"]))
                        elif d == "s":
                            w.append(("s", p.cnt["s"]))
                        elif d is not None:
                            w.append(d)
                    p.op(eng, fn, waits=w, inc=("v" if eng == "vector" else "s"))
            for c in (c0, c0 + 1):
                bi = c % 2
                bank = P[bi]
                o = (c % 2) * 32
                last_v = ("v", p.cnt["v"])
                tp = p.op("tensor", lambda t, o=o, bank=bank: t.transpose(
                    bank[0:8, 0:128], g[:, o + 24:o + 32], A["ident32"][:]),
                    waits=[last_v] + p.bank_w(bi), inc="t")
                guards["gates_tp"][c % 2] = tp
                ev = p.op("vector", lambda v, c=c, bank=bank: v.tensor_copy(
                    A["gb16"][0:8, c * 128:(c + 1) * 128], bank[0:8, 0:128]),
                    waits=[tp], inc="v")
                p.bank_r(bi, ev)
                gb_ready[c] = ev
        return gb_ready

    def emit_experts(xln_ready, gb_ready, xres_in_toks, emit_out, ne=NE):
        xres_out = [None] * NCH
        xadd_tok = {}
        grot = [0]

        def p13_phase(e, tt):
            routed = e < 8
            w13 = A[f"w13_{e % 2}"]
            hbuf = (e * NT + tt) % 2
            h = A[f"h_sb{hbuf}"]
            h_war = guards["hbuf"].get(hbuf)
            gbc_tok = None
            if routed:
                waits = [gb_ready[2 * tt], gb_ready[2 * tt + 1], consts_tok] + p.bank_w(3)
                gbc_tok = p.op("tensor", lambda t, e=e, tt=tt: t.matmul(
                    P[3][:, 0:256], A["oh16"][:, e * 128:(e + 1) * 128],
                    A["gb16"][0:8, tt * 256:(tt + 1) * 256], start=True, stop=True),
                    waits=waits, inc="t")
            htt_last = None
            for jc in range(4):
                rot = grot[0] % 3
                grot[0] += 1
                bank = P[rot]
                for half in range(2):
                    for kc in range(8):
                        waits = []
                        if half == 0 and kc == 0:
                            waits = p.bank_w(rot) + [
                                wtoks[f"w13_{e % 2}"],
                                xln_ready[2 * tt][0], xln_ready[2 * tt][-1],
                                xln_ready[2 * tt + 1][0], xln_ready[2 * tt + 1][-1]]
                        co = 256 * half
                        p.op("tensor", lambda t, jc=jc, kc=kc, tt=tt, bank=bank, co=co, half=half, w13=w13: t.matmul(
                            bank[:, co:co + 256],
                            w13[:, kc * 1024 + half * 512 + jc * 128: kc * 1024 + half * 512 + (jc + 1) * 128],
                            A["xln_fm"][:, kc * 2048 + tt * 256: kc * 2048 + (tt + 1) * 256],
                            start=(kc == 0), stop=(kc == 7)),
                            waits=waits, inc=("t" if (half == 1 and kc == 7) else None))
                p13_tok = ("t", p.cnt["t"])
                st = A[f"s_tmp{jc % 2}"]
                h0 = A[f"h0_{jc % 2}"]
                silu = p.op("scalar", lambda s, bank=bank, st=st: s.activation(
                    st[:], bank[:, 0:256], ACT.Silu, bias=0.0, scale=1.0),
                    waits=[p13_tok, guards["stmp"].get(jc % 2)], inc="s")
                if routed:
                    m1 = p.op("vector", lambda v, bank=bank, st=st, h0=h0: v.scalar_tensor_tensor(
                        h0[:], bank[:, 256:512], 1.0, st[:], ALU.mult, ALU.mult),
                        waits=[silu, gbc_tok] + ([h_war] if jc == 0 else []), inc="v")
                    m2 = p.op("vector", lambda v, jc=jc, h=h, h0=h0: v.tensor_mul(
                        h[:, jc * 256:(jc + 1) * 256], h0[:], P[3][:, 0:256]),
                        waits=[m1], inc="v")
                    htt_last = m2
                    p.bank_readers[rot] = [silu, m1]
                    p.bank_r(3, m2)
                else:
                    m1 = p.op("vector", lambda v, bank=bank, st=st, jc=jc, h=h: v.scalar_tensor_tensor(
                        h[:, jc * 256:(jc + 1) * 256], bank[:, 256:512], 0.5, st[:],
                        ALU.mult, ALU.mult),
                        waits=[silu] + ([h_war] if jc == 0 else []), inc="v")
                    htt_last = m1
                    p.bank_readers[rot] = [silu, m1]
                guards["stmp"][jc % 2] = m1
            return {"e": e, "tt": tt, "hbuf": hbuf, "h_ready": htt_last,
                    "p13_tok": ("t", p.cnt["t"])}

        def w2_phase(st):
            e, tt, hbuf = st["e"], st["tt"], st["hbuf"]
            h = A[f"h_sb{hbuf}"]
            w2 = A[f"w2_{e % 2}"]
            last_w2 = None
            for tc in range(2):
                for dhh in range(2):
                    ob = 4 + 2 * tc + dhh
                    for jc in range(4):
                        waits = []
                        if jc == 0:
                            waits = [st["h_ready"], wtoks[f"w2_{e % 2}"]] + p.bank_w(ob)
                        p.op("tensor", lambda t, jc=jc, tc=tc, dhh=dhh, ob=ob, h=h, w2=w2: t.matmul(
                            P[ob][:],
                            h[:, jc * 256 + tc * 128: jc * 256 + (tc + 1) * 128],
                            w2[:, jc * 1024 + dhh * 512: jc * 1024 + (dhh + 1) * 512],
                            start=(jc == 0), stop=(jc == 3)),
                            waits=waits, inc=("t" if jc == 3 else None))
                    last_w2 = ("t", p.cnt["t"])
                    c = 2 * tt + tc
                    xs = A["x_res"][:, c * 1024 + dhh * 512: c * 1024 + (dhh + 1) * 512]
                    aw = [last_w2, xadd_tok.get((c, dhh))]
                    if e == 0 and xres_in_toks:
                        aw.append(xres_in_toks[c])
                    add = p.op("vector", lambda v, xs=xs, ob=ob: v.tensor_add(
                        xs, xs, P[ob][:]), waits=aw, inc="v")
                    xadd_tok[(c, dhh)] = add
                    p.bank_r(ob, add)
                    if e == ne - 1:
                        xres_out[c] = add
            guards["hbuf"][hbuf] = last_w2
            if e == ne - 1 and emit_out:
                c0 = 2 * tt
                p.dma("sync",
                      y_h.rearrange("(c p) d -> p c d", p=128)[:, c0:c0 + 2, :],
                      A["x_res"][:, c0 * 1024:(c0 + 2) * 1024].rearrange("p (c d) -> p c d", c=2),
                      waits=[xres_out[c0], xres_out[c0 + 1]])
            return last_w2

        pending = None
        exp_last_p13 = {}
        for e in range(ne):
            for tt in range(NT):
                st = p13_phase(e, tt)
                exp_last_p13[e] = st["p13_tok"]
                if pending is not None:
                    last_w2 = w2_phase(pending)
                    if pending["tt"] == NT - 1:
                        ed = pending["e"]
                        wfree[f"w13_{ed % 2}"] = [exp_last_p13[ed]]
                        wfree[f"w2_{ed % 2}"] = [last_w2]
                        if ed + 2 < ne:
                            load_expert(ed + 2, [])
                pending = st
        last_w2 = w2_phase(pending)
        wfree[f"w13_{(ne - 1) % 2}"] = [exp_last_p13[ne - 1]]
        wfree[f"w2_{(ne - 1) % 2}"] = [last_w2]
        return xres_out

    def emit_attention(xln_ready, xres_in_toks):
        xres_out = list(xres_in_toks)
        ffn_drain = p.fence()
        ones_tok = None
        if attn_setup >= 1:
            for nm in ("vstage0", "vstage1"):
                p.op("vector", lambda v, nm=nm: v.memset(A[nm][:, 64:65], 1.0), waits=ffn_drain)
            for nm in ("vlv0", "vlv1"):
                ones_tok = p.op("vector", lambda v, nm=nm: v.memset(
                    A[nm][:].rearrange("p (s c) -> p s c", c=SLOTW)[:, :, 64:65], 1.0),
                    waits=ffn_drain, inc="v")

        wpair_free = {}

        def load_pair(dc):
            b = dc % 2
            spair = "dE" if b == 0 else "dF"
            tok = None
            for nm, wh in (("wq", wq_h), ("wk", wk_h), ("wv", wv_h)):
                key = f"{nm}{b}"
                tok = p.dma(
                    "gpsimd", A[key][:].rearrange("p (kc m) -> p kc m", kc=8),
                    wh.rearrange("(kc p) n -> p kc n", p=128)[:, :, dc * 128:(dc + 1) * 128],
                    waits=wpair_free.get(key, list(ffn_drain)), sem=spair)
            for nm in ("wq", "wk", "wv"):
                wtoks[f"{nm}{b}"] = tok

        if attn_setup >= 2:
            load_pair(0)
            load_pair(1)

        pair_war_t = None
        acc_war_t = None
        prev_carry = {}
        for dc in range(attn_pairs):
            b = dc % 2
            v16_toks = []
            for c in range(NCH):
                bi = 4 + c % 2
                bank = P[bi]
                for kc in range(8):
                    waits = []
                    if kc == 0:
                        waits = [wtoks[f"wv{b}"], xln_ready[c][0], xln_ready[c][-1]] + p.bank_w(bi)
                    p.op("tensor", lambda t, kc=kc, c=c, bank=bank, b=b: t.matmul(
                        bank[:, 0:128],
                        A["xln_fm"][:, kc * 2048 + c * 128: kc * 2048 + (c + 1) * 128],
                        A[f"wv{b}"][:, kc * 128:(kc + 1) * 128],
                        start=(kc == 0), stop=(kc == 7)),
                        waits=waits, inc=("t" if kc == 7 else None))
                mm = ("t", p.cnt["t"])
                ev = p.op("vector", lambda v, c=c, bank=bank, dc=dc: v.tensor_add(
                    A["v16"][:, c * 128:(c + 1) * 128], bank[:, 0:128],
                    A["bv_bc"][:, dc * 128:(dc + 1) * 128]),
                    waits=[mm, consts_tok] + ([pair_war_t] if c == 0 else []), inc="v")
                p.bank_r(bi, ev)
                v16_toks.append(ev)

            def proj_pool(wkey, bias_ap, scale, dst16, pre_mm, pre_ev):
                wap = A[wkey]
                ev_toks = []
                for tile in range(4):
                    bi = tile % 4
                    bank = P[bi]
                    for kc in range(8):
                        waits = []
                        if kc == 0:
                            waits = [wtoks[wkey]] + p.bank_w(bi) + pre_mm + \
                                    [xln_ready[NCH - 1][0], xln_ready[NCH - 1][-1]]
                        p.op("tensor", lambda t, kc=kc, tile=tile, bank=bank, wap=wap: t.matmul(
                            bank[:],
                            wap[:, kc * 128:(kc + 1) * 128],
                            A["xln_fm"][:, kc * 2048 + tile * 512: kc * 2048 + (tile + 1) * 512],
                            start=(kc == 0), stop=(kc == 7)),
                            waits=waits, inc=("t" if kc == 7 else None))
                    mm = ("t", p.cnt["t"])
                    ev = p.op("scalar", lambda s, tile=tile, bank=bank, bias_ap=bias_ap, scale=scale: s.activation(
                        A["qp32a"][:, tile * 512:(tile + 1) * 512], bank[:], ACT.Identity,
                        bias=bias_ap, scale=scale),
                        waits=[mm] + (pre_ev if tile == 0 else []), inc="s")
                    ev_toks.append(ev)
                    p.bank_r(bi, ev)
                cast_last = p.op("scalar", lambda s, dst16=dst16: s.activation(
                    dst16[:, 0:2048], A["qp32a"][:, 0:2048], ACT.Copy, bias=0.0, scale=1.0),
                    waits=[ev_toks[-1]], inc="s")
                prev_pool = None
                for lv in range(1, 6):
                    n = LV_N[lv]
                    sbp, offp = LV_P32[lv - 1]
                    sbc, offc = LV_P32[lv]
                    srcp = A["qp32a"] if sbp == "a" else A["qp32b"]
                    srcc = A["qp32a"] if sbc == "a" else A["qp32b"]
                    np_ = LV_N[lv - 1]
                    pw = [prev_pool, cast_last] if prev_pool else [ev_toks[-1], cast_last]
                    prev_pool = p.op("vector", lambda v, a=srcp, bb=srcc, op_=offp, oc=offc, n=n, np_=np_: v.tensor_add(
                        bb[:, oc:oc + n], a[:, op_:op_ + np_:2], a[:, op_ + 1:op_ + np_:2]),
                        waits=pw, inc="v")
                    cast_last = p.op("scalar", lambda s, dst16=dst16, lv=lv, bb=srcc, oc=offc, n=n: s.activation(
                        dst16[:, LV_OFF[lv]:LV_OFF[lv] + n], bb[:, oc:oc + n],
                        ACT.Copy, bias=0.0, scale=0.5 ** lv), waits=[prev_pool], inc="s")
                return cast_last, prev_pool

            q_cast, q_pool = proj_pool(f"wq{b}", A["bq_s"][:, dc:dc + 1], 0.125, A["q16"],
                                       [bqs_tok], [pair_war_t] if pair_war_t else [])
            k_cast, k_pool = proj_pool(f"wk{b}", A["bk_fm"][:, dc:dc + 1], 1.0, A["k16"],
                                       [bk_tok], [q_cast, q_pool])
            qk_done = k_cast

            vlv_toks = {0: {}, 1: {}}
            first_vlv = [True]
            for lv in range(1, 6):
                nout = max(1, LV_N[lv] // 128)
                for si in range(nout):
                    bi = 6 + si % 2
                    bank = P[bi]
                    slot = LV_SLOT0[lv] + si
                    if lv == 1:
                        w0 = [v16_toks[2 * si], v16_toks[2 * si + 1], ones_tok, consts_tok] + p.bank_w(bi)
                        p.op("tensor", lambda t, si=si, bank=bank: t.matmul(
                            bank[:, 0:128], A["plo"][:],
                            A["v16"][:, (2 * si) * 128:(2 * si + 1) * 128],
                            start=True, stop=False), waits=w0)
                        p.op("tensor", lambda t, si=si, bank=bank: t.matmul(
                            bank[:, 0:128], A["phi"][:],
                            A["v16"][:, (2 * si + 1) * 128:(2 * si + 2) * 128],
                            start=False, stop=True), inc="t")
                        mm = ("t", p.cnt["t"])
                        for hh in range(2):
                            ev = p.op("vector", lambda v, hh=hh, slot=slot, bank=bank: v.tensor_copy(
                                A[f"vlv{hh}"][:, slot * SLOTW: slot * SLOTW + 64],
                                bank[:, hh * 64:hh * 64 + 64]), waits=[mm], inc="v")
                            vlv_toks[hh][(lv, si)] = ev
                            p.bank_r(bi, ev)
                        first_vlv[0] = False
                    elif lv < 5:
                        for hh in range(2):
                            bi = 6 + (2 * si + hh) % 2
                            bank = P[bi]
                            sp = LV_SLOT0[lv - 1] + 2 * si
                            w0 = [vlv_toks[hh][(lv - 1, 2 * si)], vlv_toks[hh][(lv - 1, 2 * si + 1)]] + p.bank_w(bi)
                            p.op("tensor", lambda t, hh=hh, sp=sp, bank=bank: t.matmul(
                                bank[:, hh * 64:hh * 64 + 64], A["plo"][:],
                                A[f"vlv{hh}"][:, sp * SLOTW: sp * SLOTW + 64],
                                start=True, stop=False), waits=w0)
                            p.op("tensor", lambda t, hh=hh, sp=sp, bank=bank: t.matmul(
                                bank[:, hh * 64:hh * 64 + 64], A["phi"][:],
                                A[f"vlv{hh}"][:, (sp + 1) * SLOTW: (sp + 1) * SLOTW + 64],
                                start=False, stop=True), inc="t")
                            mm = ("t", p.cnt["t"])
                            ev = p.op("vector", lambda v, hh=hh, slot=slot, bank=bank: v.tensor_copy(
                                A[f"vlv{hh}"][:, slot * SLOTW: slot * SLOTW + 64],
                                bank[:, hh * 64:hh * 64 + 64]), waits=[mm], inc="v")
                            vlv_toks[hh][(lv, si)] = ev
                            p.bank_r(bi, ev)
                    else:
                        for hh in range(2):
                            bi = 6 + hh % 2
                            bank = P[bi]
                            sp = LV_SLOT0[4]
                            w0 = [vlv_toks[hh][(4, 0)]] + p.bank_w(bi)
                            p.op("tensor", lambda t, hh=hh, sp=sp, bank=bank: t.matmul(
                                bank[0:64, hh * 64:hh * 64 + 64], A["plo"][:, 0:64],
                                A[f"vlv{hh}"][:, sp * SLOTW: sp * SLOTW + 64],
                                start=True, stop=True), waits=w0, inc="t")
                            mm = ("t", p.cnt["t"])
                            ev = p.op("vector", lambda v, hh=hh, slot=slot, bank=bank: v.tensor_copy(
                                A[f"vlv{hh}"][0:64, slot * SLOTW: slot * SLOTW + 64],
                                bank[0:64, hh * 64:hh * 64 + 64]), waits=[mm], inc="v")
                            vlv_toks[hh][(lv, 0)] = ev
                            p.bank_r(bi, ev)

            chunks = [(hh, lv, ci)
                      for lv in range(5, -1, -1)
                      for ci in range(LV_NC[lv])
                      for hh in range(attn_heads)]
            if attn_chunks is not None:
                chunks = chunks[:attn_chunks]
            nchk = len(chunks)
            toks = {}

            def tprev(name, k):
                if k >= 0:
                    return toks.get((name, k))
                return prev_carry.get((name, k % 2))

            def stage_S(k):
                hh, lv, ci = chunks[k]
                cw = LV_CW[lv]
                rows = slice(64 * hh, 64 * hh + 64)
                bi = k % 2
                bank = P[bi]
                mb = A["mb0"] if lv == 0 else A["mb1"]
                qap = A["q16"][rows, LV_OFF[lv] + ci * 128: LV_OFF[lv] + ci * 128 + cw]
                kap = A["k16"][rows, LV_OFF[lv] + ci * 128: LV_OFF[lv] + ci * 128 + cw]
                w = [qk_done, consts_tok] + p.bank_w(bi)
                p.op("tensor", lambda t, qap=qap, kap=kap, cw=cw, bank=bank: t.matmul(
                    bank[0:cw, 0:cw], qap, kap, start=True, stop=False), waits=w)
                p.op("tensor", lambda t, cw=cw, bank=bank, mb=mb: t.matmul(
                    bank[0:cw, 0:cw], A["ma"][:, 0:cw], mb[:, 0:cw],
                    start=False, stop=True), inc="t")
                toks[("S", k)] = ("t", p.cnt["t"])

            def stage_rmax(k):
                hh, lv, ci = chunks[k]
                cw = LV_CW[lv]
                pp = k % 2
                bank = P[pp]
                w = [toks[("S", k)], tprev("exp", k - 2)]
                tok = p.op("vector", lambda v, cw=cw, pp=pp, bank=bank: v.reduce_max(
                    A["rch"][0:cw, 8 + pp:9 + pp], bank[0:cw, 0:cw], AX, negate=True),
                    waits=w, inc="v")
                toks[("rmax", k)] = tok
                p.bank_r(pp, tok)

            def stage_exp(k):
                hh, lv, ci = chunks[k]
                cw = LV_CW[lv]
                pp = k % 2
                bank = P[pp]
                w = [toks[("rmax", k)], tprev("tpA", k - 2)]
                tok = p.op("scalar", lambda s, cw=cw, pp=pp, bank=bank: s.activation(
                    A[f"A_sb{pp}"][0:cw, 0:cw], bank[0:cw, 0:cw], ACT.Exp,
                    bias=A["rch"][0:cw, 8 + pp:9 + pp], scale=1.0), waits=w, inc="s")
                toks[("exp", k)] = tok
                p.bank_r(pp, tok)

            def stage_tpA(k):
                hh, lv, ci = chunks[k]
                cw = LV_CW[lv]
                pp = k % 2
                bi = 2 + pp
                bank = P[bi]
                w = [toks[("exp", k)]] + p.bank_w(bi)
                toks[("tpA", k)] = p.op("tensor", lambda t, cw=cw, pp=pp, bank=bank: t.transpose(
                    bank[0:cw, 0:cw], A[f"A_sb{pp}"][0:cw, 0:cw],
                    A["ident32"][0:cw, 0:cw]), waits=w, inc="t")

            def stage_atev(k):
                hh, lv, ci = chunks[k]
                cw = LV_CW[lv]
                pp = k % 2
                bi = 2 + pp
                bank = P[bi]
                w = [toks[("tpA", k)], tprev("y", k - 2)]
                tok = p.op("scalar", lambda s, cw=cw, pp=pp, bank=bank: s.copy(
                    A[f"AT_sb{pp}"][0:cw, 0:cw], bank[0:cw, 0:cw]), waits=w, inc="s")
                toks[("atev", k)] = tok
                p.bank_r(bi, tok)

            def stage_vst(k):
                hh, lv, ci = chunks[k]
                if lv != 0:
                    return
                pp = k % 2
                w = [v16_toks[ci], ones_tok, tprev("y", k - 2)]
                src_ = A["v16"][:, ci * 128 + hh * 64: ci * 128 + hh * 64 + 64]
                toks[("vst", k)] = p.op("vector", lambda v, pp=pp, src_=src_: v.tensor_copy(
                    A[f"vstage{pp}"][:, 0:64], src_), waits=w, inc="v")

            def stage_y(k):
                hh, lv, ci = chunks[k]
                cw = LV_CW[lv]
                pp = k % 2
                bi = 4 + pp
                bank = P[bi]
                if lv == 0:
                    lhs = A[f"vstage{pp}"][:, 0:65]
                    lw = [toks[("vst", k)]]
                else:
                    slot = LV_SLOT0[lv] + ci
                    lhs = A[f"vlv{hh}"][0:cw, slot * SLOTW: slot * SLOTW + 65]
                    lw = [vlv_toks[hh][(lv, ci)], ones_tok]
                w = [toks[("atev", k)]] + lw + p.bank_w(bi)
                p.op("tensor", lambda t, lhs=lhs, cw=cw, pp=pp, bank=bank: t.matmul(
                    bank[0:65, 0:cw], lhs, A[f"AT_sb{pp}"][0:cw, 0:cw],
                    start=True, stop=True), waits=w, inc="t")
                toks[("y", k)] = ("t", p.cnt["t"])

            comb_last = {}

            def stage_comb(k):
                hh, lv, ci = chunks[k]
                pp = k % 2
                bi = 4 + pp
                bank = P[bi]
                acc_cur = A[f"acc_a{hh}"] if LV_ACC[lv] == "a" else A[f"acc_b{hh}"]
                acc_prev = A[f"acc_b{hh}"] if LV_ACC[lv] == "a" else A[f"acc_a{hh}"]
                w = [toks[("y", k)], comb_last.get(hh)]
                if k in (2, 3) and acc_war_t is not None:
                    w.append(acc_war_t)
                if lv == 5:
                    tok = p.op("vector", lambda v, bank=bank, hh=hh: v.tensor_copy(
                        A[f"acc_b{hh}"][0:65, 0:64], bank[0:65, 0:64]), waits=w, inc="v")
                else:
                    s0 = ci * 128
                    tok = p.op("vector", lambda v, acc_cur=acc_cur, acc_prev=acc_prev, s0=s0, bank=bank: v.tensor_add(
                        acc_cur[0:65, s0:s0 + 128].rearrange("p (a b) -> p a b", b=2),
                        bank[0:65, 0:128].rearrange("p (a b) -> p a b", b=2),
                        acc_prev[0:65, s0 // 2: s0 // 2 + 64].unsqueeze(2).broadcast_to([65, 64, 2])),
                        waits=w, inc="v")
                toks[("comb", k)] = tok
                comb_last[hh] = tok
                p.bank_r(bi, tok)

            for k in range(nchk + 2):
                if k >= 2:
                    stage_y(k - 2)
                    stage_comb(k - 2)
                if k < nchk:
                    stage_S(k)
                    stage_vst(k)
                if 1 <= k <= nchk:
                    stage_rmax(k - 1)
                    stage_exp(k - 1)
                    stage_tpA(k - 1)
                    stage_atev(k - 1)
            for nm_ in ("exp", "tpA", "y"):
                for pp in range(2):
                    kk = nchk - 1 - ((nchk - 1 - pp) % 2)
                    prev_carry[(nm_, pp)] = toks.get((nm_, kk))

            last_tp = None
            for hh in range(attn_heads):
                headcol0 = (2 * dc + hh) * 64
                for c0 in range(0, NCH, 2):
                    tps = {}
                    for ci in (c0, c0 + 1):
                        pp = ci % 2
                        bi = 6 + pp
                        bank = P[bi]
                        w = [comb_last.get(hh)] + p.bank_w(bi)
                        tps[ci] = p.op("tensor", lambda t, ci=ci, bank=bank, hh=hh: t.transpose(
                            bank[:, 0:65], A[f"acc_a{hh}"][0:65, ci * 128:(ci + 1) * 128],
                            A["ident32"][0:65, 0:65]), waits=w, inc="t")
                        last_tp = tps[ci]
                    d_toks = {}
                    for ci in (c0, c0 + 1):
                        pp = ci % 2
                        bank = P[6 + pp]
                        d_toks[ci] = p.op("vector", lambda v, bank=bank, pp=pp: v.tensor_scalar_add(
                            A["rch"][:, 10 + pp:11 + pp], bank[:, 64:65], 1e-8),
                            waits=[tps[ci], ("v", p.cnt["v"])], inc="v")
                        p.bank_r(6 + pp, d_toks[ci])
                    r_toks2 = {}
                    for ci in (c0, c0 + 1):
                        pp = ci % 2
                        r_toks2[ci] = p.op("vector", lambda v, pp=pp: v.reciprocal(
                            A["rch"][:, 12 + pp:13 + pp], A["rch"][:, 10 + pp:11 + pp]),
                            waits=[d_toks[ci]], inc="v")
                    for ci in (c0, c0 + 1):
                        pp = ci % 2
                        bank = P[6 + pp]
                        xs = A["x_res"][:, ci * 1024 + headcol0: ci * 1024 + headcol0 + 64]
                        stk = p.op("vector", lambda v, xs=xs, bank=bank, pp=pp: v.scalar_tensor_tensor(
                            xs, bank[:, 0:64], A["rch"][:, 12 + pp:13 + pp], xs,
                            ALU.mult, ALU.add), waits=[r_toks2[ci], xres_out[ci]], inc="v")
                        p.bank_r(6 + pp, stk)
                        xres_out[ci] = stk
            acc_war_t = last_tp
            pair_war_t = ("t", p.cnt["t"])
            for nm in ("wq", "wk", "wv"):
                wpair_free[f"{nm}{b}"] = [pair_war_t]
            if dc + 2 < NPAIR:
                load_pair(dc + 2)
        return xres_out

    xres_toks = None
    if "ffn1" in stages:
        load_expert(0, [])
        load_expert(1, [])
        xln = emit_ln(None)
        gb = emit_gates(xln)
        xres_toks = emit_experts(xln, gb, None, emit_out=False)
        if debug:
            p.dma("sync", dbg["d_xlnfm"], A["xln_fm"][:],
                  waits=[xln[NCH - 1][0], xln[NCH - 1][-1]])
            p.dma("sync", dbg["d_gb16"], A["gb16"][:], waits=[gb[NCH - 1]])
            dump_tok = p.dma("sync", dbg["d_xres1"].rearrange("(c p) d -> p c d", p=128),
                             A["x_res"][:].rearrange("p (c d) -> p c d", c=NCH),
                             waits=xres_toks)
            xres_toks = [dump_tok] * NCH
    if "attn" in stages:
        if attn_pairs == -2:
            xln2 = emit_ln(xres_toks)
        elif attn_pairs >= 0:
            xln2 = emit_ln(xres_toks)
            xres_toks = emit_attention(xln2, xres_toks if xres_toks else [x_tok] * NCH)
        if debug:
            fl = p.fence(("vector", "scalar"))
            p.dma("sync", dbg["d_q16"], A["q16"][:], waits=fl)
            p.dma("sync", dbg["d_k16"], A["k16"][:])
            p.dma("sync", dbg["d_acc"], A["acc_a0"][:])
            p.dma("sync", dbg["d_v16"], A["v16"][:])
            dump_tok = p.dma("sync", dbg["d_xres2"].rearrange("(c p) d -> p c d", p=128),
                             A["x_res"][:].rearrange("p (c d) -> p c d", c=NCH),
                             waits=xres_toks)
            xres_toks = [dump_tok] * NCH
    if "ffn2" in stages:
        attn_drain = p.fence()
        load_expert(0, attn_drain)
        load_expert(1, attn_drain)
        xln3 = emit_ln(xres_toks)
        gb3 = emit_gates(xln3)
        xres_toks = emit_experts(xln3, gb3, xres_toks, emit_out=ffn2_out, ne=ffn2_ne)
        if not ffn2_out:
            fl = p.fence(("vector",))
            p.dma("sync", y_h.rearrange("(c p) d -> p c d", p=128),
                  A["x_res"][:].rearrange("p (c d) -> p c d", c=NCH),
                  waits=[tk for tk in xres_toks if tk] + fl)
    else:
        fl = p.fence(("vector",))
        p.dma("sync", y_h.rearrange("(c p) d -> p c d", p=128),
              A["x_res"][:].rearrange("p (c d) -> p c d", c=NCH),
              waits=(xres_toks if xres_toks else [x_tok]) + fl)

    p.op("sync", lambda e: e.nop(),
         waits=[(s_, p.cnt[s_]) for s_ in DMASEMS if p.cnt[s_] > 0])

    import contextlib
    with contextlib.ExitStack() as stck:
        sems = {s: stck.enter_context(nc.semaphore(s)) for s in SEMS}
        p.emit(sems)
    return nc, dbg, p


_CACHE = {}


def get_program(stages=("ffn1", "attn", "ffn2"), debug=False, **kw):
    key = (tuple(stages), debug, tuple(sorted(kw.items())))
    if key not in _CACHE:
        nc1, dbg1, p1 = build_program(stages, debug, **kw)
        amts = [16 * p1.dma_splits[i] for i in range(p1.dma_idx)]
        if all(a == 16 for a in amts):
            _CACHE[key] = (nc1, dbg1)
        else:
            nc2, dbg2, p2 = build_program(stages, debug, dma_amts=amts, **kw)
            assert p2.dma_idx == p1.dma_idx
            _CACHE[key] = (nc2, dbg2)
    return _CACHE[key]


def make_inmaps(inputs):
    consts = host_consts()
    base = {
        "wq": inputs["L0_attn_Wq"],
        "bq_fm_in": np.ascontiguousarray(np.asarray(inputs["L0_attn_bq"]).reshape(8, 128).T),
        "wk": inputs["L0_attn_Wk"],
        "bk_fm_in": np.ascontiguousarray(np.asarray(inputs["L0_attn_bk"]).reshape(8, 128).T),
        "wv": inputs["L0_attn_Wv"],
        "bv_bc": np.ascontiguousarray(np.broadcast_to(inputs["L0_attn_bv"], (128, D))),
        "router_w": inputs["L0_ffn_router_w"],
        "rb_bc": np.ascontiguousarray(np.broadcast_to(inputs["L0_ffn_router_b"], (128, 8))),
        "routed_w1": inputs["L0_ffn_routed_w1"],
        "routed_w3": inputs["L0_ffn_routed_w3"],
        "routed_w2": inputs["L0_ffn_routed_w2"],
        "shared_w1": inputs["L0_ffn_shared_w1"],
        "shared_w3": inputs["L0_ffn_shared_w3"],
        "shared_w2": inputs["L0_ffn_shared_w2"],
    }
    base.update(consts)
    x = inputs["L0_attn_hidden_states"]
    return [dict(base, x=np.ascontiguousarray(x[c])) for c in range(8)]


def kernel(**inputs):
    nc, _ = get_program()
    maps = make_inmaps(inputs)
    res = run_bass_kernel_spmd(nc, maps, core_ids=list(range(8)))
    return np.stack([np.asarray(res.results[c]["y"], dtype=np.float32) for c in range(8)], axis=0)


# version 04 (final)
# speedup vs baseline: 1.1132x; 1.1132x over previous
import numpy as np
import ml_dtypes
import concourse.bass as bass
import concourse.mybir as mybir
from concourse.bass_utils import run_bass_kernel_spmd

F32 = mybir.dt.float32
BF16 = mybir.dt.bfloat16
U8 = mybir.dt.uint8
ALU = mybir.AluOpType
ACT = mybir.ActivationFunctionType
AX = mybir.AxisListType.X
bfdt = ml_dtypes.bfloat16

T, D = 2048, 1024
NCH = 16
NPAIR = 8
NE = 10
NT = 8
NEG = -28672.0

LV_N = [2048, 1024, 512, 256, 128, 64]
LV_OFF = [0, 2048, 3072, 3584, 3840, 3968]
LV_NC = [16, 8, 4, 2, 1, 1]
LV_CW = [128, 128, 128, 128, 128, 64]
LV_P32 = [("a", 0), ("b", 0), ("a", 0), ("b", 0), ("a", 512), ("b", 256)]
LV_ACC = ["a", "b", "a", "b", "a", "b"]
LV_SLOT0 = {1: 0, 2: 8, 3: 12, 4: 14, 5: 15}
SLOTW = 66

ENGS = ("sync", "tensor", "vector", "scalar", "gpsimd")
SEMS = ("t", "v", "s", "g", "dS", "dG", "dA", "dB", "dC", "dD", "dE", "dF", "b")
DMASEMS = ("dS", "dG", "dA", "dB", "dC", "dD", "dE", "dF")
BAR = 1024


class Prog:
    def __init__(self, nc, dma_amts=None):
        self.nc = nc
        self.items = {e: [] for e in ENGS}
        self.cnt = {s: 0 for s in SEMS}
        self.lastw = {e: {} for e in ENGS}
        self.bank_readers = {i: [] for i in range(8)}
        self.dma_amts = dma_amts
        self.dma_idx = 0
        self.dma_splits = {}
        self.sems = None
        self.bar_tok = None

    def op(self, eng, fn, waits=(), inc=None, amt=1, dma_id=None):
        ws = []
        if self.bar_tok is not None and eng != "sync":
            waits = [self.bar_tok] + list(waits)
        for w in waits:
            if w is None:
                continue
            s, v = w
            if v <= 0:
                continue
            assert v <= self.cnt[s], f"forward-token violation: {s}={v} > {self.cnt[s]}"
            if self.lastw[eng].get(s, 0) < v:
                ws.append((s, v))
                self.lastw[eng][s] = v
        tok = None
        if inc is not None:
            self.cnt[inc] += amt
            tok = (inc, self.cnt[inc])
        self.items[eng].append([ws, fn, inc, amt, tok, dma_id])
        return tok

    def dma(self, q, out, in_, waits=(), sem=None):
        if sem is None:
            sem = "dS" if q == "sync" else "dG"
        idx = self.dma_idx
        self.dma_idx += 1
        amt = 16 if self.dma_amts is None else self.dma_amts[idx]
        return self.op(q, lambda e, o=out, i=in_: e.dma_start(o, i), waits=waits,
                       inc=sem, amt=amt, dma_id=idx)

    def bank_w(self, i):
        w = self.bank_readers[i]
        self.bank_readers[i] = []
        return w

    def bank_r(self, i, tok):
        self.bank_readers[i].append(tok)

    def fence(self, engs=("tensor", "vector", "scalar")):
        m = {"tensor": "t", "vector": "v", "scalar": "s", "gpsimd": "g"}
        toks = []
        for e in engs:
            sem = m[e]
            assert self.items[e], f"fence on empty engine {e}"
            it = self.items[e][-1]
            if it[2] is None:
                self.cnt[sem] += 1
                it[2] = sem
                it[4] = (sem, self.cnt[sem])
            toks.append(it[4])
        return toks

    def emit(self, sems):
        self.sems = sems
        with self.nc.Block() as block:
            for eng in ENGS:
                if not self.items[eng]:
                    continue

                def mk(eng):
                    def body(e):
                        for (ws, fn, inc, amt, _tok, dma_id) in self.items[eng]:
                            for (s, v) in ws:
                                e.wait_ge(sems[s], v)
                            n0 = self.nc.n_instructions()
                            inst = fn(e)
                            if dma_id is not None:
                                self.dma_splits[dma_id] = self.nc.n_instructions() - n0
                            if inc is not None:
                                inst.then_inc(sems[inc], 16 if dma_id is not None else amt)
                    return body

                getattr(block, eng)(mk(eng))


def host_consts():
    i = np.arange(128)
    blk = i // 16
    ident32 = np.eye(128, dtype=np.float32)
    ma = np.zeros((9, 128), np.float32)
    mb0 = np.zeros((9, 128), np.float32)
    mb1 = np.zeros((9, 128), np.float32)
    ma[0, :] = NEG
    mb0[0, :] = 1.0
    mb1[0, :] = 1.0
    for b in range(8):
        ma[1 + b, blk == b] = -NEG
        mb0[1 + b, blk == b] = 1.0
        mb1[1 + b, blk == (b ^ 1)] = 1.0
    plo = np.zeros((128, 128), np.float32)
    phi = np.zeros((128, 128), np.float32)
    for m in range(64):
        plo[2 * m, m] = 1.0
        plo[2 * m + 1, m] = 1.0
        phi[2 * m, 64 + m] = 1.0
        phi[2 * m + 1, 64 + m] = 1.0
    oh = np.zeros((8, 8 * 128), np.float32)
    for e in range(8):
        oh[e, e * 128:(e + 1) * 128] = 1.0
    return {
        "c_ident32": ident32,
        "c_ma": ma.astype(bfdt), "c_mb0": mb0.astype(bfdt), "c_mb1": mb1.astype(bfdt),
        "c_plo": plo.astype(bfdt), "c_phi": phi.astype(bfdt),
        "c_oh16": oh.astype(bfdt),
    }


def build_program(stages=("ffn1", "attn", "ffn2"), debug=False,
                  attn_pairs=NPAIR, attn_heads=2, attn_chunks=None, attn_setup=3,
                  ffn2_ne=NE, ffn2_out=True, dma_amts=None):
    nc = bass.Bass("TRN2", target_bir_lowering=False)
    p = Prog(nc, dma_amts=dma_amts)

    for s_ in ("t", "v", "s", "g") + DMASEMS:
        p.op("sync", lambda e, s_=s_: e.sem_clear(p.sems[s_]))
    p.op("sync", lambda e: e.sem_inc(p.sems["b"], BAR))
    p.cnt["b"] = BAR
    p.bar_tok = ("b", BAR)

    def din(name, shape, dtype=F32):
        return nc.dram_tensor(name, list(shape), dtype, kind="ExternalInput").ap()

    x_h = din("x", (T, D))
    wq_h = din("wq", (D, D)); bq_h = din("bq_fm_in", (128, 8))
    wk_h = din("wk", (D, D)); bk_h = din("bk_fm_in", (128, 8))
    wv_h = din("wv", (D, D)); bv_h = din("bv_bc", (128, D))
    rw_h = din("router_w", (D, 8)); rb_h = din("rb_bc", (128, 8))
    rw1_h = din("routed_w1", (8, D, 512))
    rw3_h = din("routed_w3", (8, D, 512))
    rw2_h = din("routed_w2", (8, 512, D))
    sw1_h = din("shared_w1", (2, D, 512))
    sw3_h = din("shared_w3", (2, D, 512))
    sw2_h = din("shared_w2", (2, 512, D))
    id32_h = din("c_ident32", (128, 128))
    ma_h = din("c_ma", (9, 128), BF16)
    mb0_h = din("c_mb0", (9, 128), BF16)
    mb1_h = din("c_mb1", (9, 128), BF16)
    plo_h = din("c_plo", (128, 128), BF16)
    phi_h = din("c_phi", (128, 128), BF16)
    oh16_h = din("c_oh16", (8, 1024), BF16)

    y_h = nc.dram_tensor("y", [T, D], F32, kind="ExternalOutput").ap()
    dbg = {}
    if debug:
        for name, shape, dt_ in [
            ("d_xres1", (T, D), F32), ("d_xres2", (T, D), F32),
            ("d_xlnfm", (128, 8 * 2048), BF16), ("d_gb16", (8, 2048), BF16),
            ("d_q16", (128, 4096), BF16), ("d_k16", (128, 4096), BF16),
            ("d_acc", (65, 2048), F32), ("d_v16", (128, 2048), BF16),
        ]:
            dbg[name] = nc.dram_tensor(name, list(shape), dt_, kind="ExternalOutput").ap()

    A = {}

    def sb(name, shape, dtype):
        A[name] = nc.alloc_sbuf_tensor("sb_" + name, list(shape), dtype)
        return A[name]

    sb("x_res", (128, NCH * 1024), F32)
    sb("xln_fm", (128, 8 * 2048), BF16)
    sb("xln_tmp0", (128, 1024), F32); sb("xln_tmp1", (128, 1024), F32)
    sb("ident32", (128, 128), F32)
    sb("ma", (9, 128), BF16); sb("mb0", (9, 128), BF16); sb("mb1", (9, 128), BF16)
    sb("plo", (128, 128), BF16); sb("phi", (128, 128), BF16)
    sb("oh16", (8, 1024), BF16)
    sb("router_w16", (128, 64), BF16)
    sb("rb_bc", (128, 8), F32)
    sb("bq_fm", (128, 8), F32); sb("bk_fm", (128, 8), F32); sb("bq_s", (128, 8), F32)
    sb("bv_bc", (128, 1024), F32)
    sb("bnst", (128, 24), F32)
    sb("mv", (128, 8), F32)
    sb("rch", (128, 16), F32)
    sb("cells", (128, 8), F32)

    ARENA = 76800
    arena_t = nc.alloc_sbuf_tensor("arena", [128, ARENA], U8)
    arena_base = nc.lookup_mloc(arena_t).addr

    def place(name, shape, dtype, off):
        nbytes = int(np.prod(shape[1:])) * (2 if dtype == BF16 else 4)
        assert off % 32 == 0 and off + nbytes <= ARENA, (name, off, nbytes)
        A[name] = nc.alloc_sbuf_tensor_at(name, list(shape), dtype, offset=arena_base + off)
        return A[name]

    place("w13_0", (128, 8 * 1024), BF16, 0)
    place("w13_1", (128, 8 * 1024), BF16, 16384)
    place("w2_0", (128, 4 * 1024), BF16, 32768)
    place("w2_1", (128, 4 * 1024), BF16, 40960)
    place("h_sb0", (128, 4 * 256), BF16, 49152)
    place("h_sb1", (128, 4 * 256), BF16, 51200)
    place("s_tmp0", (128, 256), F32, 53248)
    place("s_tmp1", (128, 256), F32, 54272)
    place("h0_0", (128, 256), F32, 55296)
    place("h0_1", (128, 256), F32, 56320)
    place("logits", (128, 128), F32, 57344)
    place("gscr", (128, 64), F32, 57856)
    place("gb16", (8, 2048), BF16, 58112)
    place("wq0", (128, 1024), BF16, 0)
    place("wq1", (128, 1024), BF16, 2048)
    place("wk0", (128, 1024), BF16, 4096)
    place("wk1", (128, 1024), BF16, 6144)
    place("wv0", (128, 1024), BF16, 8192)
    place("wv1", (128, 1024), BF16, 10240)
    place("v16", (128, 2048), BF16, 12288)
    place("qp32a", (128, 2048), F32, 16384)
    place("qp32b", (128, 1024), F32, 24576)
    place("q16", (128, 4096), BF16, 28672)
    place("k16", (128, 4096), BF16, 36864)
    place("acc_a0", (65, 2048), F32, 45056)
    place("acc_b0", (65, 1024), F32, 53248)
    place("acc_a1", (65, 2048), F32, 64448)
    place("acc_b1", (65, 1024), F32, 72640)
    place("vlv0", (128, 16 * SLOTW), BF16, 57344)
    place("vlv1", (128, 16 * SLOTW), BF16, 59456)
    place("vstage0", (128, 80), BF16, 61568)
    place("vstage1", (128, 80), BF16, 61728)
    place("A_sb0", (128, 128), F32, 61888)
    place("A_sb1", (128, 128), F32, 62400)
    place("AT_sb0", (128, 128), BF16, 62912)
    place("AT_sb1", (128, 128), BF16, 63168)
    place("vstage2", (128, 80), BF16, 63424)
    place("A_sb2", (128, 128), F32, 63616)
    place("AT_sb2", (128, 128), BF16, 64128)

    P = [nc.alloc_psum_tensor(f"P{i}", [128, 512], F32) for i in range(8)]

    p.dma("sync", A["x_res"][:].rearrange("p (c d) -> p c d", c=NCH),
          x_h.rearrange("(c p) d -> p c d", p=128))
    for nm, h in [("ident32", id32_h), ("ma", ma_h), ("mb0", mb0_h), ("mb1", mb1_h),
                  ("plo", plo_h), ("phi", phi_h), ("oh16", oh16_h),
                  ("rb_bc", rb_h), ("bv_bc", bv_h)]:
        p.dma("sync", A[nm][:], h)
    p.dma("sync", A["bq_fm"][:], bq_h)
    p.dma("sync", A["bk_fm"][:], bk_h)
    consts_tok = ("dS", p.cnt["dS"])
    x_tok = bq_tok = bk_tok = consts_tok
    rtr_tok = p.dma("gpsimd", A["router_w16"][:].rearrange("p (dc e) -> p dc e", dc=8),
                    rw_h.rearrange("(dc p) e -> p dc e", p=128), sem="dG")
    p.op("vector", lambda v: v.memset(A["cells"][:, 0:1], 1e-5))
    bqs_tok = p.op("vector", lambda v: v.tensor_scalar_mul(A["bq_s"][:], A["bq_fm"][:], 0.125),
                   waits=[bq_tok], inc="v")

    wtoks = {}
    wfree = {}

    def load_expert(e, extra_waits):
        b = e % 2
        s13 = "dA" if b == 0 else "dB"
        s2 = "dC" if b == 0 else "dD"
        if e < 8:
            w1, w3, w2 = rw1_h[e], rw3_h[e], rw2_h[e]
        else:
            w1, w3, w2 = sw1_h[e - 8], sw3_h[e - 8], sw2_h[e - 8]
        p.dma("gpsimd",
              A[f"w13_{b}"][:].rearrange("p (kc j) -> p kc j", kc=8)[:, :, 0:512],
              w1.rearrange("(kc p) j -> p kc j", p=128),
              waits=list(extra_waits) + wfree.get(f"w13_{b}", []), sem=s13)
        t2 = p.dma("gpsimd",
                   A[f"w13_{b}"][:].rearrange("p (kc j) -> p kc j", kc=8)[:, :, 512:1024],
                   w3.rearrange("(kc p) j -> p kc j", p=128), sem=s13)
        t3 = p.dma("gpsimd",
                   A[f"w2_{b}"][:].rearrange("p (jc d) -> p jc d", jc=4),
                   w2.rearrange("(jc p) d -> p jc d", p=128),
                   waits=list(extra_waits) + wfree.get(f"w2_{b}", []), sem=s2)
        wtoks[f"w13_{b}"] = t2
        wtoks[f"w2_{b}"] = t3
        wfree[f"w13_{b}"] = []
        wfree[f"w2_{b}"] = []

    guards = {
        "xln_tp": {},
        "ln_nb": {},
        "ln_sc": {},
        "gates_tp": {},
        "stmp": {},
        "hbuf": {},
    }

    def emit_ln(xres_toks):
        xln_ready = [None] * NCH
        for c0 in range(0, NCH, 2):
            agg_toks = {}
            for c in (c0, c0 + 1):
                pp = c % 2
                xs = A["x_res"][:, c * 1024:(c + 1) * 1024]
                w = [xres_toks[c]] if xres_toks else [x_tok]
                p.op("vector", lambda v, xs=xs, pp=pp: v.bn_stats(
                    A["bnst"][:, pp * 12:pp * 12 + 6], xs[:, 0:512]), waits=w)
                p.op("vector", lambda v, xs=xs, pp=pp: v.bn_stats(
                    A["bnst"][:, pp * 12 + 6:pp * 12 + 12], xs[:, 512:1024]), inc="v")
            for c in (c0, c0 + 1):
                pp = c % 2
                agg_toks[c] = p.op("vector", lambda v, pp=pp: v.bn_aggr(
                    A["mv"][:, pp * 2:pp * 2 + 2], A["bnst"][:, pp * 12:pp * 12 + 12]),
                    waits=[guards["ln_sc"].get(pp), ("v", p.cnt["v"])], inc="v")
            ln_toks = {}
            for c in (c0, c0 + 1):
                pp = c % 2
                ln_toks[c] = p.op("scalar", lambda s, pp=pp: s.activation(
                    A["rch"][:, pp * 4:pp * 4 + 1], A["mv"][:, pp * 2 + 1:pp * 2 + 2],
                    ACT.Ln, bias=A["cells"][:, 0:1], scale=1.0),
                    waits=[agg_toks[c], guards["ln_nb"].get(pp)], inc="s")
                guards["ln_sc"][pp] = ln_toks[c]
            r_toks = {}
            for c in (c0, c0 + 1):
                pp = c % 2
                r_toks[c] = p.op("scalar", lambda s, pp=pp: s.activation(
                    A["rch"][:, pp * 4 + 1:pp * 4 + 2], A["rch"][:, pp * 4:pp * 4 + 1],
                    ACT.Exp, bias=0.0, scale=-0.5), waits=[ln_toks[c]], inc="s")
            nb_toks = {}
            for c in (c0, c0 + 1):
                pp = c % 2
                nb_toks[c] = p.op("vector", lambda v, pp=pp: v.scalar_tensor_tensor(
                    A["rch"][:, pp * 4 + 2:pp * 4 + 3], A["mv"][:, pp * 2:pp * 2 + 1], -1.0,
                    A["rch"][:, pp * 4 + 1:pp * 4 + 2], ALU.mult, ALU.mult),
                    waits=[r_toks[c], agg_toks[c]], inc="v")
                guards["ln_nb"][pp] = nb_toks[c]
            for c in (c0, c0 + 1):
                pp = c % 2
                xs = A["x_res"][:, c * 1024:(c + 1) * 1024]
                a_tok = p.op("scalar", lambda s, xs=xs, pp=pp: s.activation(
                    A[f"xln_tmp{pp}"][:], xs, ACT.Identity,
                    bias=A["rch"][:, pp * 4 + 2:pp * 4 + 3],
                    scale=A["rch"][:, pp * 4 + 1:pp * 4 + 2]),
                    waits=[nb_toks[c], guards["xln_tp"].get(pp)], inc="s")
                ev_s = ev_v = None
                tp = None
                for dcc in range(8):
                    bi = dcc % 2
                    bank = P[bi]
                    tp = p.op("tensor", lambda t, pp=pp, dcc=dcc, bank=bank: t.transpose(
                        bank[:, 0:128], A[f"xln_tmp{pp}"][:, dcc * 128:(dcc + 1) * 128],
                        A["ident32"][:]),
                        waits=[a_tok, consts_tok] + p.bank_w(bi), inc="t")
                    dst = A["xln_fm"][:, dcc * 2048 + c * 128: dcc * 2048 + (c + 1) * 128]
                    if dcc % 4 < 2:
                        ev_s = p.op("scalar", lambda s, dst=dst, bank=bank: s.copy(
                            dst, bank[:, 0:128]), waits=[tp], inc="s")
                        p.bank_r(bi, ev_s)
                    else:
                        ev_v = p.op("vector", lambda v, dst=dst, bank=bank: v.tensor_copy(
                            dst, bank[:, 0:128]), waits=[tp], inc="v")
                        p.bank_r(bi, ev_v)
                guards["xln_tp"][pp] = tp
                xln_ready[c] = [ev_s, ev_v]
        return xln_ready

    def emit_gates(xln_ready):
        gb_ready = [None] * NCH
        lg_toks = [None] * NCH
        for c in range(NCH):
            bi = 2 + c % 2
            bank = P[bi]
            for dc in range(8):
                waits = []
                if dc == 0:
                    waits = list(xln_ready[c]) + [rtr_tok] + p.bank_w(bi)
                p.op("tensor", lambda t, dc=dc, c=c, bank=bank: t.matmul(
                    bank[:, 0:8],
                    A["xln_fm"][:, dc * 2048 + c * 128: dc * 2048 + (c + 1) * 128],
                    A["router_w16"][:, dc * 8:(dc + 1) * 8],
                    start=(dc == 0), stop=(dc == 7)),
                    waits=waits, inc=("t" if dc == 7 else None))
            mm = ("t", p.cnt["t"])
            lg = A["logits"][:, c * 8:(c + 1) * 8]
            ev = p.op("vector", lambda v, lg=lg, bank=bank: v.tensor_add(
                lg, bank[:, 0:8], A["rb_bc"][:]), waits=[mm, consts_tok], inc="v")
            p.bank_r(bi, ev)
            lg_toks[c] = ev
        g = A["gscr"]
        for c0 in range(0, NCH, 2):
            steps = {c0: [], c0 + 1: []}
            for c in (c0, c0 + 1):
                o = (c % 2) * 32
                lg = A["logits"][:, c * 8:(c + 1) * 8]
                st = steps[c]
                gtp = guards["gates_tp"].get(c % 2)
                st.append(("vector", lambda v, lg=lg, o=o: v.reduce_max(g[:, o:o + 1], lg, AX), [gtp, lg_toks[c]]))
                st.append(("vector", lambda v, lg=lg, o=o: v.reduce_max(g[:, o + 1:o + 2], lg, AX, negate=True), []))
                st.append(("vector", lambda v, lg=lg, o=o: v.tensor_single_scalar(g[:, o + 8:o + 16], lg, g[:, o:o + 1], ALU.is_ge), []))
                st.append(("vector", lambda v, lg=lg, o=o: v.scalar_tensor_tensor(g[:, o + 16:o + 24], g[:, o + 8:o + 16], NEG, lg, ALU.mult, ALU.add), []))
                st.append(("vector", lambda v, o=o: v.reduce_max(g[:, o + 2:o + 3], g[:, o + 16:o + 24], AX), []))
                st.append(("scalar", lambda s, lg=lg, o=o: s.activation(g[:, o + 8:o + 16], lg, ACT.Exp, bias=g[:, o + 1:o + 2], scale=1.0), ["v"]))
                st.append(("scalar", lambda s, o=o: s.activation(g[:, o + 3:o + 4], g[:, o + 2:o + 3], ACT.Exp, bias=g[:, o + 1:o + 2], scale=1.0), ["v"]))
                st.append(("vector", lambda v, lg=lg, o=o: v.tensor_single_scalar(g[:, o + 16:o + 24], lg, g[:, o + 2:o + 3], ALU.is_ge), ["s"]))
                st.append(("vector", lambda v, o=o: v.tensor_scalar_add(g[:, o + 4:o + 5], g[:, o + 3:o + 4], 1.0), ["s"]))
                st.append(("vector", lambda v, o=o: v.reciprocal(g[:, o + 5:o + 6], g[:, o + 4:o + 5]), []))
                st.append(("vector", lambda v, o=o: v.tensor_scalar_mul(g[:, o + 6:o + 7], g[:, o + 5:o + 6], 0.5), []))
                st.append(("vector", lambda v, o=o: v.scalar_tensor_tensor(g[:, o + 24:o + 32], g[:, o + 8:o + 16], g[:, o + 6:o + 7], g[:, o + 16:o + 24], ALU.mult, ALU.mult), ["s"]))
            for i in range(len(steps[c0])):
                for c in (c0, c0 + 1):
                    eng, fn, dep = steps[c][i]
                    w = []
                    if eng == "vector":
                        w.append(("v", p.cnt["v"]))
                    for d in dep:
                        if d == "v":
                            w.append(("v", p.cnt["v"]))
                        elif d == "s":
                            w.append(("s", p.cnt["s"]))
                        elif d is not None:
                            w.append(d)
                    p.op(eng, fn, waits=w, inc=("v" if eng == "vector" else "s"))
            for c in (c0, c0 + 1):
                bi = c % 2
                bank = P[bi]
                o = (c % 2) * 32
                last_v = ("v", p.cnt["v"])
                tp = p.op("tensor", lambda t, o=o, bank=bank: t.transpose(
                    bank[0:8, 0:128], g[:, o + 24:o + 32], A["ident32"][:]),
                    waits=[last_v] + p.bank_w(bi), inc="t")
                guards["gates_tp"][c % 2] = tp
                ev = p.op("vector", lambda v, c=c, bank=bank: v.tensor_copy(
                    A["gb16"][0:8, c * 128:(c + 1) * 128], bank[0:8, 0:128]),
                    waits=[tp], inc="v")
                p.bank_r(bi, ev)
                gb_ready[c] = ev
        return gb_ready

    def emit_experts(xln_ready, gb_ready, xres_in_toks, emit_out, ne=NE):
        xres_out = [None] * NCH
        xadd_tok = {}
        grot = [0]

        def p13_phase(e, tt):
            routed = e < 8
            w13 = A[f"w13_{e % 2}"]
            hbuf = (e * NT + tt) % 2
            h = A[f"h_sb{hbuf}"]
            h_war = guards["hbuf"].get(hbuf)
            gbc_tok = None
            if routed:
                waits = [gb_ready[2 * tt], gb_ready[2 * tt + 1], consts_tok] + p.bank_w(3)
                gbc_tok = p.op("tensor", lambda t, e=e, tt=tt: t.matmul(
                    P[3][:, 0:256], A["oh16"][:, e * 128:(e + 1) * 128],
                    A["gb16"][0:8, tt * 256:(tt + 1) * 256], start=True, stop=True),
                    waits=waits, inc="t")
            htt_last = None
            for jc in range(4):
                rot = grot[0] % 3
                grot[0] += 1
                bank = P[rot]
                for half in range(2):
                    for kc in range(8):
                        waits = []
                        if half == 0 and kc == 0:
                            waits = p.bank_w(rot) + [
                                wtoks[f"w13_{e % 2}"],
                                xln_ready[2 * tt][0], xln_ready[2 * tt][-1],
                                xln_ready[2 * tt + 1][0], xln_ready[2 * tt + 1][-1]]
                        co = 256 * half
                        p.op("tensor", lambda t, jc=jc, kc=kc, tt=tt, bank=bank, co=co, half=half, w13=w13: t.matmul(
                            bank[:, co:co + 256],
                            w13[:, kc * 1024 + half * 512 + jc * 128: kc * 1024 + half * 512 + (jc + 1) * 128],
                            A["xln_fm"][:, kc * 2048 + tt * 256: kc * 2048 + (tt + 1) * 256],
                            start=(kc == 0), stop=(kc == 7)),
                            waits=waits, inc=("t" if (half == 1 and kc == 7) else None))
                p13_tok = ("t", p.cnt["t"])
                st = A[f"s_tmp{jc % 2}"]
                h0 = A[f"h0_{jc % 2}"]
                silu = p.op("scalar", lambda s, bank=bank, st=st: s.activation(
                    st[:], bank[:, 0:256], ACT.Silu, bias=0.0, scale=1.0),
                    waits=[p13_tok, guards["stmp"].get(jc % 2)], inc="s")
                if routed:
                    m1 = p.op("vector", lambda v, bank=bank, st=st, h0=h0: v.scalar_tensor_tensor(
                        h0[:], bank[:, 256:512], 1.0, st[:], ALU.mult, ALU.mult),
                        waits=[silu, gbc_tok] + ([h_war] if jc == 0 else []), inc="v")
                    m2 = p.op("vector", lambda v, jc=jc, h=h, h0=h0: v.tensor_mul(
                        h[:, jc * 256:(jc + 1) * 256], h0[:], P[3][:, 0:256]),
                        waits=[m1], inc="v")
                    htt_last = m2
                    p.bank_readers[rot] = [silu, m1]
                    p.bank_r(3, m2)
                else:
                    m1 = p.op("vector", lambda v, bank=bank, st=st, jc=jc, h=h: v.scalar_tensor_tensor(
                        h[:, jc * 256:(jc + 1) * 256], bank[:, 256:512], 0.5, st[:],
                        ALU.mult, ALU.mult),
                        waits=[silu] + ([h_war] if jc == 0 else []), inc="v")
                    htt_last = m1
                    p.bank_readers[rot] = [silu, m1]
                guards["stmp"][jc % 2] = m1
            return {"e": e, "tt": tt, "hbuf": hbuf, "h_ready": htt_last,
                    "p13_tok": ("t", p.cnt["t"])}

        def w2_phase(st):
            e, tt, hbuf = st["e"], st["tt"], st["hbuf"]
            h = A[f"h_sb{hbuf}"]
            w2 = A[f"w2_{e % 2}"]
            last_w2 = None
            for tc in range(2):
                for dhh in range(2):
                    ob = 4 + 2 * tc + dhh
                    for jc in range(4):
                        waits = []
                        if jc == 0:
                            waits = [st["h_ready"], wtoks[f"w2_{e % 2}"]] + p.bank_w(ob)
                        p.op("tensor", lambda t, jc=jc, tc=tc, dhh=dhh, ob=ob, h=h, w2=w2: t.matmul(
                            P[ob][:],
                            h[:, jc * 256 + tc * 128: jc * 256 + (tc + 1) * 128],
                            w2[:, jc * 1024 + dhh * 512: jc * 1024 + (dhh + 1) * 512],
                            start=(jc == 0), stop=(jc == 3)),
                            waits=waits, inc=("t" if jc == 3 else None))
                    last_w2 = ("t", p.cnt["t"])
                    c = 2 * tt + tc
                    xs = A["x_res"][:, c * 1024 + dhh * 512: c * 1024 + (dhh + 1) * 512]
                    aw = [last_w2, xadd_tok.get((c, dhh))]
                    if e == 0 and xres_in_toks:
                        aw.append(xres_in_toks[c])
                    add = p.op("vector", lambda v, xs=xs, ob=ob: v.tensor_add(
                        xs, xs, P[ob][:]), waits=aw, inc="v")
                    xadd_tok[(c, dhh)] = add
                    p.bank_r(ob, add)
                    if e == ne - 1:
                        xres_out[c] = add
            guards["hbuf"][hbuf] = last_w2
            if e == ne - 1 and emit_out:
                c0 = 2 * tt
                p.dma("sync",
                      y_h.rearrange("(c p) d -> p c d", p=128)[:, c0:c0 + 2, :],
                      A["x_res"][:, c0 * 1024:(c0 + 2) * 1024].rearrange("p (c d) -> p c d", c=2),
                      waits=[xres_out[c0], xres_out[c0 + 1]])
            return last_w2

        pending = None
        exp_last_p13 = {}
        for e in range(ne):
            for tt in range(NT):
                st = p13_phase(e, tt)
                exp_last_p13[e] = st["p13_tok"]
                if pending is not None:
                    last_w2 = w2_phase(pending)
                    if pending["tt"] == NT - 1:
                        ed = pending["e"]
                        wfree[f"w13_{ed % 2}"] = [exp_last_p13[ed]]
                        wfree[f"w2_{ed % 2}"] = [last_w2]
                        if ed + 2 < ne:
                            load_expert(ed + 2, [])
                pending = st
        last_w2 = w2_phase(pending)
        wfree[f"w13_{(ne - 1) % 2}"] = [exp_last_p13[ne - 1]]
        wfree[f"w2_{(ne - 1) % 2}"] = [last_w2]
        return xres_out

    def emit_attention(xln_ready, xres_in_toks):
        xres_out = list(xres_in_toks)
        ffn_drain = p.fence()
        ones_tok = None
        if attn_setup >= 1:
            for nm in ("vstage0", "vstage1", "vstage2"):
                p.op("vector", lambda v, nm=nm: v.memset(A[nm][:, 64:65], 1.0), waits=ffn_drain)
            for nm in ("vlv0", "vlv1"):
                ones_tok = p.op("vector", lambda v, nm=nm: v.memset(
                    A[nm][:].rearrange("p (s c) -> p s c", c=SLOTW)[:, :, 64:65], 1.0),
                    waits=ffn_drain, inc="v")

        wpair_free = {}

        def load_pair(dc):
            b = dc % 2
            spair = "dE" if b == 0 else "dF"
            tok = None
            for nm, wh in (("wq", wq_h), ("wk", wk_h), ("wv", wv_h)):
                key = f"{nm}{b}"
                tok = p.dma(
                    "gpsimd", A[key][:].rearrange("p (kc m) -> p kc m", kc=8),
                    wh.rearrange("(kc p) n -> p kc n", p=128)[:, :, dc * 128:(dc + 1) * 128],
                    waits=wpair_free.get(key, list(ffn_drain)), sem=spair)
            for nm in ("wq", "wk", "wv"):
                wtoks[f"{nm}{b}"] = tok

        if attn_setup >= 2:
            load_pair(0)
            load_pair(1)

        pair_war_t = None
        acc_war_t = None
        prev_carry = {}
        for dc in range(attn_pairs):
            b = dc % 2
            v16_toks = []
            for c in range(NCH):
                bi = 4 + c % 2
                bank = P[bi]
                for kc in range(8):
                    waits = []
                    if kc == 0:
                        waits = [wtoks[f"wv{b}"], xln_ready[c][0], xln_ready[c][-1]] + p.bank_w(bi)
                    p.op("tensor", lambda t, kc=kc, c=c, bank=bank, b=b: t.matmul(
                        bank[:, 0:128],
                        A["xln_fm"][:, kc * 2048 + c * 128: kc * 2048 + (c + 1) * 128],
                        A[f"wv{b}"][:, kc * 128:(kc + 1) * 128],
                        start=(kc == 0), stop=(kc == 7)),
                        waits=waits, inc=("t" if kc == 7 else None))
                mm = ("t", p.cnt["t"])
                ev = p.op("vector", lambda v, c=c, bank=bank, dc=dc: v.tensor_add(
                    A["v16"][:, c * 128:(c + 1) * 128], bank[:, 0:128],
                    A["bv_bc"][:, dc * 128:(dc + 1) * 128]),
                    waits=[mm, consts_tok] + ([pair_war_t] if c == 0 else []), inc="v")
                p.bank_r(bi, ev)
                v16_toks.append(ev)

            def proj_pool(wkey, bias_ap, scale, dst16, pre_mm, pre_ev):
                wap = A[wkey]
                ev_toks = []
                for tile in range(4):
                    bi = tile % 4
                    bank = P[bi]
                    for kc in range(8):
                        waits = []
                        if kc == 0:
                            waits = [wtoks[wkey]] + p.bank_w(bi) + pre_mm + \
                                    [xln_ready[NCH - 1][0], xln_ready[NCH - 1][-1]]
                        p.op("tensor", lambda t, kc=kc, tile=tile, bank=bank, wap=wap: t.matmul(
                            bank[:],
                            wap[:, kc * 128:(kc + 1) * 128],
                            A["xln_fm"][:, kc * 2048 + tile * 512: kc * 2048 + (tile + 1) * 512],
                            start=(kc == 0), stop=(kc == 7)),
                            waits=waits, inc=("t" if kc == 7 else None))
                    mm = ("t", p.cnt["t"])
                    ev = p.op("scalar", lambda s, tile=tile, bank=bank, bias_ap=bias_ap, scale=scale: s.activation(
                        A["qp32a"][:, tile * 512:(tile + 1) * 512], bank[:], ACT.Identity,
                        bias=bias_ap, scale=scale),
                        waits=[mm] + (pre_ev if tile == 0 else []), inc="s")
                    ev_toks.append(ev)
                    p.bank_r(bi, ev)
                cast_last = p.op("scalar", lambda s, dst16=dst16: s.activation(
                    dst16[:, 0:2048], A["qp32a"][:, 0:2048], ACT.Copy, bias=0.0, scale=1.0),
                    waits=[ev_toks[-1]], inc="s")
                prev_pool = None
                for lv in range(1, 6):
                    n = LV_N[lv]
                    sbp, offp = LV_P32[lv - 1]
                    sbc, offc = LV_P32[lv]
                    srcp = A["qp32a"] if sbp == "a" else A["qp32b"]
                    srcc = A["qp32a"] if sbc == "a" else A["qp32b"]
                    np_ = LV_N[lv - 1]
                    pw = [prev_pool, cast_last] if prev_pool else [ev_toks[-1], cast_last]
                    prev_pool = p.op("vector", lambda v, a=srcp, bb=srcc, op_=offp, oc=offc, n=n, np_=np_: v.tensor_add(
                        bb[:, oc:oc + n], a[:, op_:op_ + np_:2], a[:, op_ + 1:op_ + np_:2]),
                        waits=pw, inc="v")
                    cast_last = p.op("scalar", lambda s, dst16=dst16, lv=lv, bb=srcc, oc=offc, n=n: s.activation(
                        dst16[:, LV_OFF[lv]:LV_OFF[lv] + n], bb[:, oc:oc + n],
                        ACT.Copy, bias=0.0, scale=0.5 ** lv), waits=[prev_pool], inc="s")
                return cast_last, prev_pool

            q_cast, q_pool = proj_pool(f"wq{b}", A["bq_s"][:, dc:dc + 1], 0.125, A["q16"],
                                       [bqs_tok], [pair_war_t] if pair_war_t else [])
            k_cast, k_pool = proj_pool(f"wk{b}", A["bk_fm"][:, dc:dc + 1], 1.0, A["k16"],
                                       [bk_tok], [q_cast, q_pool])
            qk_done = k_cast

            vlv_toks = {0: {}, 1: {}}
            first_vlv = [True]
            for lv in range(1, 6):
                nout = max(1, LV_N[lv] // 128)
                for si in range(nout):
                    bi = 6 + si % 2
                    bank = P[bi]
                    slot = LV_SLOT0[lv] + si
                    if lv == 1:
                        w0 = [v16_toks[2 * si], v16_toks[2 * si + 1], ones_tok, consts_tok] + p.bank_w(bi)
                        p.op("tensor", lambda t, si=si, bank=bank: t.matmul(
                            bank[:, 0:128], A["plo"][:],
                            A["v16"][:, (2 * si) * 128:(2 * si + 1) * 128],
                            start=True, stop=False), waits=w0)
                        p.op("tensor", lambda t, si=si, bank=bank: t.matmul(
                            bank[:, 0:128], A["phi"][:],
                            A["v16"][:, (2 * si + 1) * 128:(2 * si + 2) * 128],
                            start=False, stop=True), inc="t")
                        mm = ("t", p.cnt["t"])
                        for hh in range(2):
                            ev = p.op("vector", lambda v, hh=hh, slot=slot, bank=bank: v.tensor_copy(
                                A[f"vlv{hh}"][:, slot * SLOTW: slot * SLOTW + 64],
                                bank[:, hh * 64:hh * 64 + 64]), waits=[mm], inc="v")
                            vlv_toks[hh][(lv, si)] = ev
                            p.bank_r(bi, ev)
                        first_vlv[0] = False
                    elif lv < 5:
                        for hh in range(2):
                            bi = 6 + (2 * si + hh) % 2
                            bank = P[bi]
                            sp = LV_SLOT0[lv - 1] + 2 * si
                            w0 = [vlv_toks[hh][(lv - 1, 2 * si)], vlv_toks[hh][(lv - 1, 2 * si + 1)]] + p.bank_w(bi)
                            p.op("tensor", lambda t, hh=hh, sp=sp, bank=bank: t.matmul(
                                bank[:, hh * 64:hh * 64 + 64], A["plo"][:],
                                A[f"vlv{hh}"][:, sp * SLOTW: sp * SLOTW + 64],
                                start=True, stop=False), waits=w0)
                            p.op("tensor", lambda t, hh=hh, sp=sp, bank=bank: t.matmul(
                                bank[:, hh * 64:hh * 64 + 64], A["phi"][:],
                                A[f"vlv{hh}"][:, (sp + 1) * SLOTW: (sp + 1) * SLOTW + 64],
                                start=False, stop=True), inc="t")
                            mm = ("t", p.cnt["t"])
                            ev = p.op("vector", lambda v, hh=hh, slot=slot, bank=bank: v.tensor_copy(
                                A[f"vlv{hh}"][:, slot * SLOTW: slot * SLOTW + 64],
                                bank[:, hh * 64:hh * 64 + 64]), waits=[mm], inc="v")
                            vlv_toks[hh][(lv, si)] = ev
                            p.bank_r(bi, ev)
                    else:
                        for hh in range(2):
                            bi = 6 + hh % 2
                            bank = P[bi]
                            sp = LV_SLOT0[4]
                            w0 = [vlv_toks[hh][(4, 0)]] + p.bank_w(bi)
                            p.op("tensor", lambda t, hh=hh, sp=sp, bank=bank: t.matmul(
                                bank[0:64, hh * 64:hh * 64 + 64], A["plo"][:, 0:64],
                                A[f"vlv{hh}"][:, sp * SLOTW: sp * SLOTW + 64],
                                start=True, stop=True), waits=w0, inc="t")
                            mm = ("t", p.cnt["t"])
                            ev = p.op("vector", lambda v, hh=hh, slot=slot, bank=bank: v.tensor_copy(
                                A[f"vlv{hh}"][0:64, slot * SLOTW: slot * SLOTW + 64],
                                bank[0:64, hh * 64:hh * 64 + 64]), waits=[mm], inc="v")
                            vlv_toks[hh][(lv, 0)] = ev
                            p.bank_r(bi, ev)

            chunks = [(hh, lv, ci)
                      for lv in range(5, -1, -1)
                      for ci in range(LV_NC[lv])
                      for hh in range(attn_heads)]
            if attn_chunks is not None:
                chunks = chunks[:attn_chunks]
            nchk = len(chunks)
            toks = {}
            SB3 = [0, 1, 6]
            AB3 = [2, 3, 7]

            def tprev(name, k):
                if k >= 0:
                    return toks.get((name, k))
                return prev_carry.get((name, k % 3))

            def stage_S(k):
                hh, lv, ci = chunks[k]
                cw = LV_CW[lv]
                rows = slice(64 * hh, 64 * hh + 64)
                bi = SB3[k % 3]
                bank = P[bi]
                mb = A["mb0"] if lv == 0 else A["mb1"]
                qap = A["q16"][rows, LV_OFF[lv] + ci * 128: LV_OFF[lv] + ci * 128 + cw]
                kap = A["k16"][rows, LV_OFF[lv] + ci * 128: LV_OFF[lv] + ci * 128 + cw]
                w = [qk_done, consts_tok] + p.bank_w(bi)
                p.op("tensor", lambda t, qap=qap, kap=kap, cw=cw, bank=bank: t.matmul(
                    bank[0:cw, 0:cw], qap, kap, start=True, stop=False), waits=w)
                p.op("tensor", lambda t, cw=cw, bank=bank, mb=mb: t.matmul(
                    bank[0:cw, 0:cw], A["ma"][:, 0:cw], mb[:, 0:cw],
                    start=False, stop=True), inc="t")
                toks[("S", k)] = ("t", p.cnt["t"])

            def stage_rmax(k):
                hh, lv, ci = chunks[k]
                cw = LV_CW[lv]
                pp = k % 3
                bi = SB3[pp]
                bank = P[bi]
                w = [toks[("S", k)], tprev("exp", k - 3)]
                tok = p.op("vector", lambda v, cw=cw, pp=pp, bank=bank: v.reduce_max(
                    A["rch"][0:cw, 8 + pp:9 + pp], bank[0:cw, 0:cw], AX, negate=True),
                    waits=w, inc="v")
                toks[("rmax", k)] = tok
                p.bank_r(bi, tok)

            def stage_exp(k):
                hh, lv, ci = chunks[k]
                cw = LV_CW[lv]
                pp = k % 3
                bi = SB3[pp]
                bank = P[bi]
                w = [toks[("rmax", k)], tprev("tpA", k - 3)]
                tok = p.op("scalar", lambda s, cw=cw, pp=pp, bank=bank: s.activation(
                    A[f"A_sb{pp}"][0:cw, 0:cw], bank[0:cw, 0:cw], ACT.Exp,
                    bias=A["rch"][0:cw, 8 + pp:9 + pp], scale=1.0), waits=w, inc="s")
                toks[("exp", k)] = tok
                p.bank_r(bi, tok)

            def stage_tpA(k):
                hh, lv, ci = chunks[k]
                cw = LV_CW[lv]
                pp = k % 3
                bi = AB3[pp]
                bank = P[bi]
                w = [toks[("exp", k)]] + p.bank_w(bi)
                toks[("tpA", k)] = p.op("tensor", lambda t, cw=cw, pp=pp, bank=bank: t.transpose(
                    bank[0:cw, 0:cw], A[f"A_sb{pp}"][0:cw, 0:cw],
                    A["ident32"][0:cw, 0:cw]), waits=w, inc="t")

            def stage_atev(k):
                hh, lv, ci = chunks[k]
                cw = LV_CW[lv]
                pp = k % 3
                bi = AB3[pp]
                bank = P[bi]
                w = [toks[("tpA", k)], tprev("y", k - 3)]
                tok = p.op("scalar", lambda s, cw=cw, pp=pp, bank=bank: s.copy(
                    A[f"AT_sb{pp}"][0:cw, 0:cw], bank[0:cw, 0:cw]), waits=w, inc="s")
                toks[("atev", k)] = tok
                p.bank_r(bi, tok)

            def stage_vst(k):
                hh, lv, ci = chunks[k]
                if lv != 0:
                    return
                pp = k % 3
                w = [v16_toks[ci], ones_tok, tprev("y", k - 3)]
                src_ = A["v16"][:, ci * 128 + hh * 64: ci * 128 + hh * 64 + 64]
                toks[("vst", k)] = p.op("vector", lambda v, pp=pp, src_=src_: v.tensor_copy(
                    A[f"vstage{pp}"][:, 0:64], src_), waits=w, inc="v")

            def stage_y(k):
                hh, lv, ci = chunks[k]
                cw = LV_CW[lv]
                pp3 = k % 3
                bi = 4 + k % 2
                bank = P[bi]
                if lv == 0:
                    lhs = A[f"vstage{pp3}"][:, 0:65]
                    lw = [toks[("vst", k)]]
                else:
                    slot = LV_SLOT0[lv] + ci
                    lhs = A[f"vlv{hh}"][0:cw, slot * SLOTW: slot * SLOTW + 65]
                    lw = [vlv_toks[hh][(lv, ci)], ones_tok]
                w = [toks[("atev", k)]] + lw + p.bank_w(bi)
                p.op("tensor", lambda t, lhs=lhs, cw=cw, pp3=pp3, bank=bank: t.matmul(
                    bank[0:65, 0:cw], lhs, A[f"AT_sb{pp3}"][0:cw, 0:cw],
                    start=True, stop=True), waits=w, inc="t")
                toks[("y", k)] = ("t", p.cnt["t"])

            comb_last = {}

            def stage_comb(k):
                hh, lv, ci = chunks[k]
                bi = 4 + k % 2
                bank = P[bi]
                acc_cur = A[f"acc_a{hh}"] if LV_ACC[lv] == "a" else A[f"acc_b{hh}"]
                acc_prev = A[f"acc_b{hh}"] if LV_ACC[lv] == "a" else A[f"acc_a{hh}"]
                w = [toks[("y", k)], comb_last.get(hh)]
                if k in (2, 3) and acc_war_t is not None:
                    w.append(acc_war_t)
                if lv == 5:
                    tok = p.op("vector", lambda v, bank=bank, hh=hh: v.tensor_copy(
                        A[f"acc_b{hh}"][0:65, 0:64], bank[0:65, 0:64]), waits=w, inc="v")
                else:
                    s0 = ci * 128
                    tok = p.op("vector", lambda v, acc_cur=acc_cur, acc_prev=acc_prev, s0=s0, bank=bank: v.tensor_add(
                        acc_cur[0:65, s0:s0 + 128].rearrange("p (a b) -> p a b", b=2),
                        bank[0:65, 0:128].rearrange("p (a b) -> p a b", b=2),
                        acc_prev[0:65, s0 // 2: s0 // 2 + 64].unsqueeze(2).broadcast_to([65, 64, 2])),
                        waits=w, inc="v")
                toks[("comb", k)] = tok
                comb_last[hh] = tok
                p.bank_r(bi, tok)

            for i in range(nchk + 3):
                if 3 <= i <= nchk + 2:
                    stage_y(i - 3)
                    stage_comb(i - 3)
                if i < nchk:
                    stage_S(i)
                    stage_vst(i)
                if 1 <= i <= nchk:
                    stage_rmax(i - 1)
                    stage_exp(i - 1)
                if 2 <= i <= nchk + 1:
                    stage_tpA(i - 2)
                    stage_atev(i - 2)
            for nm_ in ("exp", "tpA", "y"):
                for pp in range(3):
                    kk = nchk - 1 - ((nchk - 1 - pp) % 3)
                    prev_carry[(nm_, pp)] = toks.get((nm_, kk))

            last_tp = None
            for hh in range(attn_heads):
                headcol0 = (2 * dc + hh) * 64
                for c0 in range(0, NCH, 2):
                    tps = {}
                    for ci in (c0, c0 + 1):
                        pp = ci % 2
                        bi = 6 + pp
                        bank = P[bi]
                        w = [comb_last.get(hh)] + p.bank_w(bi)
                        tps[ci] = p.op("tensor", lambda t, ci=ci, bank=bank, hh=hh: t.transpose(
                            bank[:, 0:65], A[f"acc_a{hh}"][0:65, ci * 128:(ci + 1) * 128],
                            A["ident32"][0:65, 0:65]), waits=w, inc="t")
                        last_tp = tps[ci]
                    d_toks = {}
                    for ci in (c0, c0 + 1):
                        pp = ci % 2
                        bank = P[6 + pp]
                        d_toks[ci] = p.op("vector", lambda v, bank=bank, pp=pp: v.tensor_scalar_add(
                            A["rch"][:, 11 + pp:12 + pp], bank[:, 64:65], 1e-8),
                            waits=[tps[ci], ("v", p.cnt["v"])], inc="v")
                        p.bank_r(6 + pp, d_toks[ci])
                    r_toks2 = {}
                    for ci in (c0, c0 + 1):
                        pp = ci % 2
                        r_toks2[ci] = p.op("vector", lambda v, pp=pp: v.reciprocal(
                            A["rch"][:, 13 + pp:14 + pp], A["rch"][:, 11 + pp:12 + pp]),
                            waits=[d_toks[ci]], inc="v")
                    for ci in (c0, c0 + 1):
                        pp = ci % 2
                        bank = P[6 + pp]
                        xs = A["x_res"][:, ci * 1024 + headcol0: ci * 1024 + headcol0 + 64]
                        stk = p.op("vector", lambda v, xs=xs, bank=bank, pp=pp: v.scalar_tensor_tensor(
                            xs, bank[:, 0:64], A["rch"][:, 13 + pp:14 + pp], xs,
                            ALU.mult, ALU.add), waits=[r_toks2[ci], xres_out[ci]], inc="v")
                        p.bank_r(6 + pp, stk)
                        xres_out[ci] = stk
            acc_war_t = last_tp
            pair_war_t = ("t", p.cnt["t"])
            for nm in ("wq", "wk", "wv"):
                wpair_free[f"{nm}{b}"] = [pair_war_t]
            if dc + 2 < NPAIR:
                load_pair(dc + 2)
        return xres_out

    xres_toks = None
    if "ffn1" in stages:
        load_expert(0, [])
        load_expert(1, [])
        xln = emit_ln(None)
        gb = emit_gates(xln)
        xres_toks = emit_experts(xln, gb, None, emit_out=False)
        if debug:
            p.dma("sync", dbg["d_xlnfm"], A["xln_fm"][:],
                  waits=[xln[NCH - 1][0], xln[NCH - 1][-1]])
            p.dma("sync", dbg["d_gb16"], A["gb16"][:], waits=[gb[NCH - 1]])
            dump_tok = p.dma("sync", dbg["d_xres1"].rearrange("(c p) d -> p c d", p=128),
                             A["x_res"][:].rearrange("p (c d) -> p c d", c=NCH),
                             waits=xres_toks)
            xres_toks = [dump_tok] * NCH
    if "attn" in stages:
        if attn_pairs == -2:
            xln2 = emit_ln(xres_toks)
        elif attn_pairs >= 0:
            xln2 = emit_ln(xres_toks)
            xres_toks = emit_attention(xln2, xres_toks if xres_toks else [x_tok] * NCH)
        if debug:
            fl = p.fence(("vector", "scalar"))
            p.dma("sync", dbg["d_q16"], A["q16"][:], waits=fl)
            p.dma("sync", dbg["d_k16"], A["k16"][:])
            p.dma("sync", dbg["d_acc"], A["acc_a0"][:])
            p.dma("sync", dbg["d_v16"], A["v16"][:])
            dump_tok = p.dma("sync", dbg["d_xres2"].rearrange("(c p) d -> p c d", p=128),
                             A["x_res"][:].rearrange("p (c d) -> p c d", c=NCH),
                             waits=xres_toks)
            xres_toks = [dump_tok] * NCH
    if "ffn2" in stages:
        attn_drain = p.fence()
        load_expert(0, attn_drain)
        load_expert(1, attn_drain)
        xln3 = emit_ln(xres_toks)
        gb3 = emit_gates(xln3)
        xres_toks = emit_experts(xln3, gb3, xres_toks, emit_out=ffn2_out, ne=ffn2_ne)
        if not ffn2_out:
            fl = p.fence(("vector",))
            p.dma("sync", y_h.rearrange("(c p) d -> p c d", p=128),
                  A["x_res"][:].rearrange("p (c d) -> p c d", c=NCH),
                  waits=[tk for tk in xres_toks if tk] + fl)
    else:
        fl = p.fence(("vector",))
        p.dma("sync", y_h.rearrange("(c p) d -> p c d", p=128),
              A["x_res"][:].rearrange("p (c d) -> p c d", c=NCH),
              waits=(xres_toks if xres_toks else [x_tok]) + fl)

    p.op("sync", lambda e: e.nop(),
         waits=[(s_, p.cnt[s_]) for s_ in DMASEMS if p.cnt[s_] > 0])

    import contextlib
    with contextlib.ExitStack() as stck:
        sems = {s: stck.enter_context(nc.semaphore(s)) for s in SEMS}
        p.emit(sems)
    return nc, dbg, p


_CACHE = {}


def get_program(stages=("ffn1", "attn", "ffn2"), debug=False, **kw):
    key = (tuple(stages), debug, tuple(sorted(kw.items())))
    if key not in _CACHE:
        nc1, dbg1, p1 = build_program(stages, debug, **kw)
        amts = [16 * p1.dma_splits[i] for i in range(p1.dma_idx)]
        if all(a == 16 for a in amts):
            _CACHE[key] = (nc1, dbg1)
        else:
            nc2, dbg2, p2 = build_program(stages, debug, dma_amts=amts, **kw)
            assert p2.dma_idx == p1.dma_idx
            _CACHE[key] = (nc2, dbg2)
    return _CACHE[key]


def make_inmaps(inputs):
    consts = host_consts()
    base = {
        "wq": inputs["L0_attn_Wq"],
        "bq_fm_in": np.ascontiguousarray(np.asarray(inputs["L0_attn_bq"]).reshape(8, 128).T),
        "wk": inputs["L0_attn_Wk"],
        "bk_fm_in": np.ascontiguousarray(np.asarray(inputs["L0_attn_bk"]).reshape(8, 128).T),
        "wv": inputs["L0_attn_Wv"],
        "bv_bc": np.ascontiguousarray(np.broadcast_to(inputs["L0_attn_bv"], (128, D))),
        "router_w": inputs["L0_ffn_router_w"],
        "rb_bc": np.ascontiguousarray(np.broadcast_to(inputs["L0_ffn_router_b"], (128, 8))),
        "routed_w1": inputs["L0_ffn_routed_w1"],
        "routed_w3": inputs["L0_ffn_routed_w3"],
        "routed_w2": inputs["L0_ffn_routed_w2"],
        "shared_w1": inputs["L0_ffn_shared_w1"],
        "shared_w3": inputs["L0_ffn_shared_w3"],
        "shared_w2": inputs["L0_ffn_shared_w2"],
    }
    base.update(consts)
    x = inputs["L0_attn_hidden_states"]
    return [dict(base, x=np.ascontiguousarray(x[c])) for c in range(8)]


def kernel(**inputs):
    nc, _ = get_program()
    maps = make_inmaps(inputs)
    res = run_bass_kernel_spmd(nc, maps, core_ids=list(range(8)))
    return np.stack([np.asarray(res.results[c]["y"], dtype=np.float32) for c in range(8)], axis=0)


# version 05
# speedup vs baseline: 1.1258x; 1.0114x over previous
import numpy as np
import ml_dtypes
import concourse.bass as bass
import concourse.mybir as mybir
from concourse.bass_utils import run_bass_kernel_spmd

F32 = mybir.dt.float32
BF16 = mybir.dt.bfloat16
U8 = mybir.dt.uint8
ALU = mybir.AluOpType
ACT = mybir.ActivationFunctionType
AX = mybir.AxisListType.X
bfdt = ml_dtypes.bfloat16

T, D = 2048, 1024
NCH = 16
NPAIR = 8
NE = 10
NT = 8
NEG = -28672.0

LV_N = [2048, 1024, 512, 256, 128, 64]
LV_OFF = [0, 2048, 3072, 3584, 3840, 3968]
LV_NC = [16, 8, 4, 2, 1, 1]
LV_CW = [128, 128, 128, 128, 128, 64]
LV_P32 = [("a", 0), ("b", 0), ("a", 0), ("b", 0), ("a", 512), ("b", 256)]
LV_ACC = ["a", "b", "a", "b", "a", "b"]
LV_SLOT0 = {1: 0, 2: 8, 3: 12, 4: 14, 5: 15}
SLOTW = 66

ENGS = ("sync", "tensor", "vector", "scalar", "gpsimd")
SEMS = ("t", "v", "s", "g", "dS", "dG", "dA", "dB", "dC", "dD", "dE", "dF", "b")
DMASEMS = ("dS", "dG", "dA", "dB", "dC", "dD", "dE", "dF")
BAR = 1024


class Prog:
    def __init__(self, nc, dma_amts=None):
        self.nc = nc
        self.items = {e: [] for e in ENGS}
        self.cnt = {s: 0 for s in SEMS}
        self.lastw = {e: {} for e in ENGS}
        self.bank_readers = {i: [] for i in range(8)}
        self.dma_amts = dma_amts
        self.dma_idx = 0
        self.dma_splits = {}
        self.sems = None
        self.bar_tok = None

    def op(self, eng, fn, waits=(), inc=None, amt=1, dma_id=None):
        ws = []
        if self.bar_tok is not None and eng != "sync":
            waits = [self.bar_tok] + list(waits)
        for w in waits:
            if w is None:
                continue
            s, v = w
            if v <= 0:
                continue
            assert v <= self.cnt[s], f"forward-token violation: {s}={v} > {self.cnt[s]}"
            if self.lastw[eng].get(s, 0) < v:
                ws.append((s, v))
                self.lastw[eng][s] = v
        tok = None
        if inc is not None:
            self.cnt[inc] += amt
            tok = (inc, self.cnt[inc])
        self.items[eng].append([ws, fn, inc, amt, tok, dma_id])
        return tok

    def dma(self, q, out, in_, waits=(), sem=None):
        if sem is None:
            sem = "dS" if q == "sync" else "dG"
        idx = self.dma_idx
        self.dma_idx += 1
        amt = 16 if self.dma_amts is None else self.dma_amts[idx]
        return self.op(q, lambda e, o=out, i=in_: e.dma_start(o, i), waits=waits,
                       inc=sem, amt=amt, dma_id=idx)

    def bank_w(self, i):
        w = self.bank_readers[i]
        self.bank_readers[i] = []
        return w

    def bank_r(self, i, tok):
        self.bank_readers[i].append(tok)

    def fence(self, engs=("tensor", "vector", "scalar")):
        m = {"tensor": "t", "vector": "v", "scalar": "s", "gpsimd": "g"}
        toks = []
        for e in engs:
            sem = m[e]
            assert self.items[e], f"fence on empty engine {e}"
            it = self.items[e][-1]
            if it[2] is None:
                self.cnt[sem] += 1
                it[2] = sem
                it[4] = (sem, self.cnt[sem])
            toks.append(it[4])
        return toks

    def emit(self, sems):
        self.sems = sems
        with self.nc.Block() as block:
            for eng in ENGS:
                if not self.items[eng]:
                    continue

                def mk(eng):
                    def body(e):
                        for (ws, fn, inc, amt, _tok, dma_id) in self.items[eng]:
                            for (s, v) in ws:
                                e.wait_ge(sems[s], v)
                            n0 = self.nc.n_instructions()
                            inst = fn(e)
                            if dma_id is not None:
                                self.dma_splits[dma_id] = self.nc.n_instructions() - n0
                            if inc is not None:
                                inst.then_inc(sems[inc], 16 if dma_id is not None else amt)
                    return body

                getattr(block, eng)(mk(eng))


def host_consts():
    i = np.arange(128)
    blk = i // 16
    ident32 = np.eye(128, dtype=np.float32)
    ma = np.zeros((9, 128), np.float32)
    mb0 = np.zeros((9, 128), np.float32)
    mb1 = np.zeros((9, 128), np.float32)
    ma[0, :] = NEG
    mb0[0, :] = 1.0
    mb1[0, :] = 1.0
    for b in range(8):
        ma[1 + b, blk == b] = -NEG
        mb0[1 + b, blk == b] = 1.0
        mb1[1 + b, blk == (b ^ 1)] = 1.0
    plo = np.zeros((128, 128), np.float32)
    phi = np.zeros((128, 128), np.float32)
    for m in range(64):
        plo[2 * m, m] = 1.0
        plo[2 * m + 1, m] = 1.0
        phi[2 * m, 64 + m] = 1.0
        phi[2 * m + 1, 64 + m] = 1.0
    oh = np.zeros((8, 8 * 128), np.float32)
    for e in range(8):
        oh[e, e * 128:(e + 1) * 128] = 1.0
    return {
        "c_ident32": ident32,
        "c_ma": ma.astype(bfdt), "c_mb0": mb0.astype(bfdt), "c_mb1": mb1.astype(bfdt),
        "c_plo": plo.astype(bfdt), "c_phi": phi.astype(bfdt),
        "c_oh16": oh.astype(bfdt),
    }


def build_program(stages=("ffn1", "attn", "ffn2"), debug=False,
                  attn_pairs=NPAIR, attn_heads=2, attn_chunks=None, attn_setup=3,
                  ffn2_ne=NE, ffn2_out=True, dma_amts=None):
    nc = bass.Bass("TRN2", target_bir_lowering=False)
    p = Prog(nc, dma_amts=dma_amts)

    for s_ in ("t", "v", "s", "g") + DMASEMS:
        p.op("sync", lambda e, s_=s_: e.sem_clear(p.sems[s_]))
    p.op("sync", lambda e: e.sem_inc(p.sems["b"], BAR))
    p.cnt["b"] = BAR
    p.bar_tok = ("b", BAR)

    def din(name, shape, dtype=F32):
        return nc.dram_tensor(name, list(shape), dtype, kind="ExternalInput").ap()

    x_h = din("x", (T, D))
    wq_h = din("wq", (D, D)); bq_h = din("bq_fm_in", (128, 8))
    wk_h = din("wk", (D, D)); bk_h = din("bk_fm_in", (128, 8))
    wv_h = din("wv", (D, D)); bv_h = din("bv_bc", (128, D))
    rw_h = din("router_w", (D, 8)); rb_h = din("rb_bc", (128, 8))
    rw1_h = din("routed_w1", (8, D, 512))
    rw3_h = din("routed_w3", (8, D, 512))
    rw2_h = din("routed_w2", (8, 512, D))
    sw1_h = din("shared_w1", (2, D, 512))
    sw3_h = din("shared_w3", (2, D, 512))
    sw2_h = din("shared_w2", (2, 512, D))
    id32_h = din("c_ident32", (128, 128))
    ma_h = din("c_ma", (9, 128), BF16)
    mb0_h = din("c_mb0", (9, 128), BF16)
    mb1_h = din("c_mb1", (9, 128), BF16)
    plo_h = din("c_plo", (128, 128), BF16)
    phi_h = din("c_phi", (128, 128), BF16)
    oh16_h = din("c_oh16", (8, 1024), BF16)

    y_h = nc.dram_tensor("y", [T, D], F32, kind="ExternalOutput").ap()
    dbg = {}
    if debug:
        for name, shape, dt_ in [
            ("d_xres1", (T, D), F32), ("d_xres2", (T, D), F32),
            ("d_xlnfm", (128, 8 * 2048), BF16), ("d_gb16", (8, 2048), BF16),
            ("d_q16", (128, 4096), BF16), ("d_k16", (128, 4096), BF16),
            ("d_acc", (65, 2048), F32), ("d_v16", (128, 2048), BF16),
        ]:
            dbg[name] = nc.dram_tensor(name, list(shape), dt_, kind="ExternalOutput").ap()

    A = {}

    def sb(name, shape, dtype):
        A[name] = nc.alloc_sbuf_tensor("sb_" + name, list(shape), dtype)
        return A[name]

    sb("x_res", (128, NCH * 1024), F32)
    sb("xln_fm", (128, 8 * 2048), BF16)
    sb("xln_tmp0", (128, 1024), F32); sb("xln_tmp1", (128, 1024), F32)
    sb("ident32", (128, 128), F32)
    sb("ma", (9, 128), BF16); sb("mb0", (9, 128), BF16); sb("mb1", (9, 128), BF16)
    sb("plo", (128, 128), BF16); sb("phi", (128, 128), BF16)
    sb("oh16", (8, 1024), BF16)
    sb("router_w16", (128, 64), BF16)
    sb("rb_bc", (128, 8), F32)
    sb("bq_fm", (128, 8), F32); sb("bk_fm", (128, 8), F32); sb("bq_s", (128, 8), F32)
    sb("bv_bc", (128, 1024), F32)
    sb("bnst", (128, 24), F32)
    sb("mv", (128, 8), F32)
    sb("rch", (128, 16), F32)
    sb("cells", (128, 8), F32)

    ARENA = 80896
    arena_t = nc.alloc_sbuf_tensor("arena", [128, ARENA], U8)
    arena_base = nc.lookup_mloc(arena_t).addr

    def place(name, shape, dtype, off):
        nbytes = int(np.prod(shape[1:])) * (2 if dtype == BF16 else 4)
        assert off % 32 == 0 and off + nbytes <= ARENA, (name, off, nbytes)
        A[name] = nc.alloc_sbuf_tensor_at(name, list(shape), dtype, offset=arena_base + off)
        return A[name]

    place("w13_0", (128, 8 * 1024), BF16, 0)
    place("w13_1", (128, 8 * 1024), BF16, 16384)
    place("w2_0", (128, 4 * 1024), BF16, 32768)
    place("w2_1", (128, 4 * 1024), BF16, 40960)
    place("h_sb0", (128, 4 * 256), BF16, 49152)
    place("h_sb1", (128, 4 * 256), BF16, 51200)
    place("s_tmp0", (128, 256), F32, 53248)
    place("s_tmp1", (128, 256), F32, 54272)
    place("h0_0", (128, 256), F32, 55296)
    place("h0_1", (128, 256), F32, 56320)
    place("logits", (128, 128), F32, 57344)
    place("gscr", (128, 64), F32, 57856)
    place("gb16", (8, 2048), BF16, 58112)
    place("wq0", (128, 1024), BF16, 0)
    place("wq1", (128, 1024), BF16, 2048)
    place("wk0", (128, 1024), BF16, 4096)
    place("wk1", (128, 1024), BF16, 6144)
    place("wv0", (128, 1024), BF16, 8192)
    place("wv1", (128, 1024), BF16, 10240)
    place("v16_0", (128, 2048), BF16, 12288)
    place("qp32a", (128, 2048), F32, 16384)
    place("qp32b", (128, 1024), F32, 24576)
    place("q16", (128, 4096), BF16, 28672)
    place("k16", (128, 4096), BF16, 36864)
    place("acc_a0", (65, 2048), F32, 45056)
    place("acc_b0", (65, 1024), F32, 53248)
    place("acc_a1", (65, 2048), F32, 64448)
    place("acc_b1", (65, 1024), F32, 72640)
    place("v16_1", (128, 2048), BF16, 76800)
    place("vlv0", (128, 16 * SLOTW), BF16, 57344)
    place("vlv1", (128, 16 * SLOTW), BF16, 59456)
    place("vstage0", (128, 80), BF16, 61568)
    place("vstage1", (128, 80), BF16, 61728)
    place("A_sb0", (128, 128), F32, 61888)
    place("A_sb1", (128, 128), F32, 62400)
    place("AT_sb0", (128, 128), BF16, 62912)
    place("AT_sb1", (128, 128), BF16, 63168)
    place("vstage2", (128, 80), BF16, 63424)
    place("A_sb2", (128, 128), F32, 63616)
    place("AT_sb2", (128, 128), BF16, 64128)

    P = [nc.alloc_psum_tensor(f"P{i}", [128, 512], F32) for i in range(8)]

    p.dma("sync", A["x_res"][:].rearrange("p (c d) -> p c d", c=NCH),
          x_h.rearrange("(c p) d -> p c d", p=128))
    for nm, h in [("ident32", id32_h), ("ma", ma_h), ("mb0", mb0_h), ("mb1", mb1_h),
                  ("plo", plo_h), ("phi", phi_h), ("oh16", oh16_h),
                  ("rb_bc", rb_h), ("bv_bc", bv_h)]:
        p.dma("sync", A[nm][:], h)
    p.dma("sync", A["bq_fm"][:], bq_h)
    p.dma("sync", A["bk_fm"][:], bk_h)
    consts_tok = ("dS", p.cnt["dS"])
    x_tok = bq_tok = bk_tok = consts_tok
    rtr_tok = p.dma("gpsimd", A["router_w16"][:].rearrange("p (dc e) -> p dc e", dc=8),
                    rw_h.rearrange("(dc p) e -> p dc e", p=128), sem="dG")
    p.op("vector", lambda v: v.memset(A["cells"][:, 0:1], 1e-5))
    bqs_tok = p.op("vector", lambda v: v.tensor_scalar_mul(A["bq_s"][:], A["bq_fm"][:], 0.125),
                   waits=[bq_tok], inc="v")

    wtoks = {}
    wfree = {}

    def load_expert(e, extra_waits):
        b = e % 2
        s13 = "dA" if b == 0 else "dB"
        s2 = "dC" if b == 0 else "dD"
        if e < 8:
            w1, w3, w2 = rw1_h[e], rw3_h[e], rw2_h[e]
        else:
            w1, w3, w2 = sw1_h[e - 8], sw3_h[e - 8], sw2_h[e - 8]
        p.dma("gpsimd",
              A[f"w13_{b}"][:].rearrange("p (kc j) -> p kc j", kc=8)[:, :, 0:512],
              w1.rearrange("(kc p) j -> p kc j", p=128),
              waits=list(extra_waits) + wfree.get(f"w13_{b}", []), sem=s13)
        t2 = p.dma("gpsimd",
                   A[f"w13_{b}"][:].rearrange("p (kc j) -> p kc j", kc=8)[:, :, 512:1024],
                   w3.rearrange("(kc p) j -> p kc j", p=128), sem=s13)
        t3 = p.dma("gpsimd",
                   A[f"w2_{b}"][:].rearrange("p (jc d) -> p jc d", jc=4),
                   w2.rearrange("(jc p) d -> p jc d", p=128),
                   waits=list(extra_waits) + wfree.get(f"w2_{b}", []), sem=s2)
        wtoks[f"w13_{b}"] = t2
        wtoks[f"w2_{b}"] = t3
        wfree[f"w13_{b}"] = []
        wfree[f"w2_{b}"] = []

    guards = {
        "xln_tp": {},
        "ln_nb": {},
        "ln_sc": {},
        "gates_tp": {},
        "stmp": {},
        "hbuf": {},
    }

    def emit_ln(xres_toks):
        xln_ready = [None] * NCH
        for c0 in range(0, NCH, 2):
            agg_toks = {}
            for c in (c0, c0 + 1):
                pp = c % 2
                xs = A["x_res"][:, c * 1024:(c + 1) * 1024]
                w = [xres_toks[c]] if xres_toks else [x_tok]
                p.op("vector", lambda v, xs=xs, pp=pp: v.bn_stats(
                    A["bnst"][:, pp * 12:pp * 12 + 6], xs[:, 0:512]), waits=w)
                p.op("vector", lambda v, xs=xs, pp=pp: v.bn_stats(
                    A["bnst"][:, pp * 12 + 6:pp * 12 + 12], xs[:, 512:1024]), inc="v")
            for c in (c0, c0 + 1):
                pp = c % 2
                agg_toks[c] = p.op("vector", lambda v, pp=pp: v.bn_aggr(
                    A["mv"][:, pp * 2:pp * 2 + 2], A["bnst"][:, pp * 12:pp * 12 + 12]),
                    waits=[guards["ln_sc"].get(pp), ("v", p.cnt["v"])], inc="v")
            ln_toks = {}
            for c in (c0, c0 + 1):
                pp = c % 2
                ln_toks[c] = p.op("scalar", lambda s, pp=pp: s.activation(
                    A["rch"][:, pp * 4:pp * 4 + 1], A["mv"][:, pp * 2 + 1:pp * 2 + 2],
                    ACT.Ln, bias=A["cells"][:, 0:1], scale=1.0),
                    waits=[agg_toks[c], guards["ln_nb"].get(pp)], inc="s")
                guards["ln_sc"][pp] = ln_toks[c]
            r_toks = {}
            for c in (c0, c0 + 1):
                pp = c % 2
                r_toks[c] = p.op("scalar", lambda s, pp=pp: s.activation(
                    A["rch"][:, pp * 4 + 1:pp * 4 + 2], A["rch"][:, pp * 4:pp * 4 + 1],
                    ACT.Exp, bias=0.0, scale=-0.5), waits=[ln_toks[c]], inc="s")
            nb_toks = {}
            for c in (c0, c0 + 1):
                pp = c % 2
                nb_toks[c] = p.op("vector", lambda v, pp=pp: v.scalar_tensor_tensor(
                    A["rch"][:, pp * 4 + 2:pp * 4 + 3], A["mv"][:, pp * 2:pp * 2 + 1], -1.0,
                    A["rch"][:, pp * 4 + 1:pp * 4 + 2], ALU.mult, ALU.mult),
                    waits=[r_toks[c], agg_toks[c]], inc="v")
                guards["ln_nb"][pp] = nb_toks[c]
            for c in (c0, c0 + 1):
                pp = c % 2
                xs = A["x_res"][:, c * 1024:(c + 1) * 1024]
                a_tok = p.op("scalar", lambda s, xs=xs, pp=pp: s.activation(
                    A[f"xln_tmp{pp}"][:], xs, ACT.Identity,
                    bias=A["rch"][:, pp * 4 + 2:pp * 4 + 3],
                    scale=A["rch"][:, pp * 4 + 1:pp * 4 + 2]),
                    waits=[nb_toks[c], guards["xln_tp"].get(pp)], inc="s")
                ev_s = ev_v = None
                tp = None
                for dcc in range(8):
                    bi = dcc % 2
                    bank = P[bi]
                    tp = p.op("tensor", lambda t, pp=pp, dcc=dcc, bank=bank: t.transpose(
                        bank[:, 0:128], A[f"xln_tmp{pp}"][:, dcc * 128:(dcc + 1) * 128],
                        A["ident32"][:]),
                        waits=[a_tok, consts_tok] + p.bank_w(bi), inc="t")
                    dst = A["xln_fm"][:, dcc * 2048 + c * 128: dcc * 2048 + (c + 1) * 128]
                    if dcc % 4 < 2:
                        ev_s = p.op("scalar", lambda s, dst=dst, bank=bank: s.copy(
                            dst, bank[:, 0:128]), waits=[tp], inc="s")
                        p.bank_r(bi, ev_s)
                    else:
                        ev_v = p.op("vector", lambda v, dst=dst, bank=bank: v.tensor_copy(
                            dst, bank[:, 0:128]), waits=[tp], inc="v")
                        p.bank_r(bi, ev_v)
                guards["xln_tp"][pp] = tp
                xln_ready[c] = [ev_s, ev_v]
        return xln_ready

    def emit_gates(xln_ready):
        gb_ready = [None] * NCH
        lg_toks = [None] * NCH
        for c in range(NCH):
            bi = 2 + c % 2
            bank = P[bi]
            for dc in range(8):
                waits = []
                if dc == 0:
                    waits = list(xln_ready[c]) + [rtr_tok] + p.bank_w(bi)
                p.op("tensor", lambda t, dc=dc, c=c, bank=bank: t.matmul(
                    bank[:, 0:8],
                    A["xln_fm"][:, dc * 2048 + c * 128: dc * 2048 + (c + 1) * 128],
                    A["router_w16"][:, dc * 8:(dc + 1) * 8],
                    start=(dc == 0), stop=(dc == 7)),
                    waits=waits, inc=("t" if dc == 7 else None))
            mm = ("t", p.cnt["t"])
            lg = A["logits"][:, c * 8:(c + 1) * 8]
            ev = p.op("vector", lambda v, lg=lg, bank=bank: v.tensor_add(
                lg, bank[:, 0:8], A["rb_bc"][:]), waits=[mm, consts_tok], inc="v")
            p.bank_r(bi, ev)
            lg_toks[c] = ev
        g = A["gscr"]
        for c0 in range(0, NCH, 2):
            steps = {c0: [], c0 + 1: []}
            for c in (c0, c0 + 1):
                o = (c % 2) * 32
                lg = A["logits"][:, c * 8:(c + 1) * 8]
                st = steps[c]
                gtp = guards["gates_tp"].get(c % 2)
                st.append(("vector", lambda v, lg=lg, o=o: v.reduce_max(g[:, o:o + 1], lg, AX), [gtp, lg_toks[c]]))
                st.append(("vector", lambda v, lg=lg, o=o: v.reduce_max(g[:, o + 1:o + 2], lg, AX, negate=True), []))
                st.append(("vector", lambda v, lg=lg, o=o: v.tensor_single_scalar(g[:, o + 8:o + 16], lg, g[:, o:o + 1], ALU.is_ge), []))
                st.append(("vector", lambda v, lg=lg, o=o: v.scalar_tensor_tensor(g[:, o + 16:o + 24], g[:, o + 8:o + 16], NEG, lg, ALU.mult, ALU.add), []))
                st.append(("vector", lambda v, o=o: v.reduce_max(g[:, o + 2:o + 3], g[:, o + 16:o + 24], AX), []))
                st.append(("scalar", lambda s, lg=lg, o=o: s.activation(g[:, o + 8:o + 16], lg, ACT.Exp, bias=g[:, o + 1:o + 2], scale=1.0), ["v"]))
                st.append(("scalar", lambda s, o=o: s.activation(g[:, o + 3:o + 4], g[:, o + 2:o + 3], ACT.Exp, bias=g[:, o + 1:o + 2], scale=1.0), ["v"]))
                st.append(("vector", lambda v, lg=lg, o=o: v.tensor_single_scalar(g[:, o + 16:o + 24], lg, g[:, o + 2:o + 3], ALU.is_ge), ["s"]))
                st.append(("vector", lambda v, o=o: v.tensor_scalar_add(g[:, o + 4:o + 5], g[:, o + 3:o + 4], 1.0), ["s"]))
                st.append(("vector", lambda v, o=o: v.reciprocal(g[:, o + 5:o + 6], g[:, o + 4:o + 5]), []))
                st.append(("vector", lambda v, o=o: v.tensor_scalar_mul(g[:, o + 6:o + 7], g[:, o + 5:o + 6], 0.5), []))
                st.append(("vector", lambda v, o=o: v.scalar_tensor_tensor(g[:, o + 24:o + 32], g[:, o + 8:o + 16], g[:, o + 6:o + 7], g[:, o + 16:o + 24], ALU.mult, ALU.mult), ["s"]))
            for i in range(len(steps[c0])):
                for c in (c0, c0 + 1):
                    eng, fn, dep = steps[c][i]
                    w = []
                    if eng == "vector":
                        w.append(("v", p.cnt["v"]))
                    for d in dep:
                        if d == "v":
                            w.append(("v", p.cnt["v"]))
                        elif d == "s":
                            w.append(("s", p.cnt["s"]))
                        elif d is not None:
                            w.append(d)
                    p.op(eng, fn, waits=w, inc=("v" if eng == "vector" else "s"))
            for c in (c0, c0 + 1):
                bi = c % 2
                bank = P[bi]
                o = (c % 2) * 32
                last_v = ("v", p.cnt["v"])
                tp = p.op("tensor", lambda t, o=o, bank=bank: t.transpose(
                    bank[0:8, 0:128], g[:, o + 24:o + 32], A["ident32"][:]),
                    waits=[last_v] + p.bank_w(bi), inc="t")
                guards["gates_tp"][c % 2] = tp
                ev = p.op("vector", lambda v, c=c, bank=bank: v.tensor_copy(
                    A["gb16"][0:8, c * 128:(c + 1) * 128], bank[0:8, 0:128]),
                    waits=[tp], inc="v")
                p.bank_r(bi, ev)
                gb_ready[c] = ev
        return gb_ready

    def emit_experts(xln_ready, gb_ready, xres_in_toks, emit_out, ne=NE):
        xres_out = [None] * NCH
        xadd_tok = {}
        grot = [0]

        def p13_phase(e, tt):
            routed = e < 8
            w13 = A[f"w13_{e % 2}"]
            hbuf = (e * NT + tt) % 2
            h = A[f"h_sb{hbuf}"]
            h_war = guards["hbuf"].get(hbuf)
            gbc_tok = None
            if routed:
                waits = [gb_ready[2 * tt], gb_ready[2 * tt + 1], consts_tok] + p.bank_w(3)
                gbc_tok = p.op("tensor", lambda t, e=e, tt=tt: t.matmul(
                    P[3][:, 0:256], A["oh16"][:, e * 128:(e + 1) * 128],
                    A["gb16"][0:8, tt * 256:(tt + 1) * 256], start=True, stop=True),
                    waits=waits, inc="t")
            htt_last = None
            for jc in range(4):
                rot = grot[0] % 3
                grot[0] += 1
                bank = P[rot]
                for half in range(2):
                    for kc in range(8):
                        waits = []
                        if half == 0 and kc == 0:
                            waits = p.bank_w(rot) + [
                                wtoks[f"w13_{e % 2}"],
                                xln_ready[2 * tt][0], xln_ready[2 * tt][-1],
                                xln_ready[2 * tt + 1][0], xln_ready[2 * tt + 1][-1]]
                        co = 256 * half
                        p.op("tensor", lambda t, jc=jc, kc=kc, tt=tt, bank=bank, co=co, half=half, w13=w13: t.matmul(
                            bank[:, co:co + 256],
                            w13[:, kc * 1024 + half * 512 + jc * 128: kc * 1024 + half * 512 + (jc + 1) * 128],
                            A["xln_fm"][:, kc * 2048 + tt * 256: kc * 2048 + (tt + 1) * 256],
                            start=(kc == 0), stop=(kc == 7)),
                            waits=waits, inc=("t" if (half == 1 and kc == 7) else None))
                p13_tok = ("t", p.cnt["t"])
                st = A[f"s_tmp{jc % 2}"]
                h0 = A[f"h0_{jc % 2}"]
                silu = p.op("scalar", lambda s, bank=bank, st=st: s.activation(
                    st[:], bank[:, 0:256], ACT.Silu, bias=0.0, scale=1.0),
                    waits=[p13_tok, guards["stmp"].get(jc % 2)], inc="s")
                if routed:
                    m1 = p.op("vector", lambda v, bank=bank, st=st, h0=h0: v.scalar_tensor_tensor(
                        h0[:], bank[:, 256:512], 1.0, st[:], ALU.mult, ALU.mult),
                        waits=[silu, gbc_tok] + ([h_war] if jc == 0 else []), inc="v")
                    m2 = p.op("vector", lambda v, jc=jc, h=h, h0=h0: v.tensor_mul(
                        h[:, jc * 256:(jc + 1) * 256], h0[:], P[3][:, 0:256]),
                        waits=[m1], inc="v")
                    htt_last = m2
                    p.bank_readers[rot] = [silu, m1]
                    p.bank_r(3, m2)
                else:
                    m1 = p.op("vector", lambda v, bank=bank, st=st, jc=jc, h=h: v.scalar_tensor_tensor(
                        h[:, jc * 256:(jc + 1) * 256], bank[:, 256:512], 0.5, st[:],
                        ALU.mult, ALU.mult),
                        waits=[silu] + ([h_war] if jc == 0 else []), inc="v")
                    htt_last = m1
                    p.bank_readers[rot] = [silu, m1]
                guards["stmp"][jc % 2] = m1
            return {"e": e, "tt": tt, "hbuf": hbuf, "h_ready": htt_last,
                    "p13_tok": ("t", p.cnt["t"])}

        def w2_phase(st):
            e, tt, hbuf = st["e"], st["tt"], st["hbuf"]
            h = A[f"h_sb{hbuf}"]
            w2 = A[f"w2_{e % 2}"]
            last_w2 = None
            for tc in range(2):
                for dhh in range(2):
                    ob = 4 + 2 * tc + dhh
                    for jc in range(4):
                        waits = []
                        if jc == 0:
                            waits = [st["h_ready"], wtoks[f"w2_{e % 2}"]] + p.bank_w(ob)
                        p.op("tensor", lambda t, jc=jc, tc=tc, dhh=dhh, ob=ob, h=h, w2=w2: t.matmul(
                            P[ob][:],
                            h[:, jc * 256 + tc * 128: jc * 256 + (tc + 1) * 128],
                            w2[:, jc * 1024 + dhh * 512: jc * 1024 + (dhh + 1) * 512],
                            start=(jc == 0), stop=(jc == 3)),
                            waits=waits, inc=("t" if jc == 3 else None))
                    last_w2 = ("t", p.cnt["t"])
                    c = 2 * tt + tc
                    xs = A["x_res"][:, c * 1024 + dhh * 512: c * 1024 + (dhh + 1) * 512]
                    aw = [last_w2, xadd_tok.get((c, dhh))]
                    if e == 0 and xres_in_toks:
                        aw.append(xres_in_toks[c])
                    add = p.op("vector", lambda v, xs=xs, ob=ob: v.tensor_add(
                        xs, xs, P[ob][:]), waits=aw, inc="v")
                    xadd_tok[(c, dhh)] = add
                    p.bank_r(ob, add)
                    if e == ne - 1:
                        xres_out[c] = add
            guards["hbuf"][hbuf] = last_w2
            if e == ne - 1 and emit_out:
                c0 = 2 * tt
                p.dma("sync",
                      y_h.rearrange("(c p) d -> p c d", p=128)[:, c0:c0 + 2, :],
                      A["x_res"][:, c0 * 1024:(c0 + 2) * 1024].rearrange("p (c d) -> p c d", c=2),
                      waits=[xres_out[c0], xres_out[c0 + 1]])
            return last_w2

        pending = None
        exp_last_p13 = {}
        for e in range(ne):
            for tt in range(NT):
                st = p13_phase(e, tt)
                exp_last_p13[e] = st["p13_tok"]
                if pending is not None:
                    last_w2 = w2_phase(pending)
                    if pending["tt"] == NT - 1:
                        ed = pending["e"]
                        wfree[f"w13_{ed % 2}"] = [exp_last_p13[ed]]
                        wfree[f"w2_{ed % 2}"] = [last_w2]
                        if ed + 2 < ne:
                            load_expert(ed + 2, [])
                pending = st
        last_w2 = w2_phase(pending)
        wfree[f"w13_{(ne - 1) % 2}"] = [exp_last_p13[ne - 1]]
        wfree[f"w2_{(ne - 1) % 2}"] = [last_w2]
        return xres_out

    def emit_attention(xln_ready, xres_in_toks):
        xres_out = list(xres_in_toks)
        ffn_drain = p.fence()
        ones_tok = None
        if attn_setup >= 1:
            for nm in ("vstage0", "vstage1", "vstage2"):
                p.op("vector", lambda v, nm=nm: v.memset(A[nm][:, 64:65], 1.0), waits=ffn_drain)
            for nm in ("vlv0", "vlv1"):
                ones_tok = p.op("vector", lambda v, nm=nm: v.memset(
                    A[nm][:].rearrange("p (s c) -> p s c", c=SLOTW)[:, :, 64:65], 1.0),
                    waits=ffn_drain, inc="v")

        wpair_free = {}

        def load_pair(dc):
            b = dc % 2
            spair = "dE" if b == 0 else "dF"
            tok = None
            for nm, wh in (("wq", wq_h), ("wk", wk_h), ("wv", wv_h)):
                key = f"{nm}{b}"
                tok = p.dma(
                    "gpsimd", A[key][:].rearrange("p (kc m) -> p kc m", kc=8),
                    wh.rearrange("(kc p) n -> p kc n", p=128)[:, :, dc * 128:(dc + 1) * 128],
                    waits=wpair_free.get(key, list(ffn_drain)), sem=spair)
            for nm in ("wq", "wk", "wv"):
                wtoks[f"{nm}{b}"] = tok

        if attn_setup >= 2:
            load_pair(0)
            load_pair(1)

        pair_war_t = None
        acc_war_t = None
        prev_carry = {}
        vdone = {}
        pair_end_t = {}

        def emit_vchunk(dcv, c):
            pb = dcv % 2
            bi = 4 + c % 2
            bank = P[bi]
            for kc in range(8):
                waits = []
                if kc == 0:
                    waits = [wtoks[f"wv{pb}"], xln_ready[c][0], xln_ready[c][-1]] + p.bank_w(bi)
                p.op("tensor", lambda t, kc=kc, c=c, bank=bank, pb=pb: t.matmul(
                    bank[:, 0:128],
                    A["xln_fm"][:, kc * 2048 + c * 128: kc * 2048 + (c + 1) * 128],
                    A[f"wv{pb}"][:, kc * 128:(kc + 1) * 128],
                    start=(kc == 0), stop=(kc == 7)),
                    waits=waits, inc=("t" if kc == 7 else None))
            mm = ("t", p.cnt["t"])
            w = [mm, consts_tok]
            if c == 0:
                w.append(pair_end_t.get(dcv - 2))
            ev = p.op("vector", lambda v, c=c, bank=bank, dcv=dcv, pb=pb: v.tensor_add(
                A[f"v16_{pb}"][:, c * 128:(c + 1) * 128], bank[:, 0:128],
                A["bv_bc"][:, dcv * 128:(dcv + 1) * 128]),
                waits=w, inc="v")
            p.bank_r(bi, ev)
            vdone.setdefault(dcv, []).append(ev)

        for dc in range(attn_pairs):
            b = dc % 2
            vcur = A[f"v16_{b}"]
            if dc == 0:
                for c in range(NCH):
                    emit_vchunk(0, c)
            v16_toks = vdone[dc]
            vinj = [(dc + 1, c) for c in range(NCH)] if dc + 1 < attn_pairs else []

            def proj_pool(wkey, bias_ap, scale, dst16, pre_mm, pre_ev):
                wap = A[wkey]
                ev_toks = []
                for tile in range(4):
                    bi = tile % 4
                    bank = P[bi]
                    for kc in range(8):
                        waits = []
                        if kc == 0:
                            waits = [wtoks[wkey]] + p.bank_w(bi) + pre_mm + \
                                    [xln_ready[NCH - 1][0], xln_ready[NCH - 1][-1]]
                        p.op("tensor", lambda t, kc=kc, tile=tile, bank=bank, wap=wap: t.matmul(
                            bank[:],
                            wap[:, kc * 128:(kc + 1) * 128],
                            A["xln_fm"][:, kc * 2048 + tile * 512: kc * 2048 + (tile + 1) * 512],
                            start=(kc == 0), stop=(kc == 7)),
                            waits=waits, inc=("t" if kc == 7 else None))
                    mm = ("t", p.cnt["t"])
                    ev = p.op("scalar", lambda s, tile=tile, bank=bank, bias_ap=bias_ap, scale=scale: s.activation(
                        A["qp32a"][:, tile * 512:(tile + 1) * 512], bank[:], ACT.Identity,
                        bias=bias_ap, scale=scale),
                        waits=[mm] + (pre_ev if tile == 0 else []), inc="s")
                    ev_toks.append(ev)
                    p.bank_r(bi, ev)
                cast_last = p.op("scalar", lambda s, dst16=dst16: s.activation(
                    dst16[:, 0:2048], A["qp32a"][:, 0:2048], ACT.Copy, bias=0.0, scale=1.0),
                    waits=[ev_toks[-1]], inc="s")
                prev_pool = None
                for lv in range(1, 6):
                    n = LV_N[lv]
                    sbp, offp = LV_P32[lv - 1]
                    sbc, offc = LV_P32[lv]
                    srcp = A["qp32a"] if sbp == "a" else A["qp32b"]
                    srcc = A["qp32a"] if sbc == "a" else A["qp32b"]
                    np_ = LV_N[lv - 1]
                    pw = [prev_pool, cast_last] if prev_pool else [ev_toks[-1], cast_last]
                    prev_pool = p.op("vector", lambda v, a=srcp, bb=srcc, op_=offp, oc=offc, n=n, np_=np_: v.tensor_add(
                        bb[:, oc:oc + n], a[:, op_:op_ + np_:2], a[:, op_ + 1:op_ + np_:2]),
                        waits=pw, inc="v")
                    cast_last = p.op("scalar", lambda s, dst16=dst16, lv=lv, bb=srcc, oc=offc, n=n: s.activation(
                        dst16[:, LV_OFF[lv]:LV_OFF[lv] + n], bb[:, oc:oc + n],
                        ACT.Copy, bias=0.0, scale=0.5 ** lv), waits=[prev_pool], inc="s")
                return cast_last, prev_pool

            q_cast, q_pool = proj_pool(f"wq{b}", A["bq_s"][:, dc:dc + 1], 0.125, A["q16"],
                                       [bqs_tok], [pair_war_t] if pair_war_t else [])
            k_cast, k_pool = proj_pool(f"wk{b}", A["bk_fm"][:, dc:dc + 1], 1.0, A["k16"],
                                       [bk_tok], [q_cast, q_pool])
            qk_done = k_cast

            vlv_toks = {0: {}, 1: {}}
            first_vlv = [True]
            for lv in range(1, 6):
                nout = max(1, LV_N[lv] // 128)
                for si in range(nout):
                    bi = 6 + si % 2
                    bank = P[bi]
                    slot = LV_SLOT0[lv] + si
                    if lv == 1:
                        w0 = [v16_toks[2 * si], v16_toks[2 * si + 1], ones_tok, consts_tok] + p.bank_w(bi)
                        p.op("tensor", lambda t, si=si, bank=bank, vcur=vcur: t.matmul(
                            bank[:, 0:128], A["plo"][:],
                            vcur[:, (2 * si) * 128:(2 * si + 1) * 128],
                            start=True, stop=False), waits=w0)
                        p.op("tensor", lambda t, si=si, bank=bank, vcur=vcur: t.matmul(
                            bank[:, 0:128], A["phi"][:],
                            vcur[:, (2 * si + 1) * 128:(2 * si + 2) * 128],
                            start=False, stop=True), inc="t")
                        mm = ("t", p.cnt["t"])
                        for hh in range(2):
                            ev = p.op("vector", lambda v, hh=hh, slot=slot, bank=bank: v.tensor_copy(
                                A[f"vlv{hh}"][:, slot * SLOTW: slot * SLOTW + 64],
                                bank[:, hh * 64:hh * 64 + 64]), waits=[mm], inc="v")
                            vlv_toks[hh][(lv, si)] = ev
                            p.bank_r(bi, ev)
                        first_vlv[0] = False
                    elif lv < 5:
                        for hh in range(2):
                            bi = 6 + (2 * si + hh) % 2
                            bank = P[bi]
                            sp = LV_SLOT0[lv - 1] + 2 * si
                            w0 = [vlv_toks[hh][(lv - 1, 2 * si)], vlv_toks[hh][(lv - 1, 2 * si + 1)]] + p.bank_w(bi)
                            p.op("tensor", lambda t, hh=hh, sp=sp, bank=bank: t.matmul(
                                bank[:, hh * 64:hh * 64 + 64], A["plo"][:],
                                A[f"vlv{hh}"][:, sp * SLOTW: sp * SLOTW + 64],
                                start=True, stop=False), waits=w0)
                            p.op("tensor", lambda t, hh=hh, sp=sp, bank=bank: t.matmul(
                                bank[:, hh * 64:hh * 64 + 64], A["phi"][:],
                                A[f"vlv{hh}"][:, (sp + 1) * SLOTW: (sp + 1) * SLOTW + 64],
                                start=False, stop=True), inc="t")
                            mm = ("t", p.cnt["t"])
                            ev = p.op("vector", lambda v, hh=hh, slot=slot, bank=bank: v.tensor_copy(
                                A[f"vlv{hh}"][:, slot * SLOTW: slot * SLOTW + 64],
                                bank[:, hh * 64:hh * 64 + 64]), waits=[mm], inc="v")
                            vlv_toks[hh][(lv, si)] = ev
                            p.bank_r(bi, ev)
                    else:
                        for hh in range(2):
                            bi = 6 + hh % 2
                            bank = P[bi]
                            sp = LV_SLOT0[4]
                            w0 = [vlv_toks[hh][(4, 0)]] + p.bank_w(bi)
                            p.op("tensor", lambda t, hh=hh, sp=sp, bank=bank: t.matmul(
                                bank[0:64, hh * 64:hh * 64 + 64], A["plo"][:, 0:64],
                                A[f"vlv{hh}"][:, sp * SLOTW: sp * SLOTW + 64],
                                start=True, stop=True), waits=w0, inc="t")
                            mm = ("t", p.cnt["t"])
                            ev = p.op("vector", lambda v, hh=hh, slot=slot, bank=bank: v.tensor_copy(
                                A[f"vlv{hh}"][0:64, slot * SLOTW: slot * SLOTW + 64],
                                bank[0:64, hh * 64:hh * 64 + 64]), waits=[mm], inc="v")
                            vlv_toks[hh][(lv, 0)] = ev
                            p.bank_r(bi, ev)

            chunks = [(hh, lv, ci)
                      for lv in range(5, -1, -1)
                      for ci in range(LV_NC[lv])
                      for hh in range(attn_heads)]
            if attn_chunks is not None:
                chunks = chunks[:attn_chunks]
            nchk = len(chunks)
            toks = {}
            SB3 = [0, 1, 6]
            AB3 = [2, 3, 7]

            def tprev(name, k):
                if k >= 0:
                    return toks.get((name, k))
                return prev_carry.get((name, k % 3))

            def stage_S(k):
                hh, lv, ci = chunks[k]
                cw = LV_CW[lv]
                rows = slice(64 * hh, 64 * hh + 64)
                bi = SB3[k % 3]
                bank = P[bi]
                mb = A["mb0"] if lv == 0 else A["mb1"]
                qap = A["q16"][rows, LV_OFF[lv] + ci * 128: LV_OFF[lv] + ci * 128 + cw]
                kap = A["k16"][rows, LV_OFF[lv] + ci * 128: LV_OFF[lv] + ci * 128 + cw]
                w = [qk_done, consts_tok] + p.bank_w(bi)
                p.op("tensor", lambda t, qap=qap, kap=kap, cw=cw, bank=bank: t.matmul(
                    bank[0:cw, 0:cw], qap, kap, start=True, stop=False), waits=w)
                p.op("tensor", lambda t, cw=cw, bank=bank, mb=mb: t.matmul(
                    bank[0:cw, 0:cw], A["ma"][:, 0:cw], mb[:, 0:cw],
                    start=False, stop=True), inc="t")
                toks[("S", k)] = ("t", p.cnt["t"])

            def stage_rmax(k):
                hh, lv, ci = chunks[k]
                cw = LV_CW[lv]
                pp = k % 3
                bi = SB3[pp]
                bank = P[bi]
                w = [toks[("S", k)], tprev("exp", k - 3)]
                tok = p.op("vector", lambda v, cw=cw, pp=pp, bank=bank: v.reduce_max(
                    A["rch"][0:cw, 8 + pp:9 + pp], bank[0:cw, 0:cw], AX, negate=True),
                    waits=w, inc="v")
                toks[("rmax", k)] = tok
                p.bank_r(bi, tok)

            def stage_exp(k):
                hh, lv, ci = chunks[k]
                cw = LV_CW[lv]
                pp = k % 3
                bi = SB3[pp]
                bank = P[bi]
                w = [toks[("rmax", k)], tprev("tpA", k - 3)]
                tok = p.op("scalar", lambda s, cw=cw, pp=pp, bank=bank: s.activation(
                    A[f"A_sb{pp}"][0:cw, 0:cw], bank[0:cw, 0:cw], ACT.Exp,
                    bias=A["rch"][0:cw, 8 + pp:9 + pp], scale=1.0), waits=w, inc="s")
                toks[("exp", k)] = tok
                p.bank_r(bi, tok)

            def stage_tpA(k):
                hh, lv, ci = chunks[k]
                cw = LV_CW[lv]
                pp = k % 3
                bi = AB3[pp]
                bank = P[bi]
                w = [toks[("exp", k)]] + p.bank_w(bi)
                toks[("tpA", k)] = p.op("tensor", lambda t, cw=cw, pp=pp, bank=bank: t.transpose(
                    bank[0:cw, 0:cw], A[f"A_sb{pp}"][0:cw, 0:cw],
                    A["ident32"][0:cw, 0:cw]), waits=w, inc="t")

            def stage_atev(k):
                hh, lv, ci = chunks[k]
                cw = LV_CW[lv]
                pp = k % 3
                bi = AB3[pp]
                bank = P[bi]
                w = [toks[("tpA", k)], tprev("y", k - 3)]
                tok = p.op("scalar", lambda s, cw=cw, pp=pp, bank=bank: s.copy(
                    A[f"AT_sb{pp}"][0:cw, 0:cw], bank[0:cw, 0:cw]), waits=w, inc="s")
                toks[("atev", k)] = tok
                p.bank_r(bi, tok)

            def stage_vst(k):
                hh, lv, ci = chunks[k]
                if lv != 0:
                    return
                pp = k % 3
                w = [v16_toks[ci], ones_tok, tprev("y", k - 3)]
                src_ = vcur[:, ci * 128 + hh * 64: ci * 128 + hh * 64 + 64]
                toks[("vst", k)] = p.op("vector", lambda v, pp=pp, src_=src_: v.tensor_copy(
                    A[f"vstage{pp}"][:, 0:64], src_), waits=w, inc="v")

            def stage_y(k):
                hh, lv, ci = chunks[k]
                cw = LV_CW[lv]
                pp3 = k % 3
                bi = 4 + k % 2
                bank = P[bi]
                if lv == 0:
                    lhs = A[f"vstage{pp3}"][:, 0:65]
                    lw = [toks[("vst", k)]]
                else:
                    slot = LV_SLOT0[lv] + ci
                    lhs = A[f"vlv{hh}"][0:cw, slot * SLOTW: slot * SLOTW + 65]
                    lw = [vlv_toks[hh][(lv, ci)], ones_tok]
                w = [toks[("atev", k)]] + lw + p.bank_w(bi)
                p.op("tensor", lambda t, lhs=lhs, cw=cw, pp3=pp3, bank=bank: t.matmul(
                    bank[0:65, 0:cw], lhs, A[f"AT_sb{pp3}"][0:cw, 0:cw],
                    start=True, stop=True), waits=w, inc="t")
                toks[("y", k)] = ("t", p.cnt["t"])

            comb_last = {}

            def stage_comb(k):
                hh, lv, ci = chunks[k]
                bi = 4 + k % 2
                bank = P[bi]
                acc_cur = A[f"acc_a{hh}"] if LV_ACC[lv] == "a" else A[f"acc_b{hh}"]
                acc_prev = A[f"acc_b{hh}"] if LV_ACC[lv] == "a" else A[f"acc_a{hh}"]
                w = [toks[("y", k)], comb_last.get(hh)]
                if k in (2, 3) and acc_war_t is not None:
                    w.append(acc_war_t)
                if lv == 5:
                    tok = p.op("vector", lambda v, bank=bank, hh=hh: v.tensor_copy(
                        A[f"acc_b{hh}"][0:65, 0:64], bank[0:65, 0:64]), waits=w, inc="v")
                else:
                    s0 = ci * 128
                    tok = p.op("vector", lambda v, acc_cur=acc_cur, acc_prev=acc_prev, s0=s0, bank=bank: v.tensor_add(
                        acc_cur[0:65, s0:s0 + 128].rearrange("p (a b) -> p a b", b=2),
                        bank[0:65, 0:128].rearrange("p (a b) -> p a b", b=2),
                        acc_prev[0:65, s0 // 2: s0 // 2 + 64].unsqueeze(2).broadcast_to([65, 64, 2])),
                        waits=w, inc="v")
                toks[("comb", k)] = tok
                comb_last[hh] = tok
                p.bank_r(bi, tok)

            for i in range(nchk + 3):
                if 3 <= i <= nchk + 2:
                    stage_y(i - 3)
                    stage_comb(i - 3)
                if i < nchk:
                    stage_S(i)
                    stage_vst(i)
                if 1 <= i <= nchk:
                    stage_rmax(i - 1)
                    stage_exp(i - 1)
                if 2 <= i <= nchk + 1:
                    stage_tpA(i - 2)
                    stage_atev(i - 2)
                if i % 4 == 2 and vinj:
                    emit_vchunk(*vinj.pop(0))
            while vinj:
                emit_vchunk(*vinj.pop(0))
            for nm_ in ("exp", "tpA", "y"):
                for pp in range(3):
                    kk = nchk - 1 - ((nchk - 1 - pp) % 3)
                    prev_carry[(nm_, pp)] = toks.get((nm_, kk))

            last_tp = None
            for hh in range(attn_heads):
                headcol0 = (2 * dc + hh) * 64
                for c0 in range(0, NCH, 2):
                    tps = {}
                    for ci in (c0, c0 + 1):
                        pp = ci % 2
                        bi = 6 + pp
                        bank = P[bi]
                        w = [comb_last.get(hh)] + p.bank_w(bi)
                        tps[ci] = p.op("tensor", lambda t, ci=ci, bank=bank, hh=hh: t.transpose(
                            bank[:, 0:65], A[f"acc_a{hh}"][0:65, ci * 128:(ci + 1) * 128],
                            A["ident32"][0:65, 0:65]), waits=w, inc="t")
                        last_tp = tps[ci]
                    d_toks = {}
                    for ci in (c0, c0 + 1):
                        pp = ci % 2
                        bank = P[6 + pp]
                        d_toks[ci] = p.op("vector", lambda v, bank=bank, pp=pp: v.tensor_scalar_add(
                            A["rch"][:, 11 + pp:12 + pp], bank[:, 64:65], 1e-8),
                            waits=[tps[ci], ("v", p.cnt["v"])], inc="v")
                        p.bank_r(6 + pp, d_toks[ci])
                    r_toks2 = {}
                    for ci in (c0, c0 + 1):
                        pp = ci % 2
                        r_toks2[ci] = p.op("vector", lambda v, pp=pp: v.reciprocal(
                            A["rch"][:, 13 + pp:14 + pp], A["rch"][:, 11 + pp:12 + pp]),
                            waits=[d_toks[ci]], inc="v")
                    for ci in (c0, c0 + 1):
                        pp = ci % 2
                        bank = P[6 + pp]
                        xs = A["x_res"][:, ci * 1024 + headcol0: ci * 1024 + headcol0 + 64]
                        stk = p.op("vector", lambda v, xs=xs, bank=bank, pp=pp: v.scalar_tensor_tensor(
                            xs, bank[:, 0:64], A["rch"][:, 13 + pp:14 + pp], xs,
                            ALU.mult, ALU.add), waits=[r_toks2[ci], xres_out[ci]], inc="v")
                        p.bank_r(6 + pp, stk)
                        xres_out[ci] = stk
            acc_war_t = last_tp
            pair_war_t = ("t", p.cnt["t"])
            pair_end_t[dc] = pair_war_t
            for nm in ("wq", "wk", "wv"):
                wpair_free[f"{nm}{b}"] = [pair_war_t]
            if dc + 2 < NPAIR:
                load_pair(dc + 2)
        return xres_out

    xres_toks = None
    if "ffn1" in stages:
        load_expert(0, [])
        load_expert(1, [])
        xln = emit_ln(None)
        gb = emit_gates(xln)
        xres_toks = emit_experts(xln, gb, None, emit_out=False)
        if debug:
            p.dma("sync", dbg["d_xlnfm"], A["xln_fm"][:],
                  waits=[xln[NCH - 1][0], xln[NCH - 1][-1]])
            p.dma("sync", dbg["d_gb16"], A["gb16"][:], waits=[gb[NCH - 1]])
            dump_tok = p.dma("sync", dbg["d_xres1"].rearrange("(c p) d -> p c d", p=128),
                             A["x_res"][:].rearrange("p (c d) -> p c d", c=NCH),
                             waits=xres_toks)
            xres_toks = [dump_tok] * NCH
    if "attn" in stages:
        if attn_pairs == -2:
            xln2 = emit_ln(xres_toks)
        elif attn_pairs >= 0:
            xln2 = emit_ln(xres_toks)
            xres_toks = emit_attention(xln2, xres_toks if xres_toks else [x_tok] * NCH)
        if debug:
            fl = p.fence(("vector", "scalar"))
            p.dma("sync", dbg["d_q16"], A["q16"][:], waits=fl)
            p.dma("sync", dbg["d_k16"], A["k16"][:])
            p.dma("sync", dbg["d_acc"], A["acc_a0"][:])
            p.dma("sync", dbg["d_v16"], A["v16_1"][:])
            dump_tok = p.dma("sync", dbg["d_xres2"].rearrange("(c p) d -> p c d", p=128),
                             A["x_res"][:].rearrange("p (c d) -> p c d", c=NCH),
                             waits=xres_toks)
            xres_toks = [dump_tok] * NCH
    if "ffn2" in stages:
        attn_drain = p.fence()
        load_expert(0, attn_drain)
        load_expert(1, attn_drain)
        xln3 = emit_ln(xres_toks)
        gb3 = emit_gates(xln3)
        xres_toks = emit_experts(xln3, gb3, xres_toks, emit_out=ffn2_out, ne=ffn2_ne)
        if not ffn2_out:
            fl = p.fence(("vector",))
            p.dma("sync", y_h.rearrange("(c p) d -> p c d", p=128),
                  A["x_res"][:].rearrange("p (c d) -> p c d", c=NCH),
                  waits=[tk for tk in xres_toks if tk] + fl)
    else:
        fl = p.fence(("vector",))
        p.dma("sync", y_h.rearrange("(c p) d -> p c d", p=128),
              A["x_res"][:].rearrange("p (c d) -> p c d", c=NCH),
              waits=(xres_toks if xres_toks else [x_tok]) + fl)

    p.op("sync", lambda e: e.nop(),
         waits=[(s_, p.cnt[s_]) for s_ in DMASEMS if p.cnt[s_] > 0])

    import contextlib
    with contextlib.ExitStack() as stck:
        sems = {s: stck.enter_context(nc.semaphore(s)) for s in SEMS}
        p.emit(sems)
    return nc, dbg, p


_CACHE = {}


def get_program(stages=("ffn1", "attn", "ffn2"), debug=False, **kw):
    key = (tuple(stages), debug, tuple(sorted(kw.items())))
    if key not in _CACHE:
        nc1, dbg1, p1 = build_program(stages, debug, **kw)
        amts = [16 * p1.dma_splits[i] for i in range(p1.dma_idx)]
        if all(a == 16 for a in amts):
            _CACHE[key] = (nc1, dbg1)
        else:
            nc2, dbg2, p2 = build_program(stages, debug, dma_amts=amts, **kw)
            assert p2.dma_idx == p1.dma_idx
            _CACHE[key] = (nc2, dbg2)
    return _CACHE[key]


def make_inmaps(inputs):
    consts = host_consts()
    base = {
        "wq": inputs["L0_attn_Wq"],
        "bq_fm_in": np.ascontiguousarray(np.asarray(inputs["L0_attn_bq"]).reshape(8, 128).T),
        "wk": inputs["L0_attn_Wk"],
        "bk_fm_in": np.ascontiguousarray(np.asarray(inputs["L0_attn_bk"]).reshape(8, 128).T),
        "wv": inputs["L0_attn_Wv"],
        "bv_bc": np.ascontiguousarray(np.broadcast_to(inputs["L0_attn_bv"], (128, D))),
        "router_w": inputs["L0_ffn_router_w"],
        "rb_bc": np.ascontiguousarray(np.broadcast_to(inputs["L0_ffn_router_b"], (128, 8))),
        "routed_w1": inputs["L0_ffn_routed_w1"],
        "routed_w3": inputs["L0_ffn_routed_w3"],
        "routed_w2": inputs["L0_ffn_routed_w2"],
        "shared_w1": inputs["L0_ffn_shared_w1"],
        "shared_w3": inputs["L0_ffn_shared_w3"],
        "shared_w2": inputs["L0_ffn_shared_w2"],
    }
    base.update(consts)
    x = inputs["L0_attn_hidden_states"]
    return [dict(base, x=np.ascontiguousarray(x[c])) for c in range(8)]


def kernel(**inputs):
    nc, _ = get_program()
    maps = make_inmaps(inputs)
    res = run_bass_kernel_spmd(nc, maps, core_ids=list(range(8)))
    return np.stack([np.asarray(res.results[c]["y"], dtype=np.float32) for c in range(8)], axis=0)


# version 06
# speedup vs baseline: 1.1333x; 1.0066x over previous
import numpy as np
import ml_dtypes
import concourse.bass as bass
import concourse.mybir as mybir
from concourse.bass_utils import run_bass_kernel_spmd

F32 = mybir.dt.float32
BF16 = mybir.dt.bfloat16
U8 = mybir.dt.uint8
ALU = mybir.AluOpType
ACT = mybir.ActivationFunctionType
AX = mybir.AxisListType.X
bfdt = ml_dtypes.bfloat16

T, D = 2048, 1024
NCH = 16
NPAIR = 8
NE = 10
NT = 8
NEG = -28672.0

LV_N = [2048, 1024, 512, 256, 128, 64]
LV_OFF = [0, 2048, 3072, 3584, 3840, 3968]
LV_NC = [16, 8, 4, 2, 1, 1]
LV_CW = [128, 128, 128, 128, 128, 64]
LV_P32 = [("a", 0), ("b", 0), ("a", 0), ("b", 0), ("a", 512), ("b", 256)]
LV_ACC = ["a", "b", "a", "b", "a", "b"]
LV_SLOT0 = {1: 0, 2: 8, 3: 12, 4: 14, 5: 15}
SLOTW = 66

ENGS = ("sync", "tensor", "vector", "scalar", "gpsimd")
SEMS = ("t", "v", "s", "g", "dS", "dG", "dA", "dB", "dC", "dD", "dE", "dF", "b")
DMASEMS = ("dS", "dG", "dA", "dB", "dC", "dD", "dE", "dF")
BAR = 1024


class Prog:
    def __init__(self, nc, dma_amts=None):
        self.nc = nc
        self.items = {e: [] for e in ENGS}
        self.cnt = {s: 0 for s in SEMS}
        self.lastw = {e: {} for e in ENGS}
        self.bank_readers = {i: [] for i in range(8)}
        self.dma_amts = dma_amts
        self.dma_idx = 0
        self.dma_splits = {}
        self.sems = None
        self.bar_tok = None

    def op(self, eng, fn, waits=(), inc=None, amt=1, dma_id=None):
        ws = []
        if self.bar_tok is not None and eng != "sync":
            waits = [self.bar_tok] + list(waits)
        for w in waits:
            if w is None:
                continue
            s, v = w
            if v <= 0:
                continue
            assert v <= self.cnt[s], f"forward-token violation: {s}={v} > {self.cnt[s]}"
            if self.lastw[eng].get(s, 0) < v:
                ws.append((s, v))
                self.lastw[eng][s] = v
        tok = None
        if inc is not None:
            self.cnt[inc] += amt
            tok = (inc, self.cnt[inc])
        self.items[eng].append([ws, fn, inc, amt, tok, dma_id])
        return tok

    def dma(self, q, out, in_, waits=(), sem=None):
        if sem is None:
            sem = "dS" if q == "sync" else "dG"
        idx = self.dma_idx
        self.dma_idx += 1
        amt = 16 if self.dma_amts is None else self.dma_amts[idx]
        return self.op(q, lambda e, o=out, i=in_: e.dma_start(o, i), waits=waits,
                       inc=sem, amt=amt, dma_id=idx)

    def bank_w(self, i):
        w = self.bank_readers[i]
        self.bank_readers[i] = []
        return w

    def bank_r(self, i, tok):
        self.bank_readers[i].append(tok)

    def fence(self, engs=("tensor", "vector", "scalar")):
        m = {"tensor": "t", "vector": "v", "scalar": "s", "gpsimd": "g"}
        toks = []
        for e in engs:
            sem = m[e]
            assert self.items[e], f"fence on empty engine {e}"
            it = self.items[e][-1]
            if it[2] is None:
                self.cnt[sem] += 1
                it[2] = sem
                it[4] = (sem, self.cnt[sem])
            toks.append(it[4])
        return toks

    def emit(self, sems):
        self.sems = sems
        with self.nc.Block() as block:
            for eng in ENGS:
                if not self.items[eng]:
                    continue

                def mk(eng):
                    def body(e):
                        for (ws, fn, inc, amt, _tok, dma_id) in self.items[eng]:
                            for (s, v) in ws:
                                e.wait_ge(sems[s], v)
                            n0 = self.nc.n_instructions()
                            inst = fn(e)
                            if dma_id is not None:
                                self.dma_splits[dma_id] = self.nc.n_instructions() - n0
                            if inc is not None:
                                inst.then_inc(sems[inc], 16 if dma_id is not None else amt)
                    return body

                getattr(block, eng)(mk(eng))


def host_consts():
    i = np.arange(128)
    blk = i // 16
    ident32 = np.eye(128, dtype=np.float32)
    ma = np.zeros((9, 128), np.float32)
    mb0 = np.zeros((9, 128), np.float32)
    mb1 = np.zeros((9, 128), np.float32)
    ma[0, :] = NEG
    mb0[0, :] = 1.0
    mb1[0, :] = 1.0
    for b in range(8):
        ma[1 + b, blk == b] = -NEG
        mb0[1 + b, blk == b] = 1.0
        mb1[1 + b, blk == (b ^ 1)] = 1.0
    plo = np.zeros((128, 128), np.float32)
    phi = np.zeros((128, 128), np.float32)
    for m in range(64):
        plo[2 * m, m] = 1.0
        plo[2 * m + 1, m] = 1.0
        phi[2 * m, 64 + m] = 1.0
        phi[2 * m + 1, 64 + m] = 1.0
    oh = np.zeros((8, 8 * 128), np.float32)
    for e in range(8):
        oh[e, e * 128:(e + 1) * 128] = 1.0
    return {
        "c_ident32": ident32,
        "c_ma": ma.astype(bfdt), "c_mb0": mb0.astype(bfdt), "c_mb1": mb1.astype(bfdt),
        "c_plo": plo.astype(bfdt), "c_phi": phi.astype(bfdt),
        "c_oh16": oh.astype(bfdt),
    }


def build_program(stages=("ffn1", "attn", "ffn2"), debug=False,
                  attn_pairs=NPAIR, attn_heads=2, attn_chunks=None, attn_setup=3,
                  ffn2_ne=NE, ffn2_out=True, dma_amts=None):
    nc = bass.Bass("TRN2", target_bir_lowering=False)
    p = Prog(nc, dma_amts=dma_amts)

    for s_ in ("t", "v", "s", "g") + DMASEMS:
        p.op("sync", lambda e, s_=s_: e.sem_clear(p.sems[s_]))
    p.op("sync", lambda e: e.sem_inc(p.sems["b"], BAR))
    p.cnt["b"] = BAR
    p.bar_tok = ("b", BAR)

    def din(name, shape, dtype=F32):
        return nc.dram_tensor(name, list(shape), dtype, kind="ExternalInput").ap()

    x_h = din("x", (T, D))
    wq_h = din("wq", (D, D)); bq_h = din("bq_fm_in", (128, 8))
    wk_h = din("wk", (D, D)); bk_h = din("bk_fm_in", (128, 8))
    wv_h = din("wv", (D, D)); bv_h = din("bv_bc", (128, D))
    rw_h = din("router_w", (D, 8)); rb_h = din("rb_bc", (128, 8))
    rw1_h = din("routed_w1", (8, D, 512))
    rw3_h = din("routed_w3", (8, D, 512))
    rw2_h = din("routed_w2", (8, 512, D))
    sw1_h = din("shared_w1", (2, D, 512))
    sw3_h = din("shared_w3", (2, D, 512))
    sw2_h = din("shared_w2", (2, 512, D))
    id32_h = din("c_ident32", (128, 128))
    ma_h = din("c_ma", (9, 128), BF16)
    mb0_h = din("c_mb0", (9, 128), BF16)
    mb1_h = din("c_mb1", (9, 128), BF16)
    plo_h = din("c_plo", (128, 128), BF16)
    phi_h = din("c_phi", (128, 128), BF16)
    oh16_h = din("c_oh16", (8, 1024), BF16)

    y_h = nc.dram_tensor("y", [T, D], F32, kind="ExternalOutput").ap()
    dbg = {}
    if debug:
        for name, shape, dt_ in [
            ("d_xres1", (T, D), F32), ("d_xres2", (T, D), F32),
            ("d_xlnfm", (128, 8 * 2048), BF16), ("d_gb16", (8, 2048), BF16),
            ("d_q16", (128, 4096), BF16), ("d_k16", (128, 4096), BF16),
            ("d_acc", (65, 2048), F32), ("d_v16", (128, 2048), BF16),
        ]:
            dbg[name] = nc.dram_tensor(name, list(shape), dt_, kind="ExternalOutput").ap()

    A = {}

    def sb(name, shape, dtype):
        A[name] = nc.alloc_sbuf_tensor("sb_" + name, list(shape), dtype)
        return A[name]

    sb("x_res", (128, NCH * 1024), F32)
    sb("xln_fm", (128, 8 * 2048), BF16)
    sb("xln_tmp0", (128, 1024), F32); sb("xln_tmp1", (128, 1024), F32)
    sb("ident32", (128, 128), F32)
    sb("ma", (9, 128), BF16); sb("mb0", (9, 128), BF16); sb("mb1", (9, 128), BF16)
    sb("plo", (128, 128), BF16); sb("phi", (128, 128), BF16)
    sb("oh16", (8, 1024), BF16)
    sb("router_w16", (128, 64), BF16)
    sb("rb_bc", (128, 8), F32)
    sb("bq_fm", (128, 8), F32); sb("bk_fm", (128, 8), F32); sb("bq_s", (128, 8), F32)
    sb("bv_bc", (128, 1024), F32)
    sb("bnst", (128, 24), F32)
    sb("mv", (128, 8), F32)
    sb("rch", (128, 16), F32)
    sb("cells", (128, 8), F32)

    ARENA = 76800
    arena_t = nc.alloc_sbuf_tensor("arena", [128, ARENA], U8)
    arena_base = nc.lookup_mloc(arena_t).addr

    def place(name, shape, dtype, off):
        nbytes = int(np.prod(shape[1:])) * (2 if dtype == BF16 else 4)
        assert off % 32 == 0 and off + nbytes <= ARENA, (name, off, nbytes)
        A[name] = nc.alloc_sbuf_tensor_at(name, list(shape), dtype, offset=arena_base + off)
        return A[name]

    place("w13_0", (128, 8 * 1024), BF16, 0)
    place("w13_1", (128, 8 * 1024), BF16, 16384)
    place("w2_0", (128, 4 * 1024), BF16, 32768)
    place("w2_1", (128, 4 * 1024), BF16, 40960)
    place("h_sb0", (128, 4 * 256), BF16, 49152)
    place("h_sb1", (128, 4 * 256), BF16, 51200)
    place("s_tmp0", (128, 256), F32, 53248)
    place("s_tmp1", (128, 256), F32, 54272)
    place("h0_0", (128, 256), F32, 55296)
    place("h0_1", (128, 256), F32, 56320)
    place("logits", (128, 128), F32, 57344)
    place("gscr", (128, 64), F32, 57856)
    place("gb16", (8, 2048), BF16, 58112)
    place("wq0", (128, 1024), BF16, 0)
    place("wq1", (128, 1024), BF16, 2048)
    place("wk0", (128, 1024), BF16, 4096)
    place("wk1", (128, 1024), BF16, 6144)
    place("wv0", (128, 1024), BF16, 8192)
    place("wv1", (128, 1024), BF16, 10240)
    place("v16", (128, 2048), BF16, 12288)
    place("qp32a", (128, 2048), F32, 16384)
    place("qp32b", (128, 1024), F32, 24576)
    place("q16", (128, 4096), BF16, 28672)
    place("k16", (128, 4096), BF16, 36864)
    place("acc_a0", (65, 2048), F32, 45056)
    place("acc_b0", (65, 1024), F32, 53248)
    place("acc_a1", (65, 2048), F32, 64448)
    place("acc_b1", (65, 1024), F32, 72640)
    place("vlv0", (128, 16 * SLOTW), BF16, 57344)
    place("vlv1", (128, 16 * SLOTW), BF16, 59456)
    place("vstage0", (128, 80), BF16, 61568)
    place("vstage1", (128, 80), BF16, 61728)
    place("A_sb0", (128, 128), F32, 61888)
    place("A_sb1", (128, 128), F32, 62400)
    place("AT_sb0", (128, 128), BF16, 62912)
    place("AT_sb1", (128, 128), BF16, 63168)
    place("vstage2", (128, 80), BF16, 63424)
    place("A_sb2", (128, 128), F32, 63616)
    place("AT_sb2", (128, 128), BF16, 64128)

    P = [nc.alloc_psum_tensor(f"P{i}", [128, 512], F32) for i in range(8)]

    p.dma("sync", A["x_res"][:].rearrange("p (c d) -> p c d", c=NCH),
          x_h.rearrange("(c p) d -> p c d", p=128))
    for nm, h in [("ident32", id32_h), ("ma", ma_h), ("mb0", mb0_h), ("mb1", mb1_h),
                  ("plo", plo_h), ("phi", phi_h), ("oh16", oh16_h),
                  ("rb_bc", rb_h), ("bv_bc", bv_h)]:
        p.dma("sync", A[nm][:], h)
    p.dma("sync", A["bq_fm"][:], bq_h)
    p.dma("sync", A["bk_fm"][:], bk_h)
    consts_tok = ("dS", p.cnt["dS"])
    x_tok = bq_tok = bk_tok = consts_tok
    rtr_tok = p.dma("gpsimd", A["router_w16"][:].rearrange("p (dc e) -> p dc e", dc=8),
                    rw_h.rearrange("(dc p) e -> p dc e", p=128), sem="dG")
    p.op("vector", lambda v: v.memset(A["cells"][:, 0:1], 1e-5))
    bqs_tok = p.op("vector", lambda v: v.tensor_scalar_mul(A["bq_s"][:], A["bq_fm"][:], 0.125),
                   waits=[bq_tok], inc="v")

    wtoks = {}
    wfree = {}

    def load_expert(e, extra_waits):
        b = e % 2
        s13 = "dA" if b == 0 else "dB"
        s2 = "dC" if b == 0 else "dD"
        if e < 8:
            w1, w3, w2 = rw1_h[e], rw3_h[e], rw2_h[e]
        else:
            w1, w3, w2 = sw1_h[e - 8], sw3_h[e - 8], sw2_h[e - 8]
        p.dma("gpsimd",
              A[f"w13_{b}"][:].rearrange("p (kc j) -> p kc j", kc=8)[:, :, 0:512],
              w1.rearrange("(kc p) j -> p kc j", p=128),
              waits=list(extra_waits) + wfree.get(f"w13_{b}", []), sem=s13)
        t2 = p.dma("gpsimd",
                   A[f"w13_{b}"][:].rearrange("p (kc j) -> p kc j", kc=8)[:, :, 512:1024],
                   w3.rearrange("(kc p) j -> p kc j", p=128), sem=s13)
        t3 = p.dma("gpsimd",
                   A[f"w2_{b}"][:].rearrange("p (jc d) -> p jc d", jc=4),
                   w2.rearrange("(jc p) d -> p jc d", p=128),
                   waits=list(extra_waits) + wfree.get(f"w2_{b}", []), sem=s2)
        wtoks[f"w13_{b}"] = t2
        wtoks[f"w2_{b}"] = t3
        wfree[f"w13_{b}"] = []
        wfree[f"w2_{b}"] = []

    guards = {
        "xln_tp": {},
        "ln_nb": {},
        "ln_sc": {},
        "gates_tp": {},
        "stmp": {},
        "hbuf": {},
    }

    def emit_ln(xres_toks):
        xln_ready = [None] * NCH
        for c0 in range(0, NCH, 2):
            agg_toks = {}
            for c in (c0, c0 + 1):
                pp = c % 2
                xs = A["x_res"][:, c * 1024:(c + 1) * 1024]
                w = [xres_toks[c]] if xres_toks else [x_tok]
                p.op("vector", lambda v, xs=xs, pp=pp: v.bn_stats(
                    A["bnst"][:, pp * 12:pp * 12 + 6], xs[:, 0:512]), waits=w)
                p.op("vector", lambda v, xs=xs, pp=pp: v.bn_stats(
                    A["bnst"][:, pp * 12 + 6:pp * 12 + 12], xs[:, 512:1024]), inc="v")
            for c in (c0, c0 + 1):
                pp = c % 2
                agg_toks[c] = p.op("vector", lambda v, pp=pp: v.bn_aggr(
                    A["mv"][:, pp * 2:pp * 2 + 2], A["bnst"][:, pp * 12:pp * 12 + 12]),
                    waits=[guards["ln_sc"].get(pp), ("v", p.cnt["v"])], inc="v")
            ln_toks = {}
            for c in (c0, c0 + 1):
                pp = c % 2
                ln_toks[c] = p.op("scalar", lambda s, pp=pp: s.activation(
                    A["rch"][:, pp * 4:pp * 4 + 1], A["mv"][:, pp * 2 + 1:pp * 2 + 2],
                    ACT.Ln, bias=A["cells"][:, 0:1], scale=1.0),
                    waits=[agg_toks[c], guards["ln_nb"].get(pp)], inc="s")
                guards["ln_sc"][pp] = ln_toks[c]
            r_toks = {}
            for c in (c0, c0 + 1):
                pp = c % 2
                r_toks[c] = p.op("scalar", lambda s, pp=pp: s.activation(
                    A["rch"][:, pp * 4 + 1:pp * 4 + 2], A["rch"][:, pp * 4:pp * 4 + 1],
                    ACT.Exp, bias=0.0, scale=-0.5), waits=[ln_toks[c]], inc="s")
            nb_toks = {}
            for c in (c0, c0 + 1):
                pp = c % 2
                nb_toks[c] = p.op("vector", lambda v, pp=pp: v.scalar_tensor_tensor(
                    A["rch"][:, pp * 4 + 2:pp * 4 + 3], A["mv"][:, pp * 2:pp * 2 + 1], -1.0,
                    A["rch"][:, pp * 4 + 1:pp * 4 + 2], ALU.mult, ALU.mult),
                    waits=[r_toks[c], agg_toks[c]], inc="v")
                guards["ln_nb"][pp] = nb_toks[c]
            for c in (c0, c0 + 1):
                pp = c % 2
                xs = A["x_res"][:, c * 1024:(c + 1) * 1024]
                a_tok = p.op("scalar", lambda s, xs=xs, pp=pp: s.activation(
                    A[f"xln_tmp{pp}"][:], xs, ACT.Identity,
                    bias=A["rch"][:, pp * 4 + 2:pp * 4 + 3],
                    scale=A["rch"][:, pp * 4 + 1:pp * 4 + 2]),
                    waits=[nb_toks[c], guards["xln_tp"].get(pp)], inc="s")
                ev_s = ev_v = None
                tp = None
                for dcc in range(8):
                    bi = dcc % 2
                    bank = P[bi]
                    tp = p.op("tensor", lambda t, pp=pp, dcc=dcc, bank=bank: t.transpose(
                        bank[:, 0:128], A[f"xln_tmp{pp}"][:, dcc * 128:(dcc + 1) * 128],
                        A["ident32"][:]),
                        waits=[a_tok, consts_tok] + p.bank_w(bi), inc="t")
                    dst = A["xln_fm"][:, dcc * 2048 + c * 128: dcc * 2048 + (c + 1) * 128]
                    if dcc % 4 < 2:
                        ev_s = p.op("scalar", lambda s, dst=dst, bank=bank: s.copy(
                            dst, bank[:, 0:128]), waits=[tp], inc="s")
                        p.bank_r(bi, ev_s)
                    else:
                        ev_v = p.op("vector", lambda v, dst=dst, bank=bank: v.tensor_copy(
                            dst, bank[:, 0:128]), waits=[tp], inc="v")
                        p.bank_r(bi, ev_v)
                guards["xln_tp"][pp] = tp
                xln_ready[c] = [ev_s, ev_v]
        return xln_ready

    def emit_gates(xln_ready):
        gb_ready = [None] * NCH
        lg_toks = [None] * NCH
        for c in range(NCH):
            bi = 2 + c % 2
            bank = P[bi]
            for dc in range(8):
                waits = []
                if dc == 0:
                    waits = list(xln_ready[c]) + [rtr_tok] + p.bank_w(bi)
                p.op("tensor", lambda t, dc=dc, c=c, bank=bank: t.matmul(
                    bank[:, 0:8],
                    A["xln_fm"][:, dc * 2048 + c * 128: dc * 2048 + (c + 1) * 128],
                    A["router_w16"][:, dc * 8:(dc + 1) * 8],
                    start=(dc == 0), stop=(dc == 7)),
                    waits=waits, inc=("t" if dc == 7 else None))
            mm = ("t", p.cnt["t"])
            lg = A["logits"][:, c * 8:(c + 1) * 8]
            ev = p.op("vector", lambda v, lg=lg, bank=bank: v.tensor_add(
                lg, bank[:, 0:8], A["rb_bc"][:]), waits=[mm, consts_tok], inc="v")
            p.bank_r(bi, ev)
            lg_toks[c] = ev
        g = A["gscr"]
        for c0 in range(0, NCH, 2):
            steps = {c0: [], c0 + 1: []}
            for c in (c0, c0 + 1):
                o = (c % 2) * 32
                lg = A["logits"][:, c * 8:(c + 1) * 8]
                st = steps[c]
                gtp = guards["gates_tp"].get(c % 2)
                st.append(("vector", lambda v, lg=lg, o=o: v.reduce_max(g[:, o:o + 1], lg, AX), [gtp, lg_toks[c]]))
                st.append(("vector", lambda v, lg=lg, o=o: v.reduce_max(g[:, o + 1:o + 2], lg, AX, negate=True), []))
                st.append(("vector", lambda v, lg=lg, o=o: v.tensor_single_scalar(g[:, o + 8:o + 16], lg, g[:, o:o + 1], ALU.is_ge), []))
                st.append(("vector", lambda v, lg=lg, o=o: v.scalar_tensor_tensor(g[:, o + 16:o + 24], g[:, o + 8:o + 16], NEG, lg, ALU.mult, ALU.add), []))
                st.append(("vector", lambda v, o=o: v.reduce_max(g[:, o + 2:o + 3], g[:, o + 16:o + 24], AX), []))
                st.append(("scalar", lambda s, lg=lg, o=o: s.activation(g[:, o + 8:o + 16], lg, ACT.Exp, bias=g[:, o + 1:o + 2], scale=1.0), ["v"]))
                st.append(("scalar", lambda s, o=o: s.activation(g[:, o + 3:o + 4], g[:, o + 2:o + 3], ACT.Exp, bias=g[:, o + 1:o + 2], scale=1.0), ["v"]))
                st.append(("vector", lambda v, lg=lg, o=o: v.tensor_single_scalar(g[:, o + 16:o + 24], lg, g[:, o + 2:o + 3], ALU.is_ge), ["s"]))
                st.append(("vector", lambda v, o=o: v.tensor_scalar_add(g[:, o + 4:o + 5], g[:, o + 3:o + 4], 1.0), ["s"]))
                st.append(("vector", lambda v, o=o: v.reciprocal(g[:, o + 5:o + 6], g[:, o + 4:o + 5]), []))
                st.append(("vector", lambda v, o=o: v.tensor_scalar_mul(g[:, o + 6:o + 7], g[:, o + 5:o + 6], 0.5), []))
                st.append(("vector", lambda v, o=o: v.scalar_tensor_tensor(g[:, o + 24:o + 32], g[:, o + 8:o + 16], g[:, o + 6:o + 7], g[:, o + 16:o + 24], ALU.mult, ALU.mult), ["s"]))
            for i in range(len(steps[c0])):
                for c in (c0, c0 + 1):
                    eng, fn, dep = steps[c][i]
                    w = []
                    if eng == "vector":
                        w.append(("v", p.cnt["v"]))
                    for d in dep:
                        if d == "v":
                            w.append(("v", p.cnt["v"]))
                        elif d == "s":
                            w.append(("s", p.cnt["s"]))
                        elif d is not None:
                            w.append(d)
                    p.op(eng, fn, waits=w, inc=("v" if eng == "vector" else "s"))
            for c in (c0, c0 + 1):
                bi = c % 2
                bank = P[bi]
                o = (c % 2) * 32
                last_v = ("v", p.cnt["v"])
                tp = p.op("tensor", lambda t, o=o, bank=bank: t.transpose(
                    bank[0:8, 0:128], g[:, o + 24:o + 32], A["ident32"][:]),
                    waits=[last_v] + p.bank_w(bi), inc="t")
                guards["gates_tp"][c % 2] = tp
                ev = p.op("vector", lambda v, c=c, bank=bank: v.tensor_copy(
                    A["gb16"][0:8, c * 128:(c + 1) * 128], bank[0:8, 0:128]),
                    waits=[tp], inc="v")
                p.bank_r(bi, ev)
                gb_ready[c] = ev
        return gb_ready

    def emit_experts(xln_ready, gb_ready, xres_in_toks, emit_out, ne=NE):
        xres_out = [None] * NCH
        xadd_tok = {}
        grot = [0]

        def p13_phase(e, tt):
            routed = e < 8
            w13 = A[f"w13_{e % 2}"]
            hbuf = (e * NT + tt) % 2
            h = A[f"h_sb{hbuf}"]
            h_war = guards["hbuf"].get(hbuf)
            gbc_tok = None
            if routed:
                waits = [gb_ready[2 * tt], gb_ready[2 * tt + 1], consts_tok] + p.bank_w(3)
                gbc_tok = p.op("tensor", lambda t, e=e, tt=tt: t.matmul(
                    P[3][:, 0:256], A["oh16"][:, e * 128:(e + 1) * 128],
                    A["gb16"][0:8, tt * 256:(tt + 1) * 256], start=True, stop=True),
                    waits=waits, inc="t")
            htt_last = None
            for jc in range(4):
                rot = grot[0] % 3
                grot[0] += 1
                bank = P[rot]
                for half in range(2):
                    for kc in range(8):
                        waits = []
                        if half == 0 and kc == 0:
                            waits = p.bank_w(rot) + [
                                wtoks[f"w13_{e % 2}"],
                                xln_ready[2 * tt][0], xln_ready[2 * tt][-1],
                                xln_ready[2 * tt + 1][0], xln_ready[2 * tt + 1][-1]]
                        co = 256 * half
                        p.op("tensor", lambda t, jc=jc, kc=kc, tt=tt, bank=bank, co=co, half=half, w13=w13: t.matmul(
                            bank[:, co:co + 256],
                            w13[:, kc * 1024 + half * 512 + jc * 128: kc * 1024 + half * 512 + (jc + 1) * 128],
                            A["xln_fm"][:, kc * 2048 + tt * 256: kc * 2048 + (tt + 1) * 256],
                            start=(kc == 0), stop=(kc == 7)),
                            waits=waits, inc=("t" if (half == 1 and kc == 7) else None))
                p13_tok = ("t", p.cnt["t"])
                st = A[f"s_tmp{jc % 2}"]
                h0 = A[f"h0_{jc % 2}"]
                silu = p.op("scalar", lambda s, bank=bank, st=st: s.activation(
                    st[:], bank[:, 0:256], ACT.Silu, bias=0.0, scale=1.0),
                    waits=[p13_tok, guards["stmp"].get(jc % 2)], inc="s")
                if routed:
                    m1 = p.op("vector", lambda v, bank=bank, st=st, h0=h0: v.scalar_tensor_tensor(
                        h0[:], bank[:, 256:512], 1.0, st[:], ALU.mult, ALU.mult),
                        waits=[silu, gbc_tok] + ([h_war] if jc == 0 else []), inc="v")
                    m2 = p.op("vector", lambda v, jc=jc, h=h, h0=h0: v.tensor_mul(
                        h[:, jc * 256:(jc + 1) * 256], h0[:], P[3][:, 0:256]),
                        waits=[m1], inc="v")
                    htt_last = m2
                    p.bank_readers[rot] = [silu, m1]
                    p.bank_r(3, m2)
                else:
                    m1 = p.op("vector", lambda v, bank=bank, st=st, jc=jc, h=h: v.scalar_tensor_tensor(
                        h[:, jc * 256:(jc + 1) * 256], bank[:, 256:512], 0.5, st[:],
                        ALU.mult, ALU.mult),
                        waits=[silu] + ([h_war] if jc == 0 else []), inc="v")
                    htt_last = m1
                    p.bank_readers[rot] = [silu, m1]
                guards["stmp"][jc % 2] = m1
            return {"e": e, "tt": tt, "hbuf": hbuf, "h_ready": htt_last,
                    "p13_tok": ("t", p.cnt["t"])}

        def w2_phase(st):
            e, tt, hbuf = st["e"], st["tt"], st["hbuf"]
            h = A[f"h_sb{hbuf}"]
            w2 = A[f"w2_{e % 2}"]
            last_w2 = None
            for tc in range(2):
                for dhh in range(2):
                    ob = 4 + 2 * tc + dhh
                    for jc in range(4):
                        waits = []
                        if jc == 0:
                            waits = [st["h_ready"], wtoks[f"w2_{e % 2}"]] + p.bank_w(ob)
                        p.op("tensor", lambda t, jc=jc, tc=tc, dhh=dhh, ob=ob, h=h, w2=w2: t.matmul(
                            P[ob][:],
                            h[:, jc * 256 + tc * 128: jc * 256 + (tc + 1) * 128],
                            w2[:, jc * 1024 + dhh * 512: jc * 1024 + (dhh + 1) * 512],
                            start=(jc == 0), stop=(jc == 3)),
                            waits=waits, inc=("t" if jc == 3 else None))
                    last_w2 = ("t", p.cnt["t"])
                    c = 2 * tt + tc
                    xs = A["x_res"][:, c * 1024 + dhh * 512: c * 1024 + (dhh + 1) * 512]
                    aw = [last_w2, xadd_tok.get((c, dhh))]
                    if e == 0 and xres_in_toks:
                        aw.append(xres_in_toks[c])
                    add = p.op("vector", lambda v, xs=xs, ob=ob: v.tensor_add(
                        xs, xs, P[ob][:]), waits=aw, inc="v")
                    xadd_tok[(c, dhh)] = add
                    p.bank_r(ob, add)
                    if e == ne - 1:
                        xres_out[c] = add
            guards["hbuf"][hbuf] = last_w2
            if e == ne - 1 and emit_out:
                c0 = 2 * tt
                p.dma("sync",
                      y_h.rearrange("(c p) d -> p c d", p=128)[:, c0:c0 + 2, :],
                      A["x_res"][:, c0 * 1024:(c0 + 2) * 1024].rearrange("p (c d) -> p c d", c=2),
                      waits=[xres_out[c0], xres_out[c0 + 1]])
            return last_w2

        pending = None
        exp_last_p13 = {}
        for e in range(ne):
            for tt in range(NT):
                st = p13_phase(e, tt)
                exp_last_p13[e] = st["p13_tok"]
                if pending is not None:
                    last_w2 = w2_phase(pending)
                    if pending["tt"] == NT - 1:
                        ed = pending["e"]
                        wfree[f"w13_{ed % 2}"] = [exp_last_p13[ed]]
                        wfree[f"w2_{ed % 2}"] = [last_w2]
                        if ed + 2 < ne:
                            load_expert(ed + 2, [])
                pending = st
        last_w2 = w2_phase(pending)
        wfree[f"w13_{(ne - 1) % 2}"] = [exp_last_p13[ne - 1]]
        wfree[f"w2_{(ne - 1) % 2}"] = [last_w2]
        return xres_out

    def emit_attention(xln_ready, xres_in_toks):
        xres_out = list(xres_in_toks)
        ffn_drain = p.fence()
        ones_tok = None
        if attn_setup >= 1:
            for nm in ("vstage0", "vstage1", "vstage2"):
                p.op("vector", lambda v, nm=nm: v.memset(A[nm][:, 64:65], 1.0), waits=ffn_drain)
            for nm in ("vlv0", "vlv1"):
                ones_tok = p.op("vector", lambda v, nm=nm: v.memset(
                    A[nm][:].rearrange("p (s c) -> p s c", c=SLOTW)[:, :, 64:65], 1.0),
                    waits=ffn_drain, inc="v")

        wpair_free = {}

        def load_pair(dc):
            b = dc % 2
            spair = "dE" if b == 0 else "dF"
            tok = None
            for nm, wh in (("wq", wq_h), ("wk", wk_h), ("wv", wv_h)):
                key = f"{nm}{b}"
                tok = p.dma(
                    "gpsimd", A[key][:].rearrange("p (kc m) -> p kc m", kc=8),
                    wh.rearrange("(kc p) n -> p kc n", p=128)[:, :, dc * 128:(dc + 1) * 128],
                    waits=wpair_free.get(key, list(ffn_drain)), sem=spair)
            for nm in ("wq", "wk", "wv"):
                wtoks[f"{nm}{b}"] = tok

        if attn_setup >= 2:
            load_pair(0)
            load_pair(1)

        pair_war_t = None
        acc_war_t = None
        prev_carry = {}
        for dc in range(attn_pairs):
            b = dc % 2
            v16_toks = []
            for c in range(NCH):
                bi = 4 + c % 2
                bank = P[bi]
                for kc in range(8):
                    waits = []
                    if kc == 0:
                        waits = [wtoks[f"wv{b}"], xln_ready[c][0], xln_ready[c][-1]] + p.bank_w(bi)
                    p.op("tensor", lambda t, kc=kc, c=c, bank=bank, b=b: t.matmul(
                        bank[:, 0:128],
                        A["xln_fm"][:, kc * 2048 + c * 128: kc * 2048 + (c + 1) * 128],
                        A[f"wv{b}"][:, kc * 128:(kc + 1) * 128],
                        start=(kc == 0), stop=(kc == 7)),
                        waits=waits, inc=("t" if kc == 7 else None))
                mm = ("t", p.cnt["t"])
                ev = p.op("vector", lambda v, c=c, bank=bank, dc=dc: v.tensor_add(
                    A["v16"][:, c * 128:(c + 1) * 128], bank[:, 0:128],
                    A["bv_bc"][:, dc * 128:(dc + 1) * 128]),
                    waits=[mm, consts_tok] + ([pair_war_t] if c == 0 else []), inc="v")
                p.bank_r(bi, ev)
                v16_toks.append(ev)

            def proj_pool(wkey, bias_ap, scale, dst16, pre_mm, pre_ev):
                wap = A[wkey]
                ev_toks = []
                for tile in range(4):
                    bi = tile % 4
                    bank = P[bi]
                    for kc in range(8):
                        waits = []
                        if kc == 0:
                            waits = [wtoks[wkey]] + p.bank_w(bi) + pre_mm + \
                                    [xln_ready[NCH - 1][0], xln_ready[NCH - 1][-1]]
                        p.op("tensor", lambda t, kc=kc, tile=tile, bank=bank, wap=wap: t.matmul(
                            bank[:],
                            wap[:, kc * 128:(kc + 1) * 128],
                            A["xln_fm"][:, kc * 2048 + tile * 512: kc * 2048 + (tile + 1) * 512],
                            start=(kc == 0), stop=(kc == 7)),
                            waits=waits, inc=("t" if kc == 7 else None))
                    mm = ("t", p.cnt["t"])
                    ev = p.op("scalar", lambda s, tile=tile, bank=bank, bias_ap=bias_ap, scale=scale: s.activation(
                        A["qp32a"][:, tile * 512:(tile + 1) * 512], bank[:], ACT.Identity,
                        bias=bias_ap, scale=scale),
                        waits=[mm] + (pre_ev if tile == 0 else []), inc="s")
                    ev_toks.append(ev)
                    p.bank_r(bi, ev)
                cast_last = p.op("scalar", lambda s, dst16=dst16: s.activation(
                    dst16[:, 0:2048], A["qp32a"][:, 0:2048], ACT.Copy, bias=0.0, scale=1.0),
                    waits=[ev_toks[-1]], inc="s")
                prev_pool = None
                for lv in range(1, 6):
                    n = LV_N[lv]
                    sbp, offp = LV_P32[lv - 1]
                    sbc, offc = LV_P32[lv]
                    srcp = A["qp32a"] if sbp == "a" else A["qp32b"]
                    srcc = A["qp32a"] if sbc == "a" else A["qp32b"]
                    np_ = LV_N[lv - 1]
                    pw = [prev_pool, cast_last] if prev_pool else [ev_toks[-1], cast_last]
                    prev_pool = p.op("vector", lambda v, a=srcp, bb=srcc, op_=offp, oc=offc, n=n, np_=np_: v.tensor_add(
                        bb[:, oc:oc + n], a[:, op_:op_ + np_:2], a[:, op_ + 1:op_ + np_:2]),
                        waits=pw, inc="v")
                    cast_last = p.op("scalar", lambda s, dst16=dst16, lv=lv, bb=srcc, oc=offc, n=n: s.activation(
                        dst16[:, LV_OFF[lv]:LV_OFF[lv] + n], bb[:, oc:oc + n],
                        ACT.Copy, bias=0.0, scale=0.5 ** lv), waits=[prev_pool], inc="s")
                return cast_last, prev_pool

            q_cast, q_pool = proj_pool(f"wq{b}", A["bq_s"][:, dc:dc + 1], 0.125, A["q16"],
                                       [bqs_tok], [pair_war_t] if pair_war_t else [])
            k_cast, k_pool = proj_pool(f"wk{b}", A["bk_fm"][:, dc:dc + 1], 1.0, A["k16"],
                                       [bk_tok], [q_cast, q_pool])
            qk_done = k_cast

            vlv_toks = {0: {}, 1: {}}
            first_vlv = [True]
            for lv in range(1, 6):
                nout = max(1, LV_N[lv] // 128)
                for si in range(nout):
                    bi = 6 + si % 2
                    bank = P[bi]
                    slot = LV_SLOT0[lv] + si
                    if lv == 1:
                        w0 = [v16_toks[2 * si], v16_toks[2 * si + 1], ones_tok, consts_tok] + p.bank_w(bi)
                        p.op("tensor", lambda t, si=si, bank=bank: t.matmul(
                            bank[:, 0:128], A["plo"][:],
                            A["v16"][:, (2 * si) * 128:(2 * si + 1) * 128],
                            start=True, stop=False), waits=w0)
                        p.op("tensor", lambda t, si=si, bank=bank: t.matmul(
                            bank[:, 0:128], A["phi"][:],
                            A["v16"][:, (2 * si + 1) * 128:(2 * si + 2) * 128],
                            start=False, stop=True), inc="t")
                        mm = ("t", p.cnt["t"])
                        for hh in range(2):
                            ev = p.op("vector", lambda v, hh=hh, slot=slot, bank=bank: v.tensor_copy(
                                A[f"vlv{hh}"][:, slot * SLOTW: slot * SLOTW + 64],
                                bank[:, hh * 64:hh * 64 + 64]), waits=[mm], inc="v")
                            vlv_toks[hh][(lv, si)] = ev
                            p.bank_r(bi, ev)
                        first_vlv[0] = False
                    elif lv < 5:
                        for hh in range(2):
                            bi = 6 + (2 * si + hh) % 2
                            bank = P[bi]
                            sp = LV_SLOT0[lv - 1] + 2 * si
                            w0 = [vlv_toks[hh][(lv - 1, 2 * si)], vlv_toks[hh][(lv - 1, 2 * si + 1)]] + p.bank_w(bi)
                            p.op("tensor", lambda t, hh=hh, sp=sp, bank=bank: t.matmul(
                                bank[:, hh * 64:hh * 64 + 64], A["plo"][:],
                                A[f"vlv{hh}"][:, sp * SLOTW: sp * SLOTW + 64],
                                start=True, stop=False), waits=w0)
                            p.op("tensor", lambda t, hh=hh, sp=sp, bank=bank: t.matmul(
                                bank[:, hh * 64:hh * 64 + 64], A["phi"][:],
                                A[f"vlv{hh}"][:, (sp + 1) * SLOTW: (sp + 1) * SLOTW + 64],
                                start=False, stop=True), inc="t")
                            mm = ("t", p.cnt["t"])
                            ev = p.op("vector", lambda v, hh=hh, slot=slot, bank=bank: v.tensor_copy(
                                A[f"vlv{hh}"][:, slot * SLOTW: slot * SLOTW + 64],
                                bank[:, hh * 64:hh * 64 + 64]), waits=[mm], inc="v")
                            vlv_toks[hh][(lv, si)] = ev
                            p.bank_r(bi, ev)
                    else:
                        for hh in range(2):
                            bi = 6 + hh % 2
                            bank = P[bi]
                            sp = LV_SLOT0[4]
                            w0 = [vlv_toks[hh][(4, 0)]] + p.bank_w(bi)
                            p.op("tensor", lambda t, hh=hh, sp=sp, bank=bank: t.matmul(
                                bank[0:64, hh * 64:hh * 64 + 64], A["plo"][:, 0:64],
                                A[f"vlv{hh}"][:, sp * SLOTW: sp * SLOTW + 64],
                                start=True, stop=True), waits=w0, inc="t")
                            mm = ("t", p.cnt["t"])
                            ev = p.op("vector", lambda v, hh=hh, slot=slot, bank=bank: v.tensor_copy(
                                A[f"vlv{hh}"][0:64, slot * SLOTW: slot * SLOTW + 64],
                                bank[0:64, hh * 64:hh * 64 + 64]), waits=[mm], inc="v")
                            vlv_toks[hh][(lv, 0)] = ev
                            p.bank_r(bi, ev)

            chunks = [(hh, lv, ci)
                      for lv in range(5, -1, -1)
                      for ci in range(LV_NC[lv])
                      for hh in range(attn_heads)]
            if attn_chunks is not None:
                chunks = chunks[:attn_chunks]
            nchk = len(chunks)
            toks = {}
            SB3 = [0, 1, 6]
            AB3 = [2, 3, 7]

            def tprev(name, k):
                if k >= 0:
                    return toks.get((name, k))
                return prev_carry.get((name, k % 3))

            def stage_S(k):
                hh, lv, ci = chunks[k]
                cw = LV_CW[lv]
                rows = slice(64 * hh, 64 * hh + 64)
                bi = SB3[k % 3]
                bank = P[bi]
                mb = A["mb0"] if lv == 0 else A["mb1"]
                qap = A["q16"][rows, LV_OFF[lv] + ci * 128: LV_OFF[lv] + ci * 128 + cw]
                kap = A["k16"][rows, LV_OFF[lv] + ci * 128: LV_OFF[lv] + ci * 128 + cw]
                w = [qk_done, consts_tok] + p.bank_w(bi)
                p.op("tensor", lambda t, qap=qap, kap=kap, cw=cw, bank=bank: t.matmul(
                    bank[0:cw, 0:cw], qap, kap, start=True, stop=False), waits=w)
                p.op("tensor", lambda t, cw=cw, bank=bank, mb=mb: t.matmul(
                    bank[0:cw, 0:cw], A["ma"][:, 0:cw], mb[:, 0:cw],
                    start=False, stop=True), inc="t")
                toks[("S", k)] = ("t", p.cnt["t"])

            def stage_rmax(k):
                hh, lv, ci = chunks[k]
                cw = LV_CW[lv]
                pp = k % 3
                bi = SB3[pp]
                bank = P[bi]
                w = [toks[("S", k)], tprev("exp", k - 3)]
                tok = p.op("vector", lambda v, cw=cw, pp=pp, bank=bank: v.reduce_max(
                    A["rch"][0:cw, 8 + pp:9 + pp], bank[0:cw, 0:cw], AX, negate=True),
                    waits=w, inc="v")
                toks[("rmax", k)] = tok
                p.bank_r(bi, tok)

            def stage_exp(k):
                hh, lv, ci = chunks[k]
                cw = LV_CW[lv]
                pp = k % 3
                bi = SB3[pp]
                bank = P[bi]
                w = [toks[("rmax", k)], tprev("tpA", k - 3)]
                tok = p.op("scalar", lambda s, cw=cw, pp=pp, bank=bank: s.activation(
                    A[f"A_sb{pp}"][0:cw, 0:cw], bank[0:cw, 0:cw], ACT.Exp,
                    bias=A["rch"][0:cw, 8 + pp:9 + pp], scale=1.0), waits=w, inc="s")
                toks[("exp", k)] = tok
                p.bank_r(bi, tok)

            def stage_tpA(k):
                hh, lv, ci = chunks[k]
                cw = LV_CW[lv]
                pp = k % 3
                bi = AB3[pp]
                bank = P[bi]
                w = [toks[("exp", k)]] + p.bank_w(bi)
                toks[("tpA", k)] = p.op("tensor", lambda t, cw=cw, pp=pp, bank=bank: t.transpose(
                    bank[0:cw, 0:cw], A[f"A_sb{pp}"][0:cw, 0:cw],
                    A["ident32"][0:cw, 0:cw]), waits=w, inc="t")

            def stage_atev(k):
                hh, lv, ci = chunks[k]
                cw = LV_CW[lv]
                pp = k % 3
                bi = AB3[pp]
                bank = P[bi]
                w = [toks[("tpA", k)], tprev("y", k - 3)]
                if pp == 0:
                    tok = p.op("vector", lambda v, cw=cw, pp=pp, bank=bank: v.tensor_copy(
                        A[f"AT_sb{pp}"][0:cw, 0:cw], bank[0:cw, 0:cw]), waits=w, inc="v")
                else:
                    tok = p.op("scalar", lambda s, cw=cw, pp=pp, bank=bank: s.copy(
                        A[f"AT_sb{pp}"][0:cw, 0:cw], bank[0:cw, 0:cw]), waits=w, inc="s")
                toks[("atev", k)] = tok
                p.bank_r(bi, tok)

            def stage_vst(k):
                hh, lv, ci = chunks[k]
                if lv != 0:
                    return
                pp = k % 3
                w = [v16_toks[ci], ones_tok, tprev("y", k - 3)]
                src_ = A["v16"][:, ci * 128 + hh * 64: ci * 128 + hh * 64 + 64]
                toks[("vst", k)] = p.op("vector", lambda v, pp=pp, src_=src_: v.tensor_copy(
                    A[f"vstage{pp}"][:, 0:64], src_), waits=w, inc="v")

            def stage_y(k):
                hh, lv, ci = chunks[k]
                cw = LV_CW[lv]
                pp3 = k % 3
                bi = 4 + k % 2
                bank = P[bi]
                if lv == 0:
                    lhs = A[f"vstage{pp3}"][:, 0:65]
                    lw = [toks[("vst", k)]]
                else:
                    slot = LV_SLOT0[lv] + ci
                    lhs = A[f"vlv{hh}"][0:cw, slot * SLOTW: slot * SLOTW + 65]
                    lw = [vlv_toks[hh][(lv, ci)], ones_tok]
                w = [toks[("atev", k)]] + lw + p.bank_w(bi)
                p.op("tensor", lambda t, lhs=lhs, cw=cw, pp3=pp3, bank=bank: t.matmul(
                    bank[0:65, 0:cw], lhs, A[f"AT_sb{pp3}"][0:cw, 0:cw],
                    start=True, stop=True), waits=w, inc="t")
                toks[("y", k)] = ("t", p.cnt["t"])

            comb_last = {}

            def stage_comb(k):
                hh, lv, ci = chunks[k]
                bi = 4 + k % 2
                bank = P[bi]
                acc_cur = A[f"acc_a{hh}"] if LV_ACC[lv] == "a" else A[f"acc_b{hh}"]
                acc_prev = A[f"acc_b{hh}"] if LV_ACC[lv] == "a" else A[f"acc_a{hh}"]
                w = [toks[("y", k)], comb_last.get(hh)]
                if k in (2, 3) and acc_war_t is not None:
                    w.append(acc_war_t)
                if lv == 5:
                    tok = p.op("vector", lambda v, bank=bank, hh=hh: v.tensor_copy(
                        A[f"acc_b{hh}"][0:65, 0:64], bank[0:65, 0:64]), waits=w, inc="v")
                else:
                    s0 = ci * 128
                    tok = p.op("vector", lambda v, acc_cur=acc_cur, acc_prev=acc_prev, s0=s0, bank=bank: v.tensor_add(
                        acc_cur[0:65, s0:s0 + 128].rearrange("p (a b) -> p a b", b=2),
                        bank[0:65, 0:128].rearrange("p (a b) -> p a b", b=2),
                        acc_prev[0:65, s0 // 2: s0 // 2 + 64].unsqueeze(2).broadcast_to([65, 64, 2])),
                        waits=w, inc="v")
                toks[("comb", k)] = tok
                comb_last[hh] = tok
                p.bank_r(bi, tok)

            for i in range(nchk + 3):
                if 3 <= i <= nchk + 2:
                    stage_y(i - 3)
                    stage_comb(i - 3)
                if i < nchk:
                    stage_S(i)
                    stage_vst(i)
                if 1 <= i <= nchk:
                    stage_rmax(i - 1)
                    stage_exp(i - 1)
                if 2 <= i <= nchk + 1:
                    stage_tpA(i - 2)
                    stage_atev(i - 2)
            for nm_ in ("exp", "tpA", "y"):
                for pp in range(3):
                    kk = nchk - 1 - ((nchk - 1 - pp) % 3)
                    prev_carry[(nm_, pp)] = toks.get((nm_, kk))

            last_tp = None
            for hh in range(attn_heads):
                headcol0 = (2 * dc + hh) * 64
                for c0 in range(0, NCH, 2):
                    tps = {}
                    for ci in (c0, c0 + 1):
                        pp = ci % 2
                        bi = 6 + pp
                        bank = P[bi]
                        w = [comb_last.get(hh)] + p.bank_w(bi)
                        tps[ci] = p.op("tensor", lambda t, ci=ci, bank=bank, hh=hh: t.transpose(
                            bank[:, 0:65], A[f"acc_a{hh}"][0:65, ci * 128:(ci + 1) * 128],
                            A["ident32"][0:65, 0:65]), waits=w, inc="t")
                        last_tp = tps[ci]
                    d_toks = {}
                    for ci in (c0, c0 + 1):
                        pp = ci % 2
                        bank = P[6 + pp]
                        d_toks[ci] = p.op("vector", lambda v, bank=bank, pp=pp: v.tensor_scalar_add(
                            A["rch"][:, 11 + pp:12 + pp], bank[:, 64:65], 1e-8),
                            waits=[tps[ci], ("v", p.cnt["v"])], inc="v")
                        p.bank_r(6 + pp, d_toks[ci])
                    r_toks2 = {}
                    for ci in (c0, c0 + 1):
                        pp = ci % 2
                        r_toks2[ci] = p.op("vector", lambda v, pp=pp: v.reciprocal(
                            A["rch"][:, 13 + pp:14 + pp], A["rch"][:, 11 + pp:12 + pp]),
                            waits=[d_toks[ci]], inc="v")
                    for ci in (c0, c0 + 1):
                        pp = ci % 2
                        bank = P[6 + pp]
                        xs = A["x_res"][:, ci * 1024 + headcol0: ci * 1024 + headcol0 + 64]
                        stk = p.op("vector", lambda v, xs=xs, bank=bank, pp=pp: v.scalar_tensor_tensor(
                            xs, bank[:, 0:64], A["rch"][:, 13 + pp:14 + pp], xs,
                            ALU.mult, ALU.add), waits=[r_toks2[ci], xres_out[ci]], inc="v")
                        p.bank_r(6 + pp, stk)
                        xres_out[ci] = stk
            acc_war_t = last_tp
            pair_war_t = ("t", p.cnt["t"])
            for nm in ("wq", "wk", "wv"):
                wpair_free[f"{nm}{b}"] = [pair_war_t]
            if dc + 2 < NPAIR:
                load_pair(dc + 2)
        return xres_out

    xres_toks = None
    if "ffn1" in stages:
        load_expert(0, [])
        load_expert(1, [])
        xln = emit_ln(None)
        gb = emit_gates(xln)
        xres_toks = emit_experts(xln, gb, None, emit_out=False)
        if debug:
            p.dma("sync", dbg["d_xlnfm"], A["xln_fm"][:],
                  waits=[xln[NCH - 1][0], xln[NCH - 1][-1]])
            p.dma("sync", dbg["d_gb16"], A["gb16"][:], waits=[gb[NCH - 1]])
            dump_tok = p.dma("sync", dbg["d_xres1"].rearrange("(c p) d -> p c d", p=128),
                             A["x_res"][:].rearrange("p (c d) -> p c d", c=NCH),
                             waits=xres_toks)
            xres_toks = [dump_tok] * NCH
    if "attn" in stages:
        if attn_pairs == -2:
            xln2 = emit_ln(xres_toks)
        elif attn_pairs >= 0:
            xln2 = emit_ln(xres_toks)
            xres_toks = emit_attention(xln2, xres_toks if xres_toks else [x_tok] * NCH)
        if debug:
            fl = p.fence(("vector", "scalar"))
            p.dma("sync", dbg["d_q16"], A["q16"][:], waits=fl)
            p.dma("sync", dbg["d_k16"], A["k16"][:])
            p.dma("sync", dbg["d_acc"], A["acc_a0"][:])
            p.dma("sync", dbg["d_v16"], A["v16"][:])
            dump_tok = p.dma("sync", dbg["d_xres2"].rearrange("(c p) d -> p c d", p=128),
                             A["x_res"][:].rearrange("p (c d) -> p c d", c=NCH),
                             waits=xres_toks)
            xres_toks = [dump_tok] * NCH
    if "ffn2" in stages:
        attn_drain = p.fence()
        load_expert(0, attn_drain)
        load_expert(1, attn_drain)
        xln3 = emit_ln(xres_toks)
        gb3 = emit_gates(xln3)
        xres_toks = emit_experts(xln3, gb3, xres_toks, emit_out=ffn2_out, ne=ffn2_ne)
        if not ffn2_out:
            fl = p.fence(("vector",))
            p.dma("sync", y_h.rearrange("(c p) d -> p c d", p=128),
                  A["x_res"][:].rearrange("p (c d) -> p c d", c=NCH),
                  waits=[tk for tk in xres_toks if tk] + fl)
    else:
        fl = p.fence(("vector",))
        p.dma("sync", y_h.rearrange("(c p) d -> p c d", p=128),
              A["x_res"][:].rearrange("p (c d) -> p c d", c=NCH),
              waits=(xres_toks if xres_toks else [x_tok]) + fl)

    p.op("sync", lambda e: e.nop(),
         waits=[(s_, p.cnt[s_]) for s_ in DMASEMS if p.cnt[s_] > 0])

    import contextlib
    with contextlib.ExitStack() as stck:
        sems = {s: stck.enter_context(nc.semaphore(s)) for s in SEMS}
        p.emit(sems)
    return nc, dbg, p


_CACHE = {}


def get_program(stages=("ffn1", "attn", "ffn2"), debug=False, **kw):
    key = (tuple(stages), debug, tuple(sorted(kw.items())))
    if key not in _CACHE:
        nc1, dbg1, p1 = build_program(stages, debug, **kw)
        amts = [16 * p1.dma_splits[i] for i in range(p1.dma_idx)]
        if all(a == 16 for a in amts):
            _CACHE[key] = (nc1, dbg1)
        else:
            nc2, dbg2, p2 = build_program(stages, debug, dma_amts=amts, **kw)
            assert p2.dma_idx == p1.dma_idx
            _CACHE[key] = (nc2, dbg2)
    return _CACHE[key]


def make_inmaps(inputs):
    consts = host_consts()
    base = {
        "wq": inputs["L0_attn_Wq"],
        "bq_fm_in": np.ascontiguousarray(np.asarray(inputs["L0_attn_bq"]).reshape(8, 128).T),
        "wk": inputs["L0_attn_Wk"],
        "bk_fm_in": np.ascontiguousarray(np.asarray(inputs["L0_attn_bk"]).reshape(8, 128).T),
        "wv": inputs["L0_attn_Wv"],
        "bv_bc": np.ascontiguousarray(np.broadcast_to(inputs["L0_attn_bv"], (128, D))),
        "router_w": inputs["L0_ffn_router_w"],
        "rb_bc": np.ascontiguousarray(np.broadcast_to(inputs["L0_ffn_router_b"], (128, 8))),
        "routed_w1": inputs["L0_ffn_routed_w1"],
        "routed_w3": inputs["L0_ffn_routed_w3"],
        "routed_w2": inputs["L0_ffn_routed_w2"],
        "shared_w1": inputs["L0_ffn_shared_w1"],
        "shared_w3": inputs["L0_ffn_shared_w3"],
        "shared_w2": inputs["L0_ffn_shared_w2"],
    }
    base.update(consts)
    x = inputs["L0_attn_hidden_states"]
    return [dict(base, x=np.ascontiguousarray(x[c])) for c in range(8)]


def kernel(**inputs):
    nc, _ = get_program()
    maps = make_inmaps(inputs)
    res = run_bass_kernel_spmd(nc, maps, core_ids=list(range(8)))
    return np.stack([np.asarray(res.results[c]["y"], dtype=np.float32) for c in range(8)], axis=0)
